# version 09
# baseline (speedup 1.0000x reference)
import numpy as np
import concourse.bass as bass
import concourse.mybir as mybir
from concourse.bass_utils import run_bass_kernel_spmd
from concourse.aot_env import allow_ici_all_to_all
from contextlib import ExitStack

F32 = mybir.dt.float32
BF16 = mybir.dt.bfloat16
AF = mybir.ActivationFunctionType
ALU = mybir.AluOpType
AX = mybir.AxisListType

P = 128
CPC = 128
NB = 2
NI = 64
NH = 2
NM = 32
Q = 256
L = Q * NI
D = 1024
FF = 4096
TOK = 4096
NIL = 32
NFT = 32
NDT = 8
NCH = 8
TWO_PI = float(2.0 * np.pi)
INV_2PI = float(1.0 / (2.0 * np.pi))
HALF_PI = float(0.5 * np.pi)
M23 = 8388608.0
KB_STRIDE = 383
KBUF_LEN = CPC * KB_STRIDE + 512
RW = 512
T_IN = 13 * 16
T_H1 = 4 * 16
N_RECV = 32 * 16

QORD = [14, 15] + list(range(31, 15, -1)) + list(range(13, -1, -1))

QWT = [("q", k) for k in range(32)] + [("w", k) for k in range(32)]
NSLOT = len(QWT)
NTRS = NSLOT * 8


def qslot_idx(p):
    return p


def wslot_idx(k):
    return 32 + k


QWT_OWNER = []
QWT_DST = []
for _kind, _k in QWT:
    for _j in range(8):
        QWT_OWNER.append("v" if _j < 4 else "s")
        if _kind == "q":
            QWT_DST.append(("q", QORD[_k], _j))
        else:
            QWT_DST.append(("w", _k, _j))


def qwt_gv(x):
    return sum(1 for y in range(x) if QWT_OWNER[y] == "v")


def ev_owner(k):
    return "v" if k % 6 < 3 else "s"


def bev_owner(c):
    return "v" if c % 8 < 4 else "s"


def cnt_upto(owner_fn, m, who):
    return sum(1 for x in range(m + 1) if owner_fn(x) == who)


def rap(handle, off, dims):
    return bass.AP(handle, off, [list(d) for d in dims])


def round_kind(rnd):
    return rnd // 24, ("m1", "k", "cm")[(rnd % 24) // 8], rnd % 8


def x2t_vcnt(m):
    return 4 * (m // 8) + min(m % 8 + 1, 4)


def x2t_scnt(m):
    return 4 * (m // 8) + max(m % 8 - 3, 0)


_DRAIN_AFTER = {"tensor_tensor", "tensor_copy", "tensor_scalar", "tensor_scalar_mul",
                "tensor_scalar_add", "tensor_scalar_sub", "tensor_scalar_max", "tensor_scalar_min",
                "tensor_single_scalar", "scalar_tensor_tensor", "tensor_reduce", "reduce_max",
                "reduce_sum", "memset", "activation", "tensor_max", "tensor_relu", "tensor_add",
                "tensor_sub", "tensor_mul", "select", "reciprocal"}


class _InstProxy:

    def __init__(self, ins, rec):
        object.__setattr__(self, "_ins", ins)
        object.__setattr__(self, "_rec", rec)

    def then_inc(self, sem, v=1):
        self._rec.append((sem, v))
        return self._ins.then_inc(sem, v)

    def __getattr__(self, a):
        return getattr(self._ins, a)


class DrainEng:

    def __init__(self, eng, sem=None, multi=()):
        self._e = eng
        self._sem = sem
        self._multi = set(multi)
        self._n = 0
        self._counts = {}
        self._pend = None
        self._edge = None

    def _settle(self):
        if self._pend is None:
            return
        ins, rec = self._pend
        self._pend = None
        if not rec:
            ins.then_inc(self._sem, 1)
            self._n += 1
            self._edge = ("chain", self._n)
            return
        sem, v = rec[-1]
        key = id(sem)
        if key in self._multi:
            self._e.drain()
            self._edge = None
            return
        self._counts[key] = self._counts.get(key, 0) + sum(x[1] for x in rec)
        self._edge = ("sem", sem, self._counts[key])

    def __getattr__(self, name):
        f = getattr(self._e, name)
        if name in _DRAIN_AFTER:
            if self._sem is None:
                def wrap(*a, **k):
                    r = f(*a, **k)
                    self._e.drain()
                    return r
                return wrap

            def wrap(*a, **k):
                self._settle()
                if self._edge is not None:
                    if self._edge[0] == "chain":
                        self._e.wait_ge(self._sem, self._edge[1])
                    else:
                        self._e.wait_ge(self._edge[1], self._edge[2])
                r = f(*a, **k)
                rec = []
                self._pend = (r, rec)
                return _InstProxy(r, rec)
            return wrap
        if name in ("sem_inc", "drain", "wait_ge"):
            def passthrough(*a, **k):
                if name == "sem_inc":
                    self._settle()
                    if self._edge is not None and self._edge[0] == "chain":
                        pass
                return f(*a, **k)
            return passthrough
        return f


def build_program():
    nc = bass.Bass("TRN2", target_bir_lowering=False)

    u_conv = nc.dram_tensor("u_conv", [P, CPC * NH * NB * NI], F32, kind="ExternalInput").ap()
    u_tok = nc.dram_tensor("u_tok", [P, NIL * D], F32, kind="ExternalInput").ap()
    w1_in = nc.dram_tensor("w1_in", [FF, D], F32, kind="ExternalInput").ap()
    w2_in = nc.dram_tensor("w2_in", [D, FF], F32, kind="ExternalInput").ap()
    b1c_in = nc.dram_tensor("b1c_in", [P, NFT], F32, kind="ExternalInput").ap()
    b2c_in = nc.dram_tensor("b2c_in", [P, NDT], F32, kind="ExternalInput").ap()
    cT_in = nc.dram_tensor("cT_in", [D, 2], F32, kind="ExternalInput").ap()
    wmod_ch = nc.dram_tensor("wmod_ch", [D, 3 * CPC], F32, kind="ExternalInput").ap()
    wmod_tok = nc.dram_tensor("wmod_tok", [D, 3 * D], F32, kind="ExternalInput").ap()
    arw_in = nc.dram_tensor("arw_in", [1, CPC * NM], F32, kind="ExternalInput").ap()
    aiw_in = nc.dram_tensor("aiw_in", [1, CPC * NM], F32, kind="ExternalInput").ap()
    ldt_row = nc.dram_tensor("ldt_row", [1, CPC * NM], F32, kind="ExternalInput").ap()
    d_in = nc.dram_tensor("d_in", [1, 8], F32, kind="ExternalInput").ap()
    ssmq = nc.dram_tensor("ssmq", [64, 192], F32, kind="ExternalInput").ap()
    ssmq2 = nc.dram_tensor("ssmq2", [32, 640], F32, kind="ExternalInput").ap()
    liota = nc.dram_tensor("liota", [P, 8], F32, kind="ExternalInput").ap()
    jmat = nc.dram_tensor("jmat", [P, P], F32, kind="ExternalInput").ap()
    ident = nc.dram_tensor("ident", [P, P], F32, kind="ExternalInput").ap()
    identb = nc.dram_tensor("identb", [P, P], BF16, kind="ExternalInput").ap()
    sel5 = nc.dram_tensor("sel5", [P, 3 * P], F32, kind="ExternalInput").ap()
    out_e = nc.dram_tensor("out_e", [D, TOK], F32, kind="ExternalOutput").ap()

    kbuf_d = nc.dram_tensor("kbuf_d", [1, KBUF_LEN], BF16)
    cz_d = nc.dram_tensor("cz_d", [64, CPC * 2 * 128], BF16)
    kimg_d = nc.dram_tensor("kimg_d", [P, CPC * 2 * 128], BF16)
    czt_d = nc.dram_tensor("czt_d", [1, 2 * CPC * NM], F32)
    modb_d = nc.dram_tensor("modb_d", [1, 4096], F32)
    ybo_d = nc.dram_tensor("ybo_d", [P, CPC * NH * NB * NI], F32)
    send_d = [nc.dram_tensor("send_d%d" % g, [8, TOK * 32], F32) for g in range(4)]
    recv_d = [nc.dram_tensor("recv_d%d" % g, [8, TOK * 32], F32) for g in range(4)]
    lnin_d = nc.dram_tensor("lnin_d", [P, 512], F32)
    lnout_d = nc.dram_tensor("lnout_d", [P, 512], F32)
    mx1i_d = nc.dram_tensor("mx1i_d", [1, 1], F32)
    mx1o_d = nc.dram_tensor("mx1o_d", [1, 1], F32)
    mx2i_d = nc.dram_tensor("mx2i_d", [1, 1], F32)
    mx2o_d = nc.dram_tensor("mx2o_d", [1, 1], F32)
    x2t_d = nc.dram_tensor("x2t_d", [1, D * TOK], F32)
    h1sp_d = nc.dram_tensor("h1sp_d", [1, NFT * NCH * P * 512], F32)

    RG = [list(range(8))]

    es = ExitStack()
    with es:
        A_h1 = es.enter_context(nc.sbuf_tensor("A_h1", [P, 32768], BF16))
        A_m1 = es.enter_context(nc.sbuf_tensor("A_m1", [P, 16384], BF16))
        A_zb = es.enter_context(nc.sbuf_tensor("A_zb", [P, 16384], BF16))
        A_tm = es.enter_context(nc.sbuf_tensor("A_tm", [P, 16384], BF16))
        A_st = es.enter_context(nc.sbuf_tensor("A_st", [64, 512], BF16))
        A_ys = es.enter_context(nc.sbuf_tensor("A_ys", [P, 2048], BF16))
        A_sm = es.enter_context(nc.sbuf_tensor("A_sm", [P, 2048], BF16))
        A_rw = es.enter_context(nc.sbuf_tensor("A_rw", [P, 8192], BF16))
        A_tp = es.enter_context(nc.sbuf_tensor("A_tp", [P, 4096], BF16))
        A_fs = es.enter_context(nc.sbuf_tensor("A_fs", [P, 6144], BF16))
        liota_s = es.enter_context(nc.sbuf_tensor("liota_s", [P, 8], F32))
        j_s = es.enter_context(nc.sbuf_tensor("j_s", [P, P], F32))
        id_s = es.enter_context(nc.sbuf_tensor("id_s", [P, P], F32))
        idb_s = es.enter_context(nc.sbuf_tensor("idb_s", [P, P], BF16))
        sel_s = es.enter_context(nc.sbuf_tensor("sel_s", [P, 3 * P], F32))
        pb = [es.enter_context(nc.psum_tensor("pb%d" % i, [P, 512], F32)) for i in range(8)]

        h1 = A_h1[:].rearrange("p (c h b i) -> p c h b i", c=CPC, h=NH, b=NB, i=NI)
        h1f2 = A_h1[:].rearrange("p (c t) -> p c t", c=CPC, t=256)
        m1f = A_m1[:].bitcast(F32)
        m1 = m1f.rearrange("p (c e n) -> p c e n", c=CPC, e=2, n=NM)
        bar = m1f.rearrange("p (c2 b i) -> p c2 b i", c2=64, b=NB, i=NI)
        toep = A_m1[:, 0:8192].rearrange("p (c k r) -> p c k r", c=32, k=2, r=128)
        czg = A_m1[:, 8192:16384].rearrange("p (c l r) -> p c l r", c=32, l=2, r=128)
        czg_pv = A_m1[:, 8192:16384].rearrange("p (c2 q l r) -> p c2 q l r", c2=16, q=2, l=2, r=128)
        zb = A_zb[:].rearrange("p (c k e n) -> p c k e n", c=CPC, k=2, e=2, n=NM)
        archb = A_zb[:, 0:8192].rearrange("p (c2 b j) -> p c2 b j", c2=64, b=NB, j=NI)
        tmf = A_tm[:].bitcast(F32)
        arch = tmf.rearrange("p (c2 b j) -> p c2 b j", c2=64, b=NB, j=NI)
        tpf = A_tp[:].bitcast(F32)

        def tslot(slot):
            return tpf[:, slot * 512:slot * 512 + 512]

        zbf32 = A_zb[:].bitcast(F32)
        u0 = zbf32[0:32, 0:128]
        u1 = zbf32[0:32, 128:256]
        u2 = zbf32[0:32, 256:384]
        u3 = zbf32[0:32, 384:512]
        u4 = zbf32[0:32, 512:640]
        u5 = zbf32[0:32, 640:768]
        u2a = zbf32[0:32, 768:896]
        v_zre = zbf32[0:32, 896:1024]
        v_zim = zbf32[0:32, 1024:1152]
        v_e = zbf32[0:32, 1152:1280]
        v_cr = zbf32[0:32, 1280:1408]
        v_ci = zbf32[0:32, 1408:1536]
        v_t = zbf32[0:32, 1536:1664]
        wmstv = zbf32[:, 2048:5120].rearrange("p (k g) -> p k g", k=8, g=384)
        wtst = [tmf[:, 0:1024], tmf[:, 1024:2048]]
        wtstv = [w.rearrange("p (k c) -> p k c", k=8, c=128) for w in wtst]
        sqst = A_tm[:, 4096:8192]

        rep = None
        stage = A_st[:].rearrange("k (c r) -> k c r", c=4, r=128)
        ysf = A_ys[:].bitcast(F32)
        rows = A_rw[:].bitcast(F32)
        R_AR, R_AI, R_DT = 0, 1, 2

        sm = A_sm[:].bitcast(F32)
        zrr = sm[:, 0:64]
        zi_t = sm[0:64, 64:128]
        p1v = sm[:, 128:256].rearrange("p (c b) -> p c b", c=64, b=2)
        t2v = sm[0:64, 256:384].rearrange("p (c b) -> p c b", c=64, b=2)
        t3v = sm[64:128, 384:512].rearrange("p (c b) -> p c b", c=64, b=2)
        czq = sm[0:64, 128:256]
        czh = sm[0:64, 256:384]
        kred = sm[:, 512:768].rearrange("p (l c) -> p l c", l=2, c=CPC)
        ktr = sm[:, 512:768].rearrange("p (l q) -> p l q", l=2, q=128)
        zq_re = sm[0:64, 768:832]
        zq_im = sm[0:64, 832:896]
        zq_t1 = sm[0:64, 896:960]
        zq_t2 = sm[0:64, 960:1024]
        dval = sm[0:1, 384:392]
        ctsil = sm[:, 400:416]
        mstg = sm[:, 416:422]
        tokt = sm[:, 422:424]
        gbt = sm[:, 424:440]


        x2a = A_h1[:].bitcast(F32)
        x2b = m1f
        x2c = tmf

        def x2chunk(il):
            if il < 16:
                return x2a[:, il * 1024:(il + 1) * 1024]
            if il < 24:
                return x2b[:, (il - 16) * 1024:(il - 16) * 1024 + 1024]
            return x2c[:, (il - 24) * 1024:(il - 24) * 1024 + 1024]

        x1qt = [A_zb[:, 0:4096], A_zb[:, 4096:8192], A_zb[:, 8192:12288], A_zb[:, 12288:16384],
                A_rw[:, 0:4096], A_rw[:, 4096:8192], A_tp[:, 0:4096], A_h1[:, 28672:32768]]
        w1qt = [A_m1[:, i * 4096:(i + 1) * 4096] for i in range(4)] + \
               [A_tm[:, i * 4096:(i + 1) * 4096] for i in range(4)]
        w2qt = [A_zb[:, i * 1024:(i + 1) * 1024] for i in range(16)] + \
               [A_rw[:, i * 1024:(i + 1) * 1024] for i in range(8)] + \
               [A_tm[:, i * 1024:(i + 1) * 1024] for i in range(8)]
        bstg1 = [A_h1[:, 0:1024], A_h1[:, 1024:2048]]
        bstg2 = [A_tm[:, 8192:9216], A_tm[:, 9216:10240]]
        x2qs = [A_h1[:, 0:16384], A_h1[:, 16384:32768], A_m1[:, 0:16384]]

        fsf = A_fs[:].bitcast(F32)
        wqin = fsf[:, 0:1024]
        h1st = [fsf[:, 1024 + i * 512:1024 + (i + 1) * 512] for i in range(4)]
        h1in = h1st
        outst = [fsf[:, i * 512:(i + 1) * 512] for i in range(2)]
        x2tst = fsf[:, 1024:2048]
        rep = [fsf[:, 0:512], fsf[:, 512:1024], fsf[:, 1024:1536]]
        ssmq2_s = fsf[0:32, 1536:2176]
        ssmq_s = fsf[0:64, 2176:2368]
        ct_s = fsf[:, 2368:2384]
        gar_s = fsf[:, 1536:1792]
        stat_s = fsf[:, 2384:2896]
        mods_s = fsf[:, 2896:2920]
        statv_m = stat_s[:, 0:256]
        statv_v = stat_s[:, 256:512]
        m_bc = statv_m.rearrange("p (o t) -> p o t", o=1).broadcast_to([P, 128, 256])
        r_bc = statv_v.rearrange("p (o t) -> p o t", o=1).broadcast_to([P, 128, 256])
        x2tstv = x2tst.rearrange("p (dt q) -> p dt q", dt=8, q=128)
        qstage = A_ys[:, 0:2048]
        qslotB = fsf[:, 2048:2560]

        fsm = sm
        mxc = fsm[:, 0:32]
        rsc = fsm[:, 32:64]
        mrs = fsm[:, 64:96]
        maxw1 = fsm[:, 96:128]
        rw1c = fsm[:, 128:160]
        sc1c = fsm[:, 160:192]
        b1qc = fsm[:, 192:224]
        b1cc = fsm[:, 224:256]
        maxw2 = fsm[:, 256:264]
        rw2c = fsm[:, 264:272]
        sc2g = fsm[:, 272:280]
        b2qg = fsm[:, 280:288]
        b2cc = fsm[:, 288:296]
        s1c = fsm[:, 300:301]
        is1c = fsm[:, 301:302]
        s2c = fsm[:, 302:303]
        is2c = fsm[:, 303:304]
        gro = fsm[0:1, 304:312]
        mx1col = fsm[:, 312:313]
        mx2col = fsm[:, 313:314]
        tcol = fsm[:, 314:315]
        tco2 = fsm[:, 315:316]
        trow = fsm[0:1, 316:320]
        csm1 = fsm[:, 320:352]
        csm2 = fsm[:, 352:384]
        qslotA = fsm[:, 384:896]
        gbs = fsm[:, 896:904]
        x2tin = [ysf[:, 0:512], qslotA]
        rstg = [ysf[:, 512:1024], tmf[:, 5120:5632]]

        SEMN = ["s_in", "s_h1", "s_rep", "s_repev", "s_arg", "s_trig", "s_m1", "s_zb", "s_zbev", "s_kred", "s_cm", "s_tp", "s_ev_v", "s_ev_s", "s_czd", "s_zq", "s_czq", "s_bmm", "s_bev_v", "s_bev_s", "s_scan", "s_tab", "s_tabm", "s_ymm", "s_yev", "s_yout", "s_ktr", "s_kimg", "s_kz", "s_czt", "s_silu", "s_mch", "s_mchv", "s_modb", "s_modr", "s_wts", "s_mtok", "s_mtev", "s_chrm", "s_chre", "s_stat", "s_lnb", "s_cc1", "s_h1m", "s_yga", "s_send", "s_send1", "s_send2", "s_send3", "s_recv", "s_utok", "s_acc", "s_x2t", "s_x2ve", "s_x2vs", "s_x2td", "s_h2r", "s_ln2s", "s_ln2f", "s_h2s", "s_h2v", "s_mx1tp", "s_mx1v", "s_mx1d", "s_mx1r", "s_s1row", "s_s1rm", "s_colq", "s_bc", "s_sq", "s_wmi1", "s_w1a", "s_wmv1", "s_w1mx", "s_w1i", "s_w1s", "s_w1v", "s_qs", "s_q", "s_qt", "s_qev", "s_qes", "s_mm1", "s_ev1", "s_sp", "s_w2tp", "s_w2ev", "s_w2es", "s_m2acc", "s_hin", "s_gsend", "s_rmg", "s_r2s", "s_mm2", "s_e2s", "s_e2v", "s_xin", "s_out", "s_vch", "s_sch"]
        SEM = {}
        for nm_ in SEMN:
            SEM[nm_] = es.enter_context(nc.semaphore(nm_))

        def W(eng, s, v):
            if v > 0:
                eng.wait_ge(SEM[s], v)

        block = es.enter_context(nc.Block())

        @block.sync
        def _(sy):
            W(sy, "s_kz", 2)
            n_in = 0
            for dst_r, src in [(R_AR, arw_in), (R_AI, aiw_in), (R_DT, ldt_row)]:
                sy.dma_start(out=rows[dst_r:dst_r + 1, :], in_=src[:, :]).then_inc(SEM["s_in"], 16); n_in += 1
            sy.dma_start(out=dval[:, :], in_=d_in[:, :]).then_inc(SEM["s_in"], 16); n_in += 1
            sy.dma_start(out=liota_s[:], in_=liota[:, :]).then_inc(SEM["s_in"], 16); n_in += 1
            sy.dma_start(out=j_s[:], in_=jmat[:, :]).then_inc(SEM["s_in"], 16); n_in += 1
            sy.dma_start(out=id_s[:], in_=ident[:, :]).then_inc(SEM["s_in"], 16); n_in += 1
            sy.dma_start(out=idb_s[:], in_=identb[:, :]).then_inc(SEM["s_in"], 16); n_in += 1
            sy.dma_start(out=sel_s[:], in_=sel5[:, :]).then_inc(SEM["s_in"], 16); n_in += 1
            sy.dma_start(out=ssmq_s[:], in_=ssmq[:, :]).then_inc(SEM["s_in"], 16); n_in += 1
            sy.dma_start(out=ssmq2_s[:], in_=ssmq2[:, :]).then_inc(SEM["s_in"], 16); n_in += 1
            sy.dma_start(out=ct_s[:].rearrange("p (k o) -> p k o", k=8, o=2),
                         in_=cT_in.rearrange("(k p) o -> p k o", k=8),
                         allow_slow_non_contiguous=True).then_inc(SEM["s_in"], 16); n_in += 1
            sy.dma_start(out=wmstv, in_=wmod_ch.rearrange("(k p) g -> p k g", k=8)
                         ).then_inc(SEM["s_in"], 16); n_in += 1
            assert n_in * 16 == T_IN, n_in
            sy.dma_start(out=rap(kbuf_d, 0, [(383, 128), (1, 127)]),
                         in_=A_ys[:, 0:127]).then_inc(SEM["s_kz"], 16)
            sy.dma_start(out=rap(kbuf_d, CPC * 383, [(512, 1), (1, 512)]),
                         in_=A_ys[0:1, 0:512]).then_inc(SEM["s_kz"], 16)
            def wtst_blk(blk):
                if blk >= 1:
                    W(sy, "s_mtok", blk)
                sy.dma_start(out=wtstv[blk % 2],
                             in_=wmod_tok.rearrange("(k p) f -> p k f", k=8)[:, :, blk * 128:(blk + 1) * 128]
                             ).then_inc(SEM["s_wts"], 16)

            for blk in range(0, 4):
                wtst_blk(blk)
            for g in range(3):
                W(sy, "s_mchv", g + 1)
                sy.dma_start(out=rap(modb_d, g * 256, [(2, 128), (1, 2)]),
                             in_=mstg[:, 2 * g:2 * g + 2]).then_inc(SEM["s_modb"], 16)
            W(sy, "s_modb", 48)
            sy.dma_start(out=rows[64:65, 0:768], in_=modb_d.ap()[:, 0:768]).then_inc(SEM["s_modr"], 16)
            for blk in range(4, 8):
                wtst_blk(blk)
            W(sy, "s_czt", 1)
            for t_i, t_src in enumerate([v_cr, v_ci]):
                dst = rap(czt_d, t_i * CPC * NM, [(1, 32), (32, 128)])
                sy.dma_start(out=dst, in_=t_src, allow_slow_non_contiguous=True).then_inc(SEM["s_czt"], 16)
            W(sy, "s_czt", 33)
            sy.dma_start(out=rows[32:33, :], in_=czt_d.ap()[:, 0:CPC * NM]).then_inc(SEM["s_czt"], 16)
            sy.dma_start(out=rows[33:34, :], in_=czt_d.ap()[:, CPC * NM:2 * CPC * NM]).then_inc(SEM["s_czt"], 16)
            for blk in range(8, 11):
                wtst_blk(blk)
            W(sy, "s_stat", 1)
            sy.dma_start(out=lnin_d.ap()[:, :], in_=stat_s[:, :]).then_inc(SEM["s_lnb"], 16)
            W(sy, "s_cc1", 1)
            sy.dma_start(out=stat_s[:, :], in_=lnout_d.ap()[:, :]).then_inc(SEM["s_lnb"], 16)
            for blk in range(11, 24):
                wtst_blk(blk)

            def cz_blk(kb):
                lh, cb = kb // 32, kb % 32
                k_hi = lh * 128 + (cb + 1) * 4 - 1
                W(sy, "s_ev_v", cnt_upto(ev_owner, k_hi, "v"))
                W(sy, "s_ev_s", cnt_upto(ev_owner, k_hi, "s"))
                dst = rap(cz_d, lh * 128 + cb * 4 * 256, [(CPC * 2 * 128, 64), (256, 4), (1, 128)])
                sy.dma_start(out=dst, in_=stage[:, :, :]).then_inc(SEM["s_czd"], 16)

            for kb in range(0, 32):
                cz_blk(kb)
            W(sy, "s_mtev", 24)
            sy.dma_start(out=rap(modb_d, 768, [(1, 128), (128, 8)]),
                         in_=mods_s[:, 8:16], allow_slow_non_contiguous=True).then_inc(SEM["s_modb"], 16)
            sy.dma_start(out=rap(modb_d, 768 + 1024, [(1, 128), (128, 8)]),
                         in_=mods_s[:, 0:8], allow_slow_non_contiguous=True).then_inc(SEM["s_modb"], 16)
            W(sy, "s_modb", 80)
            sy.dma_start(out=rows[64:65, 768:2816], in_=modb_d.ap()[:, 768:2816]).then_inc(SEM["s_modr"], 16)
            for kb in range(32, 64):
                cz_blk(kb)
            W(sy, "s_kz", 34)
            W(sy, "s_h1", 80)
            for blk in range(2):
                kimg_w = rap(kimg_d, blk * 128, [(256, CPC), (32768, P), (1, 128)])
                kimg_r = rap(kbuf_d, 127 + 128 * blk, [(KB_STRIDE, CPC), (-1, P), (1, 128)])
                sy.dma_start(out=kimg_w, in_=kimg_r).then_inc(SEM["s_kimg"], 16)
            W(sy, "s_scan", 2)
            sy.dma_start(out=b1cc[:, :], in_=b1c_in[:, :]).then_inc(SEM["s_bc"], 16)
            sy.dma_start(out=b2cc[:, :], in_=b2c_in[:, :]).then_inc(SEM["s_bc"], 16)
            ybv = ybo_d.ap().rearrange("p (c h b i) -> p c h b i", c=CPC, h=NH, b=NB, i=NI)
            for g in range(4):
                W(sy, "s_kimg", 32)
                if g == 0:
                    W(sy, "s_scan", 2)
                else:
                    W(sy, "s_ymm", g * 32)
                sy.dma_start(out=A_m1[:, 0:8192],
                             in_=kimg_d.ap()[:, g * 8192:(g + 1) * 8192]).then_inc(SEM["s_tab"], 16)
                W(sy, "s_tabm", g + 1)
                for reim in range(2):
                    for par_ in range(2):
                        dst2 = czg_pv[reim * 64 + par_ * 32:reim * 64 + par_ * 32 + 32, :, par_, :, :]
                        srcv = cz_d.ap().rearrange("k (gq c2 q l r) -> k gq c2 q l r",
                                                   gq=4, c2=16, q=2, l=2, r=128)
                        src2 = srcv[reim * 32:(reim + 1) * 32, g, :, par_, :, :]
                        sy.dma_start(out=dst2, in_=src2).then_inc(SEM["s_tab"], 16)
                for blk in range(g * 8, (g + 1) * 8):
                    W(sy, "s_yga", (blk + 1) * 4)
                    sy.dma_start(out=ybo_d.ap()[:, blk * 1024:(blk + 1) * 1024],
                                 in_=ysf[:, 0:1024]).then_inc(SEM["s_yout"], 16)
                W(sy, "s_yout", (g + 1) * 8 * 16)
                sgsem = ["s_send", "s_send1", "s_send2", "s_send3"][g]
                for s in range(3):
                    b_s, j_q = s // 4, s % 4
                    src = ybv[:, g * 32:(g + 1) * 32, :, b_s, j_q * 16:(j_q + 1) * 16]
                    dst = rap(send_d[g], s * TOK * 32, [(1024, P), (32, 32), (16, 2), (1, 16)])
                    sy.dma_start(out=dst, in_=src).then_inc(SEM[sgsem], 16)
            for il in range(NIL):
                W(sy, "s_ymm", 128)
                sy.dma_start(out=x2chunk(il), in_=u_tok[:, il * 1024:(il + 1) * 1024]).then_inc(SEM["s_utok"], 16)
            for il in range(NIL):
                W(sy, "s_x2ve", (il + 1) * 4)
                W(sy, "s_x2vs", (il + 1) * 4)
                dst = rap(x2t_d, il * 128, [(TOK, 128), (TOK * 128, 8), (1, 128)])
                sy.dma_start(out=dst, in_=x2tstv).then_inc(SEM["s_x2td"], 16)
                k = il
                if k >= 1:
                    W(sy, "s_wmv1", k)
                sy.dma_start(out=wqin, in_=w1_in[k * 128:(k + 1) * 128, :]).then_inc(SEM["s_wmi1"], 16)
            W(sy, "s_mx1v", 1)
            sy.dma_start(out=mx1i_d.ap()[:, 0:1], in_=gro[:, 0:1]).then_inc(SEM["s_mx1d"], 16)
            W(sy, "s_cc1", 6)
            sy.dma_start(out=gro[:, 1:2], in_=mx1o_d.ap()[:, 0:1]).then_inc(SEM["s_mx1r"], 16)
            W(sy, "s_wmv1", 32)
            for k in range(32):
                if k >= 1:
                    W(sy, "s_w1v", k)
                sy.dma_start(out=wqin, in_=w1_in[k * 128:(k + 1) * 128, :]).then_inc(SEM["s_w1i"], 16)
            W(sy, "s_mx1v", 2)
            sy.dma_start(out=mx2i_d.ap()[:, 0:1], in_=gro[:, 4:5]).then_inc(SEM["s_mx1d"], 16)
            W(sy, "s_cc1", 7)
            sy.dma_start(out=gro[:, 5:6], in_=mx2o_d.ap()[:, 0:1]).then_inc(SEM["s_mx1r"], 16)
            W(sy, "s_wmv1", 64)
            for k in range(32):
                dt, fq = k // 4, k % 4
                if k >= 1:
                    W(sy, "s_w1v", 32 + k)
                sy.dma_start(out=wqin,
                             in_=w2_in[dt * 128:(dt + 1) * 128, fq * 1024:(fq + 1) * 1024]
                             ).then_inc(SEM["s_w1i"], 16)
            W(sy, "s_sp", NFT * NCH * 16)
            W(sy, "s_w2tp", 256)
            sptr = {"x": 0, "o": 0}

            def xin_dma(j):
                chg, dt, par_ = j // 16, (j % 16) // 2, j % 2
                och = 2 * chg + par_
                if j >= 2:
                    W(sy, "s_e2v", j - 1)
                sy.dma_start(out=x2tin[j % 2],
                             in_=rap(x2t_d, dt * 128 * TOK + och * 512, [(TOK, 128), (1, 512)])
                             ).then_inc(SEM["s_xin"], 16)

            def out_dma(j):
                chg, dt, par_ = j // 16, (j % 16) // 2, j % 2
                och = 2 * chg + par_
                W(sy, "s_e2v", j + 1)
                sy.dma_start(out=out_e[dt * 128:(dt + 1) * 128, och * 512:(och + 1) * 512],
                             in_=outst[j % 2]).then_inc(SEM["s_out"], 16)

            def io_below(lim):
                while sptr["x"] < min(lim, 64):
                    if sptr["o"] < sptr["x"] - 1:
                        out_dma(sptr["o"]); sptr["o"] += 1
                    xin_dma(sptr["x"]); sptr["x"] += 1
                while sptr["o"] < min(lim - 1, 63):
                    out_dma(sptr["o"]); sptr["o"] += 1

            for ch in range(NCH):
                if ch >= 3:
                    io_below(16 * ((ch - 3) // 2 + 1))
                for ft in range(NFT):
                    k2 = ch * NFT + ft
                    if k2 >= 4:
                        W(sy, "s_r2s", k2 - 3)
                    blkN = ft * NCH + ch
                    sy.dma_start(out=h1in[k2 % 4],
                                 in_=rap(h1sp_d, blkN * P * 512, [(512, P), (1, 512)])
                                 ).then_inc(SEM["s_hin"], 16)
            io_below(64)
            out_dma(63)
            W(sy, "s_out", 64 * 16)

        @block.gpsimd
        def _(g):
            g.memset(A_ys[:, 0:512], 0.0).then_inc(SEM["s_kz"], 1)
            g.memset(A_rw[:, :], 0.0).then_inc(SEM["s_kz"], 1)
            for q_ in range(4):
                g.dma_start(out=A_h1[:, q_ * 8192:(q_ + 1) * 8192],
                            in_=u_conv[:, q_ * 8192:(q_ + 1) * 8192]).then_inc(SEM["s_h1"], 16)
            W(g, "s_lnb", 16)
            g.collective_compute("AllReduce", ALU.add, replica_groups=RG,
                                 ins=[lnin_d.ap().opt()], outs=[lnout_d.ap().opt()]).then_inc(SEM["s_cc1"], 1)
            W(g, "s_ktr", 3)
            g.dma_start(out=rap(kbuf_d, 127, [(383, 128), (128, 2), (1, 128)]),
                        in_=ktr[:, :, :]).then_inc(SEM["s_h1"], 16)
            def tabm_memset(g_i):
                if g_i == 0:
                    W(g, "s_scan", 2)
                else:
                    W(g, "s_ymm", g_i * 32)
                g.memset(A_m1[:, 8192:16384], 0.0).then_inc(SEM["s_tabm"], 1)

            ybg = ybo_d.ap().rearrange("p (c h b i) -> p c h b i", c=CPC, h=NH, b=NB, i=NI)

            def gp_sends(g_i):
                W(g, "s_yout", (g_i + 1) * 8 * 16)
                if g_i >= 1:
                    W(g, "s_cc1", g_i + 1)
                for s in range(6, 8):
                    b_s, j_q = s // 4, s % 4
                    srcg = ybg[:, g_i * 32:(g_i + 1) * 32, :, b_s, j_q * 16:(j_q + 1) * 16]
                    dstg = rap(send_d[g_i], s * TOK * 32, [(1024, P), (32, 32), (16, 2), (1, 16)])
                    g.dma_start(out=dstg, in_=srcg).then_inc(SEM["s_gsend"], 16)

            def a2a(g_i):
                W(g, ["s_send", "s_send1", "s_send2", "s_send3"][g_i], 6 * 16)
                W(g, "s_gsend", (g_i + 1) * 2 * 16)
                with allow_ici_all_to_all():
                    g.collective_compute("AllToAll", ALU.bypass, replica_groups=RG,
                                         ins=[send_d[g_i].ap().opt()],
                                         outs=[recv_d[g_i].ap().opt()]).then_inc(SEM["s_cc1"], 1)

            tabm_memset(0)
            tabm_memset(1)
            gp_sends(0)
            a2a(0)
            tabm_memset(2)
            gp_sends(1)
            a2a(1)
            tabm_memset(3)
            gp_sends(2)
            a2a(2)
            gp_sends(3)
            a2a(3)
            for n in range(32):
                g_i, src = n // 8, n % 8
                W(g, "s_cc1", g_i + 2)
                if n >= 2:
                    W(g, "s_acc", 3 * (n - 1))
                g.dma_start(out=tpf[:, (n % 2) * 1024:(n % 2) * 1024 + 1024],
                            in_=rap(recv_d[g_i], src * TOK * 32, [(1024, P), (1, 1024)])
                            ).then_inc(SEM["s_recv"], 16)
            W(g, "s_mx1d", 16)
            g.collective_compute("AllReduce", ALU.max, replica_groups=RG,
                                 ins=[mx1i_d.ap().opt()], outs=[mx1o_d.ap().opt()]).then_inc(SEM["s_cc1"], 1)
            W(g, "s_qt", 256)
            for blkN in range(NFT * NCH):
                W(g, "s_ev1", blkN + 1)
                g.dma_start(out=rap(h1sp_d, blkN * P * 512, [(512, P), (1, 512)]),
                            in_=h1st[blkN % 4]).then_inc(SEM["s_sp"], 16)
                if blkN % 8 == 0:
                    k = blkN // 8
                    dt, fq = k // 4, k % 4
                    if k >= 1:
                        W(g, "s_wmv1", 32 + k)
                    g.dma_start(out=ysf[:, 0:1024],
                                in_=w2_in[dt * 128:(dt + 1) * 128, fq * 1024:(fq + 1) * 1024]
                                ).then_inc(SEM["s_gsend"], 16)
            W(g, "s_mx1d", 32)
            g.collective_compute("AllReduce", ALU.max, replica_groups=RG,
                                 ins=[mx2i_d.ap().opt()], outs=[mx2o_d.ap().opt()]).then_inc(SEM["s_cc1"], 1)

        @block.tensor
        def _(te):
            W(te, "s_in", T_IN)
            W(te, "s_silu", 1)
            for grp in range(3):
                for k in range(8):
                    te.matmul(pb[grp][:, 0:2], wmstv[:, k, grp * 128:(grp + 1) * 128],
                              ctsil[:, 2 * k:2 * k + 2], start=(k == 0), stop=(k == 7)
                              ).then_inc(SEM["s_mch"], 1 if k == 7 else 0)
            for rnd in range(48):
                lh, kind, sub = round_kind(rnd)
                if rnd == 8:
                    W(te, "s_czt", 65)
                if rnd == 24:
                    W(te, "s_ev_v", cnt_upto(ev_owner, 127, "v"))
                    W(te, "s_ev_s", cnt_upto(ev_owner, 127, "s"))
                W(te, "s_repev", rnd)
                sels = (0, 1, 2) if kind == "m1" else (3, 4)
                sl = slice(sub * RW, (sub + 1) * RW)
                for i_, j_ in enumerate(sels):
                    if kind == "m1":
                        lhsT, rhs2 = sel_s[0:3, j_ * P:(j_ + 1) * P], rows[0:3, sl]
                    else:
                        lhsT, rhs2 = sel_s[32:34, (j_ - 3) * P:(j_ - 2) * P], rows[32:34, sl]
                    mm = te.matmul(pb[4 + i_][:, 0:512], lhsT, rhs2, start=True, stop=True)
                    if i_ == len(sels) - 1:
                        mm.then_inc(SEM["s_rep"], 1)
                if rnd == 3:
                    W(te, "s_modr", 16)
                    W(te, "s_mtev", 3)
                    te.matmul(pb[7][:, 0:256], sel_s[64:65, 0:128], rows[64:65, 512:768],
                              start=True, stop=True).then_inc(SEM["s_chrm"], 1)
                if rnd < 24:
                    blk = rnd
                    if blk == 3:
                        W(te, "s_chre", 1)
                    W(te, "s_mtev", blk)
                    W(te, "s_wts", (blk + 1) * 16)
                    for k in range(8):
                        te.matmul(pb[7][:, 0:2], wtstv[blk % 2][:, k, :],
                                  ctsil[:, 2 * k:2 * k + 2], start=(k == 0), stop=(k == 7)
                                  ).then_inc(SEM["s_mtok"], 1 if k == 7 else 0)
                if rnd == 25:
                    W(te, "s_mtev", 24)
                    W(te, "s_chre", 2)
                    te.matmul(pb[7][:, 0:512], sel_s[64:65, 0:128], rows[64:65, 0:512],
                              start=True, stop=True).then_inc(SEM["s_chrm"], 1)
                if rnd == 39:
                    W(te, "s_kred", 18)
                    W(te, "s_zbev", 32)
                    te.transpose(pb[0][:, 0:128], kred[:, 0, :], id_s[:])
                    te.transpose(pb[1][:, 0:128], kred[:, 1, :], id_s[:]).then_inc(SEM["s_ktr"], 1)
                if kind == "m1" and sub == 7:
                    W(te, "s_m1", (lh + 1) * 8)
                    for jj in range(16):
                        if lh * 16 + jj >= 4:
                            W(te, "s_zbev", lh * 16 + jj - 3)
                        te.matmul(pb[jj % 4][:, 0:512], j_s[:], m1f[:, jj * 512:(jj + 1) * 512],
                                  start=True, stop=True).then_inc(SEM["s_zb"], 1)
                if kind == "cm" and sub == 7:
                    W(te, "s_cm", (lh + 1) * 8)
                    if lh == 1:
                        W(te, "s_ktr", 3)
                    for c in range(CPC):
                        k = lh * CPC + c
                        if k >= 6:
                            ko = k - 6
                            W(te, "s_ev_" + ev_owner(ko), cnt_upto(ev_owner, ko, ev_owner(ko)))
                        te.transpose(pb[k % 6][0:64, 0:128], m1[:, c, :, :], id_s[:]).then_inc(SEM["s_tp"], 1)
            W(te, "s_czd", 64 * 16)
            W(te, "s_h1", T_H1)
            W(te, "s_h1m", 4)
            W(te, "s_zbev", 32)
            for c in range(CPC):
                if c >= 8:
                    co = c - 8
                    W(te, "s_bev_" + bev_owner(co), cnt_upto(bev_owner, co, bev_owner(co)))
                bk = pb[c % 8]
                te.matmul(bk[0:64, 0:128], zb[:, c, 0, :, :], h1[:, c, 0, :, :], start=True, stop=False)
                te.matmul(bk[0:64, 0:128], zb[:, c, 1, :, :], h1[:, c, 1, :, :], start=False, stop=True
                          ).then_inc(SEM["s_bmm"], 1)
            for g_i in range(4):
                W(te, "s_tab", (g_i + 1) * 80)
                if g_i == 0:
                    W(te, "s_scan", 2)
                for cl in range(32):
                    c = g_i * 32 + cl
                    if c >= 4:
                        W(te, "s_yev", c - 3)
                    ylo = pb[(c % 4) * 2][:, 0:128]
                    yhi = pb[(c % 4) * 2 + 1][:, 0:128]
                    sarch = archb[:, c // 2, :, :]
                    te.matmul(ylo, toep[:, cl, 0, :], h1[:, c, 0, :, :], start=True, stop=False)
                    te.matmul(yhi, toep[:, cl, 0, :], h1[:, c, 1, :, :], start=True, stop=False)
                    te.matmul(yhi, toep[:, cl, 1, :], h1[:, c, 0, :, :], start=False, stop=False)
                    te.matmul(ylo, czg[:, cl, 0, :], sarch, start=False, stop=True)
                    te.matmul(yhi, czg[:, cl, 1, :], sarch, start=False, stop=True).then_inc(SEM["s_ymm"], 1)
            W(te, "s_yev", 128)
            W(te, "s_modr", 32)
            W(te, "s_chre", 3)
            te.matmul(pb[4][:, 0:512], sel_s[64:65, 0:128], rows[64:65, 768:1280], start=True, stop=True)
            te.matmul(pb[5][:, 0:512], sel_s[64:65, 0:128], rows[64:65, 1280:1792], start=True, stop=True)
            te.matmul(pb[6][:, 0:512], sel_s[64:65, 0:128], rows[64:65, 1792:2304], start=True, stop=True)
            te.matmul(pb[7][:, 0:512], sel_s[64:65, 0:128], rows[64:65, 2304:2816], start=True, stop=True
                      ).then_inc(SEM["s_h2r"], 4)
            W(te, "s_acc", 96)
            for il in range(NIL):
                for dt in range(8):
                    n = il * 8 + dt
                    if n >= 4:
                        W(te, "s_x2ve", x2t_vcnt(n - 4))
                        W(te, "s_x2vs", x2t_scnt(n - 4))
                    te.transpose(pb[n % 4][:, 0:128], x2chunk(il)[:, dt * 128:(dt + 1) * 128],
                                 id_s[:]).then_inc(SEM["s_x2t"], 1)
            W(te, "s_h2v", 32)
            te.transpose(pb[4][0:1, 0:128], mx1col, id_s[:]).then_inc(SEM["s_mx1tp"], 1)
            W(te, "s_s1row", 1)
            te.matmul(pb[5][:, 0:2], sel_s[0:1, 0:128], gro[:, 2:4], start=True, stop=True
                      ).then_inc(SEM["s_s1rm"], 1)
            n_t = 0

            def qw_slot(kind, k):
                nonlocal n_t
                if kind == "q":
                    W(te, "s_q", k + 1)
                else:
                    W(te, "s_w1v", k + 1)
                for j in range(8):
                    if n_t >= 4:
                        gv = qwt_gv(n_t - 3)
                        W(te, "s_qev", gv)
                        W(te, "s_qes", (n_t - 3) - gv)
                    if kind == "q":
                        srcq = qstage[:, (k % 2) * 1024 + j * 128:(k % 2) * 1024 + (j + 1) * 128]
                    else:
                        srcq = bstg1[k % 2][:, j * 128:(j + 1) * 128]
                    te.matmul(pb[n_t % 4][:, 0:128], srcq, idb_s[:],
                              start=True, stop=True).then_inc(SEM["s_qt"], 1)
                    n_t += 1

            for pos in range(32):
                qw_slot("q", pos)
            vev_all = [x for x in range(NTRS) if QWT_OWNER[x] == "v"]
            sev_all = [x for x in range(NTRS) if QWT_OWNER[x] == "s"]
            vpos = {x: i + 1 for i, x in enumerate(vev_all)}
            spos = {x: i + 1 for i, x in enumerate(sev_all)}

            def piece_gates(ft):
                base = (32 + ft) * 8
                xs = range(base, base + 8)
                return (max(vpos[x] for x in xs if x in vpos),
                        max(spos[x] for x in xs if x in spos))

            W(te, "s_qt", 32 * 8)
            W(te, "s_qev", sum(1 for x in vev_all if x < 256))
            W(te, "s_qes", sum(1 for x in sev_all if x < 256))
            for ft in range(NFT):
                qw_slot("w", ft)
                gv_, gs_ = piece_gates(ft)
                W(te, "s_qev", gv_)
                W(te, "s_qes", gs_)
                for ch in range(NCH):
                    blkN = ft * 8 + ch
                    if blkN >= 4:
                        W(te, "s_ev1", blkN - 3)
                    for kt in range(8):
                        te.matmul(pb[4 + ch % 4][:, 0:512], w1qt[kt][:, ft * 128:(ft + 1) * 128],
                                  x1qt[kt][:, ch * 512:(ch + 1) * 512],
                                  start=(kt == 0), stop=(kt == 7)).then_inc(
                            SEM["s_mm1"], 1 if kt == 7 else 0)
            W(te, "s_m2acc", 256)
            W(te, "s_ev1", 256)
            te.transpose(pb[4][0:1, 0:128], mx2col, id_s[:]).then_inc(SEM["s_mx1tp"], 1)
            W(te, "s_s1row", 2)
            te.matmul(pb[5][:, 0:2], sel_s[0:1, 0:128], gro[:, 6:8], start=True, stop=True
                      ).then_inc(SEM["s_s1rm"], 1)
            n2 = 0
            for k in range(32):
                W(te, "s_w1v", 32 + k + 1)
                for j in range(8):
                    if n2 >= 4:
                        W(te, "s_w2ev", 4 * ((n2 - 4) // 8) + min((n2 - 4) % 8 + 1, 4))
                        W(te, "s_w2es", 4 * ((n2 - 4) // 8) + max((n2 - 4) % 8 - 3, 0))
                    te.matmul(pb[n2 % 4][:, 0:128], bstg2[k % 2][:, j * 128:(j + 1) * 128],
                              idb_s[:], start=True, stop=True).then_inc(SEM["s_w2tp"], 1)
                    n2 += 1
            for chg in range(4):
                W(te, "s_rmg", (2 * chg + 2) * 32)
                for dt in range(NDT):
                    b2 = chg * 8 + dt
                    if b2 >= 4:
                        W(te, "s_e2s", 2 * (b2 - 4) + 2)
                    if chg == 0:
                        W(te, "s_w2ev", 4 * (4 * dt + 4))
                        W(te, "s_w2es", 4 * (4 * dt + 4))
                    ps0 = pb[(dt % 4) * 2][:, 0:512]
                    ps1 = pb[(dt % 4) * 2 + 1][:, 0:512]
                    sl0 = x2qs[(2 * chg) % 3]
                    sl1 = x2qs[(2 * chg + 1) % 3]
                    for kt2 in range(32):
                        lhsT = w2qt[kt2][:, dt * 128:(dt + 1) * 128]
                        te.matmul(ps0, lhsT, sl0[:, kt2 * 512:(kt2 + 1) * 512],
                                  start=(kt2 == 0), stop=(kt2 == 31)).then_inc(
                            SEM["s_mm2"], 1 if kt2 == 31 else 0)
                        te.matmul(ps1, lhsT, sl1[:, kt2 * 512:(kt2 + 1) * 512],
                                  start=(kt2 == 0), stop=(kt2 == 31)).then_inc(
                            SEM["s_mm2"], 1 if kt2 == 31 else 0)

        @block.vector
        def _(ve):
            ve = DrainEng(ve, SEM["s_vch"], multi=[
                id(SEM[n]) for n in ("s_czq", "s_zq", "s_czt", "s_ktr", "s_sq",
                                     "s_stat", "s_ln2s")])
            W(ve, "s_in", T_IN)
            ar2, ai2 = ssmq2_s[:, 0:128], ssmq2_s[:, 128:256]
            dt2, cr2, ci2 = ssmq2_s[:, 256:384], ssmq2_s[:, 384:512], ssmq2_s[:, 512:640]
            ve.tensor_tensor(u0, ar2, dt2, ALU.add).then_inc(SEM["s_czq"], 1)
            W(ve, "s_czq", 3)
            ve.tensor_tensor(u2, ai2, u1, ALU.mult)
            for xi, (xs, czx) in enumerate([(1.0, None), (256.0, czq), (128.0, czh)]):
                ve.tensor_scalar_mul(u3, u2, xs)
                ve.tensor_scalar_mul(u5, u3, INV_2PI)
                ve.tensor_scalar(u1, u5, 0.25, None, ALU.add)
                ve.tensor_scalar(u5, u5, M23, -M23, ALU.add, ALU.add)
                ve.tensor_scalar(u1, u1, M23, -M23, ALU.add, ALU.add)
                ve.scalar_tensor_tensor(u1, u1, -TWO_PI, u3, ALU.mult, ALU.add)
                ve.scalar_tensor_tensor(u3, u5, -TWO_PI, u3, ALU.mult, ALU.add)
                ve.tensor_scalar(u1, u1, -1.5 * TWO_PI / 2.0, 0.5 * TWO_PI / 2.0, ALU.max, ALU.min)
                ve.tensor_scalar(u3, u3, -TWO_PI / 2.0, TWO_PI / 2.0, ALU.max, ALU.min)
                ve.tensor_scalar_mul(u4, u0, xs).then_inc(SEM["s_czq"], 1)
                W(ve, "s_czq", {0: 6, 1: 12, 2: 15}[xi])
                zre, zim = u1, u3
                ve.tensor_tensor(zre, zre, u4, ALU.mult)
                ve.tensor_tensor(zim, zim, u4, ALU.mult)
                if czx is None:
                    ve.tensor_copy(v_zre, zre)
                    ve.tensor_copy(v_zim, zim).then_inc(SEM["s_czq"], 1)
                    W(ve, "s_czq", 8)
                    ve.tensor_scalar(u2a, v_zre, -1.0, None, ALU.add)
                    ve.tensor_tensor(v_cr, cr2, u2a, ALU.mult)
                    ve.tensor_tensor(u5, ci2, v_zim, ALU.mult)
                    ve.tensor_tensor(v_cr, v_cr, u5, ALU.subtract)
                    ve.tensor_tensor(v_ci, cr2, v_zim, ALU.mult)
                    ve.tensor_tensor(u5, ci2, u2a, ALU.mult)
                    ve.tensor_tensor(v_ci, v_ci, u5, ALU.add)
                    ve.tensor_tensor(u5, v_e, v_e, ALU.mult)
                    ve.tensor_tensor(u2a, ai2, ai2, ALU.mult)
                    ve.tensor_tensor(u5, u5, u2a, ALU.add)
                    ve.reciprocal(u2a, u5)
                    ve.drain()
                    ve.tensor_tensor(u5, v_cr, v_e, ALU.mult)
                    ve.tensor_tensor(v_t, v_ci, ai2, ALU.mult)
                    ve.tensor_tensor(u5, v_t, u5, ALU.subtract)
                    ve.tensor_tensor(u5, u5, u2a, ALU.mult)
                    ve.tensor_tensor(v_t, v_ci, v_e, ALU.mult)
                    ve.tensor_tensor(v_ci, v_cr, ai2, ALU.mult)
                    ve.tensor_tensor(v_ci, v_ci, v_t, ALU.add)
                    ve.tensor_scalar_mul(v_ci, v_ci, -1.0)
                    ve.tensor_tensor(v_ci, v_ci, u2a, ALU.mult)
                    ve.tensor_copy(v_cr, u5).then_inc(SEM["s_czt"], 1)
                    ve.drain()
                    ve.sem_inc(SEM["s_czq"], 1)
                    continue
                ve.tensor_tensor(u2a, v_cr, zre, ALU.mult)
                ve.tensor_tensor(u5, v_ci, zim, ALU.mult)
                ve.tensor_tensor(u2a, u2a, u5, ALU.subtract)
                ve.tensor_copy(czx[0:32, :], u2a)
                ve.tensor_tensor(u2a, v_cr, zim, ALU.mult)
                ve.tensor_tensor(u5, v_ci, zre, ALU.mult)
                ve.tensor_tensor(u2a, u2a, u5, ALU.add)
                ve.tensor_copy(czx[32:64, :], u2a)
            ntrig = 0
            for rnd in range(48):
                lh, kind, sub = round_kind(rnd)
                W(ve, "s_rep", rnd + 1)
                nmm = 3 if kind == "m1" else 2
                for i_ in range(nmm):
                    cp = ve.tensor_copy(rep[i_][:, :], pb[4 + i_][:, 0:512])
                    if i_ == nmm - 1:
                        cp.then_inc(SEM["s_repev"], 1)
                c0 = sub * 16
                if kind == "m1":
                    if lh == 1 and sub == 0:
                        W(ve, "s_tp", 128)
                        W(ve, "s_zb", 16)
                    sA, sB, sC, sD = tslot(0), tslot(1), tslot(2), tslot(3)
                    lio = liota_s[:, lh:lh + 1].broadcast_to([P, 512])
                    ve.tensor_tensor(sA, rep[0][:, :], rep[2][:, :], ALU.add).then_inc(SEM["s_arg"], 1)
                    ntrig += 1
                    W(ve, "s_trig", 3 * ntrig - 2)
                    ve.tensor_tensor(sB, rep[1][:, :], sB, ALU.mult)
                    ve.tensor_tensor(sB, sB, lio, ALU.mult)
                    ve.tensor_scalar_mul(sC, sB, INV_2PI)
                    ve.tensor_scalar(sD, sC, 0.25, None, ALU.add)
                    ve.tensor_scalar(sC, sC, M23, -M23, ALU.add, ALU.add)
                    ve.tensor_scalar(sD, sD, M23, -M23, ALU.add, ALU.add)
                    ve.scalar_tensor_tensor(sC, sC, -TWO_PI, sB, ALU.mult, ALU.add)
                    ve.scalar_tensor_tensor(sD, sD, -TWO_PI, sB, ALU.mult, ALU.add)
                    ve.tensor_scalar(sC, sC, -TWO_PI / 2.0, TWO_PI / 2.0, ALU.max, ALU.min)
                    ve.tensor_scalar(sD, sD, -1.5 * TWO_PI / 2.0, 0.5 * TWO_PI / 2.0, ALU.max, ALU.min)
                    ve.tensor_tensor(sB, sA, lio, ALU.mult).then_inc(SEM["s_arg"], 1)
                    W(ve, "s_trig", 3 * ntrig)
                    av = sD.rearrange("p (c n) -> p c n", c=16, n=NM)
                    dv = sC.rearrange("p (c n) -> p c n", c=16, n=NM)
                    bv = sB.rearrange("p (c n) -> p c n", c=16, n=NM)
                    ve.tensor_tensor(m1[:, c0:c0 + 16, 0, :], bv, av, ALU.mult)
                    ve.tensor_tensor(m1[:, c0:c0 + 16, 1, :], bv, dv, ALU.mult).then_inc(SEM["s_m1"], 1)
                elif kind == "k":
                    re = m1[:, c0:c0 + 16, 0, :]
                    im = m1[:, c0:c0 + 16, 1, :]
                    w1v = tslot(0).rearrange("p (c n) -> p c n", c=16, n=NM)
                    w2v = tslot(1).rearrange("p (c n) -> p c n", c=16, n=NM)
                    r0v = rep[0][:, :].rearrange("p (c n) -> p c n", c=16, n=NM)
                    r1v = rep[1][:, :].rearrange("p (c n) -> p c n", c=16, n=NM)
                    ve.tensor_tensor(w1v, re, r0v, ALU.mult)
                    ve.tensor_tensor(w2v, im, r1v, ALU.mult)
                    ve.tensor_tensor(w1v, w1v, w2v, ALU.subtract)
                    ve.tensor_reduce(kred[:, lh, c0:c0 + 16], w1v, AX.X, ALU.add).then_inc(SEM["s_kred"], 1)
                    if lh == 1 and sub == 7:
                        ve.tensor_scalar_mul(kred[:, :, :], kred[:, :, :], 2.0)
                        dbc = dval[0:1, 0:1].broadcast_to([1, CPC])
                        ve.tensor_tensor(kred[0:1, 0, :], kred[0:1, 0, :], dbc, ALU.add).then_inc(SEM["s_kred"], 2)
                        W(ve, "s_ktr", 1)
                        ve.tensor_copy(ktr[:, 0, :], pb[0][:, 0:128])
                        ve.tensor_copy(ktr[:, 1, :], pb[1][:, 0:128]).then_inc(SEM["s_ktr"], 2)
                else:
                    W(ve, "s_zb", (lh + 1) * 16)
                    re = m1[:, c0:c0 + 16, 0, :]
                    im = m1[:, c0:c0 + 16, 1, :]
                    r0v = rep[0][:, :].rearrange("p (c n) -> p c n", c=16, n=NM)
                    r1v = rep[1][:, :].rearrange("p (c n) -> p c n", c=16, n=NM)
                    us = [tslot(i).rearrange("p (c n) -> p c n", c=16, n=NM) for i in range(4)]
                    ve.tensor_tensor(us[0], re, r0v, ALU.mult)
                    ve.tensor_tensor(us[1], im, r1v, ALU.mult)
                    ve.tensor_tensor(us[2], im, r0v, ALU.mult)
                    ve.tensor_tensor(us[3], re, r1v, ALU.mult)
                    ve.tensor_tensor(re, us[0], us[1], ALU.subtract)
                    ve.tensor_tensor(im, us[2], us[3], ALU.add).then_inc(SEM["s_cm"], 1)
                if kind == "m1" and sub == 7:
                    if lh == 0:
                        W(ve, "s_czt", 33)
                        W(ve, "s_czt", 65)
                    for jj in range(16):
                        W(ve, "s_zb", lh * 16 + jj + 1)
                        src = pb[jj % 4][:, 0:512].rearrange("p (c e n) -> p c e n", c=8, e=2, n=NM)
                        ve.tensor_copy(zb[:, jj * 8:(jj + 1) * 8, 1 - lh, :, :], src).then_inc(SEM["s_zbev"], 1)
                if kind == "cm" and sub == 7:
                    czx = czh if lh == 0 else czq
                    for c in range(CPC):
                        k = lh * CPC + c
                        if ev_owner(k) != "v":
                            continue
                        kb = k // 4
                        if kb >= 1:
                            W(ve, "s_czd", 16 * kb)
                        W(ve, "s_tp", k + 1)
                        ve.tensor_copy(stage[:, c % 4, 0:127], pb[k % 6][0:64, 1:128])
                        ve.tensor_copy(stage[:, c % 4, 127:128], czx[:, c:c + 1]).then_inc(SEM["s_ev_v"], 1)
                if rnd < 3:
                    W(ve, "s_mch", rnd + 1)
                    ve.tensor_copy(mstg[:, 2 * rnd:2 * rnd + 2], pb[rnd][:, 0:2]).then_inc(SEM["s_mchv"], 1)
                if rnd == 3:
                    W(ve, "s_chrm", 1)
                    ve.tensor_copy(gar_s, pb[7][:, 0:256]).then_inc(SEM["s_chre"], 1)
                if rnd == 5:
                    W(ve, "s_kz", 34)
                    ga_r = gar_s
                    tt1 = ysf[:, 256:512]
                    tt2 = ysf[:, 512:768]
                    ve.tensor_scalar(tt1, ga_r, 0.0, None, ALU.is_ge)
                    ve.tensor_scalar(tt1, tt1, 2.0, -1.0, ALU.mult, ALU.add)
                    ve.tensor_scalar_mul(tt2, ga_r, -1.0)
                    ve.tensor_tensor(tt2, tt2, ga_r, ALU.max)
                    ve.tensor_scalar(tt2, tt2, 0.25, None, ALU.max)
                    ve.tensor_tensor(ga_r, tt1, tt2, ALU.mult)
                if rnd < 24:
                    W(ve, "s_mtok", rnd + 1)
                    grp, dtt = rnd // 8, rnd % 8
                    ve.tensor_tensor(tokt, pb[7][:, 0:2], liota_s[:, 6:8], ALU.mult)
                    ve.tensor_reduce(mods_s[:, grp * 8 + dtt:grp * 8 + dtt + 1], tokt, AX.X, ALU.add
                                     ).then_inc(SEM["s_mtev"], 1)
                if rnd == 24:
                    W(ve, "s_modr", 16)
                    ve.tensor_scalar(rows[64:65, 256:512], rows[64:65, 256:512], 1.0, None, ALU.add
                                     ).then_inc(SEM["s_chre"], 1)
                if rnd == 25:
                    W(ve, "s_modr", 32)
                    ve.tensor_scalar(rows[64:65, 768:1792], rows[64:65, 768:1792], 1.0, None, ALU.add
                                     ).then_inc(SEM["s_chre"], 1)
                if rnd == 26:
                    gbc = mods_s[:, 16:24]
                    tt1, tt2 = gbt[:, 0:8], gbt[:, 8:16]
                    ve.tensor_scalar(tt1, gbc, 0.0, None, ALU.is_ge)
                    ve.tensor_scalar(tt1, tt1, 2.0, -1.0, ALU.mult, ALU.add)
                    ve.tensor_scalar_mul(tt2, gbc, -1.0)
                    ve.tensor_tensor(tt2, tt2, gbc, ALU.max)
                    ve.tensor_scalar(tt2, tt2, 0.25, None, ALU.max)
                    ve.tensor_tensor(gbc, tt1, tt2, ALU.mult)
                if rnd < 8:
                    q = rnd
                    W(ve, "s_h1", T_H1)
                    W(ve, "s_sq", 2 * q + 1)
                    uview = A_h1[:, 4096 * q:4096 * (q + 1)].rearrange("p (c t) -> p t c", c=16, t=256)
                    sview = sqst[:, :].rearrange("p (c t) -> p t c", c=16, t=256)
                    if q == 0:
                        ve.tensor_reduce(statv_m, uview, AX.X, ALU.add)
                        ve.tensor_reduce(statv_v, sview, AX.X, ALU.add).then_inc(SEM["s_sq"], 1)
                    else:
                        ve.tensor_reduce(ysf[:, 512:768], uview, AX.X, ALU.add)
                        ve.tensor_reduce(ysf[:, 768:1024], sview, AX.X, ALU.add).then_inc(SEM["s_sq"], 1)
                        ve.tensor_tensor(statv_m, statv_m, ysf[:, 512:768], ALU.add)
                        ve.tensor_tensor(statv_v, statv_v, ysf[:, 768:1024], ALU.add)
                    if q == 7:
                        ve.drain()
                        ve.sem_inc(SEM["s_stat"], 1)
                if rnd == 14:
                    W(ve, "s_lnb", 32)
                    tt1 = ysf[:, 256:512]
                    ve.tensor_scalar_mul(statv_m, statv_m, 1.0 / 1024.0)
                    ve.tensor_scalar_mul(statv_v, statv_v, 1.0 / 1024.0)
                    ve.tensor_tensor(tt1, statv_m, statv_m, ALU.mult)
                    ve.tensor_tensor(statv_v, statv_v, tt1, ALU.subtract).then_inc(SEM["s_stat"], 1)
                if rnd == 17:
                    W(ve, "s_stat", 4)
                    tt1 = ysf[:, 256:512]
                    ve.reciprocal(statv_v, statv_v)
                    ve.drain()
                    ve.tensor_tensor(tt1, statv_v, ysf[:, 512:768], ALU.mult)
                    ve.tensor_scalar(tt1, tt1, -1.0, 2.0, ALU.mult, ALU.add)
                    ve.tensor_tensor(statv_v, statv_v, tt1, ALU.mult)
                if rnd == 19:
                    ve.tensor_tensor(h1f2, h1f2, m_bc, ALU.subtract).then_inc(SEM["s_h1m"], 1)
                if rnd == 22:
                    ve.tensor_tensor(h1f2, h1f2, r_bc, ALU.mult).then_inc(SEM["s_h1m"], 1)
                if rnd == 27:
                    W(ve, "s_chrm", 2)
                    for b in range(2):
                        ca_b = pb[7][:, 256:512].rearrange("p (c b o q) -> p c b o q",
                                                           c=128, b=2, o=1, q=1
                                                           )[:, :, b, :, :].broadcast_to([P, 128, 2, 64])
                        ve.tensor_tensor(h1[:, :, :, b, :], h1[:, :, :, b, :], ca_b, ALU.mult
                                         ).then_inc(SEM["s_h1m"], 1 if b == 1 else 0)
                if rnd == 31:
                    for b in range(2):
                        sa_b = pb[7][:, 0:256].rearrange("p (c b o q) -> p c b o q",
                                                         c=128, b=2, o=1, q=1
                                                         )[:, :, b, :, :].broadcast_to([P, 128, 2, 64])
                        ve.tensor_tensor(h1[:, :, :, b, :], h1[:, :, :, b, :], sa_b, ALU.add
                                         ).then_inc(SEM["s_h1m"], 1 if b == 1 else 0)
            W(ve, "s_zq", 2)
            sq_ai = ssmq_s[:, 64:128]
            ve.tensor_tensor(zq_im, sq_ai, zq_t1, ALU.mult)
            ve.tensor_scalar_mul(zq_im, zq_im, 256.0)
            ve.tensor_tensor(zq_t2, zq_t2, zq_t1, ALU.mult)
            ve.tensor_scalar_mul(zq_t2, zq_t2, -256.0)
            ve.tensor_scalar_mul(zq_t1, zq_im, INV_2PI)
            ve.tensor_scalar(zq_re, zq_t1, 0.25, None, ALU.add)
            ve.tensor_scalar(zq_t1, zq_t1, M23, -M23, ALU.add, ALU.add)
            ve.tensor_scalar(zq_re, zq_re, M23, -M23, ALU.add, ALU.add)
            ve.scalar_tensor_tensor(zq_re, zq_re, -TWO_PI, zq_im, ALU.mult, ALU.add)
            ve.scalar_tensor_tensor(zq_im, zq_t1, -TWO_PI, zq_im, ALU.mult, ALU.add)
            ve.tensor_scalar(zq_re, zq_re, -1.5 * TWO_PI / 2.0, 0.5 * TWO_PI / 2.0, ALU.max, ALU.min)
            ve.tensor_scalar(zq_im, zq_im, -TWO_PI / 2.0, TWO_PI / 2.0, ALU.max, ALU.min).then_inc(SEM["s_zq"], 1)
            W(ve, "s_zq", 5)
            ve.tensor_tensor(zq_re, zq_re, zq_t1, ALU.mult)
            ve.tensor_tensor(zq_im, zq_im, zq_t1, ALU.mult)
            ve.tensor_copy(zrr[0:64, :], zq_re)
            ve.tensor_copy(zrr[64:128, :], zq_re)
            ve.tensor_copy(zi_t, zq_im)
            for c in range(CPC):
                if bev_owner(c) != "v":
                    continue
                W(ve, "s_bmm", c + 1)
                par_ = c % 2
                bk = pb[c % 8]
                src_re = bk[0:32, 0:128].rearrange("p (b i) -> p b i", b=2, i=64)
                src_im = bk[32:64, 0:128].rearrange("p (b i) -> p b i", b=2, i=64)
                ve.tensor_copy(bar[par_ * 32:par_ * 32 + 32, c // 2, :, :], src_re)
                ve.tensor_copy(bar[64 + par_ * 32:64 + par_ * 32 + 32, c // 2, :, :], src_im
                               ).then_inc(SEM["s_bev_v"], 1)
            W(ve, "s_bev_v", cnt_upto(bev_owner, CPC - 1, "v"))
            W(ve, "s_bev_s", cnt_upto(bev_owner, CPC - 1, "s"))
            ve.memset(arch[:, :, :, 0:1], 0.0)
            ve.tensor_copy(sm[64:128, 64:128], zi_t)
            zrrb = zrr.rearrange("p (c o) -> p c o", o=1).broadcast_to([P, 64, 2])
            zib = zi_t.rearrange("p (c o) -> p c o", o=1).broadcast_to([64, 64, 2])
            zib64 = sm[64:128, 64:128].rearrange("p (c o) -> p c o", o=1).broadcast_to([64, 64, 2])
            for j in range(63):
                ve.tensor_tensor(p1v, arch[:, :, :, j], zrrb, ALU.mult)
                ve.tensor_tensor(p1v, p1v, bar[:, :, :, j], ALU.add)
                ve.tensor_tensor(t2v, arch[64:128, :, :, j], zib64, ALU.mult)
                ve.tensor_tensor(t3v, arch[0:64, :, :, j], zib, ALU.mult)
                ve.scalar_tensor_tensor(arch[0:64, :, :, j + 1], t2v, -1.0, p1v[0:64], ALU.mult, ALU.add)
                ve.tensor_tensor(arch[64:128, :, :, j + 1], p1v[64:128], t3v, ALU.add)
            ve.tensor_scalar_mul(archb[0:64, :, :, :], arch[0:64, :, :, :], 2.0)
            ve.tensor_scalar_mul(archb[64:128, :, :, :], arch[64:128, :, :, :], -2.0).then_inc(SEM["s_scan"], 2)
            for c in range(CPC):
                W(ve, "s_yev", c + 1)
                blkv = ysf[:, (c % 4) * 256:(c % 4) * 256 + 256].rearrange(
                    "p (h b i) -> p h b i", h=2, b=2, i=64)
                gav = gar_s[:, 2 * c:2 * c + 2].rearrange(
                    "p (o b q) -> p o b q", o=1, b=2, q=1).broadcast_to([P, 2, 2, 64])
                ve.tensor_tensor(blkv, blkv, gav, ALU.mult).then_inc(SEM["s_yga"], 1)
            W(ve, "s_utok", NIL * 16)
            x2a_v = x2a.rearrange("p (i16 h d) -> p d h i16", i16=8, h=2, d=1024)
            x2b_v = x2b.rearrange("p (i16 h d) -> p d h i16", i16=4, h=2, d=1024)
            x2c_v = x2c.rearrange("p (i16 h d) -> p d h i16", i16=4, h=2, d=1024)
            for n in range(32):
                g_i, src_ = n // 8, n % 8
                off = src_ * 128 + g_i * 32
                W(ve, "s_recv", 32 * (n // 2 + 1))
                tv = tpf[:, (n % 2) * 1024:(n % 2) * 1024 + 1024].rearrange(
                    "p (cl h i) -> p cl h i", cl=32, h=2, i=16)
                ve.tensor_tensor(x2a_v[:, off:off + 32, :, :], x2a_v[:, off:off + 32, :, :],
                                 tv[:, :, :, 0:8], ALU.add).then_inc(SEM["s_acc"], 1)
                ve.tensor_tensor(x2b_v[:, off:off + 32, :, :], x2b_v[:, off:off + 32, :, :],
                                 tv[:, :, :, 8:12], ALU.add).then_inc(SEM["s_acc"], 1)
                ve.tensor_tensor(x2c_v[:, off:off + 32, :, :], x2c_v[:, off:off + 32, :, :],
                                 tv[:, :, :, 12:16], ALU.add).then_inc(SEM["s_acc"], 1)
            ve.tensor_copy(gbs, mods_s[:, 16:24])
            for il in range(NIL):
                W(ve, "s_x2td", il * 16)
                for dt in range(4):
                    n = il * 8 + dt
                    W(ve, "s_x2t", n + 1)
                    ve.tensor_copy(x2tst[:, dt * 128:(dt + 1) * 128], pb[n % 4][:, 0:128]
                                   ).then_inc(SEM["s_x2ve"], 1)
                k = il
                W(ve, "s_w1a", k + 1)
                ve.tensor_reduce(maxw1[:, k:k + 1], wqin, AX.X, ALU.max).then_inc(SEM["s_wmv1"], 1)
            ve.tensor_scalar(maxw1, maxw1, 1e-8, None, ALU.max)
            ve.reciprocal(rw1c, maxw1)
            ve.drain()
            ve.tensor_tensor(csm1, rw1c, maxw1, ALU.mult)
            ve.tensor_scalar(csm1, csm1, -1.0, 2.0, ALU.mult, ALU.add)
            ve.tensor_tensor(rw1c, rw1c, csm1, ALU.mult)
            ve.tensor_scalar_mul(rw1c, rw1c, 127.0).then_inc(SEM["s_w1mx"], 1)
            W(ve, "s_ln2s", 64)
            ve.tensor_scalar_mul(mxc, mxc, 1.0 / 1024.0)
            ve.tensor_scalar_mul(rsc, rsc, 1.0 / 1024.0)
            ve.tensor_tensor(csm1, mxc, mxc, ALU.mult)
            ve.tensor_tensor(rsc, rsc, csm1, ALU.subtract).then_inc(SEM["s_ln2s"], 1)
            W(ve, "s_ln2s", 67)
            ve.tensor_copy(csm2, rsc)
            ve.reciprocal(rsc, rsc)
            ve.drain()
            ve.tensor_tensor(csm1, rsc, csm2, ALU.mult)
            ve.tensor_scalar(csm1, csm1, -1.0, 2.0, ALU.mult, ALU.add)
            ve.tensor_tensor(rsc, rsc, csm1, ALU.mult)
            ve.tensor_tensor(mrs, mxc, rsc, ALU.mult)
            ve.tensor_scalar_mul(mrs, mrs, -1.0).then_inc(SEM["s_ln2f"], 1)
            W(ve, "s_h2r", 4)
            for il in range(NIL):
                W(ve, "s_h2s", il + 1)
                xc = x2chunk(il)
                ve.tensor_tensor(xc[:, 0:512], xc[:, 0:512], pb[4][:, 0:512], ALU.mult)
                ve.tensor_tensor(xc[:, 512:1024], xc[:, 512:1024], pb[5][:, 0:512], ALU.mult)
                ve.tensor_tensor(xc[:, 0:512], xc[:, 0:512], pb[6][:, 0:512], ALU.add)
                ve.tensor_tensor(xc[:, 512:1024], xc[:, 512:1024], pb[7][:, 0:512], ALU.add)
                ve.tensor_reduce(tcol, xc, AX.X, ALU.max)
                if il == 0:
                    ve.tensor_copy(mx1col, tcol).then_inc(SEM["s_h2v"], 1)
                else:
                    ve.tensor_tensor(mx1col, mx1col, tcol, ALU.max).then_inc(SEM["s_h2v"], 1)
            W(ve, "s_mx1tp", 1)
            ve.tensor_reduce(gro[:, 0:1], pb[4][0:1, 0:128], AX.X, ALU.max).then_inc(SEM["s_mx1v"], 1)
            W(ve, "s_mx1r", 16)
            ve.tensor_scalar(gro[:, 1:2], gro[:, 1:2], 1e-8, None, ALU.max)
            ve.tensor_scalar(gro[:, 2:3], gro[:, 1:2], 1.0 / 255.0, None, ALU.mult)
            ve.reciprocal(gro[:, 3:4], gro[:, 1:2])
            ve.drain()
            ve.tensor_tensor(trow[:, 0:1], gro[:, 3:4], gro[:, 1:2], ALU.mult)
            ve.tensor_scalar(trow[:, 0:1], trow[:, 0:1], -1.0, 2.0, ALU.mult, ALU.add)
            ve.tensor_tensor(gro[:, 3:4], gro[:, 3:4], trow[:, 0:1], ALU.mult)
            ve.tensor_scalar_mul(gro[:, 3:4], gro[:, 3:4], 255.0).then_inc(SEM["s_s1row"], 1)
            W(ve, "s_s1rm", 1)
            ve.tensor_copy(s1c, pb[5][:, 0:1])
            ve.tensor_copy(is1c, pb[5][:, 1:2])
            W(ve, "s_bc", 32)
            ve.tensor_scalar(csm1, maxw1, 1.0 / 127.0, None, ALU.mult)
            ve.tensor_tensor(sc1c, csm1, s1c.broadcast_to([P, 32]), ALU.mult)
            ve.tensor_tensor(csm1, b1cc, rw1c, ALU.mult)
            ve.tensor_tensor(csm1, csm1, is1c.broadcast_to([P, 32]), ALU.mult)
            ve.tensor_scalar(csm1, csm1, 1.0 / 255.0, None, ALU.mult)
            ve.tensor_scalar(csm1, csm1, M23, -M23, ALU.add, ALU.add)
            ve.tensor_tensor(b1qc, csm1, sc1c, ALU.mult).then_inc(SEM["s_colq"], 1)
            vev = [x for x in range(NTRS) if QWT_OWNER[x] == "v"]
            vstate = {"p": 0}

            def v_drain_below(lim):
                while vstate["p"] < len(vev) and vev[vstate["p"]] < lim:
                    x = vev[vstate["p"]]
                    W(ve, "s_qt", x + 1)
                    kindd, a, jj = QWT_DST[x]
                    if kindd == "q":
                        dst = x1qt[jj][:, a * 128:(a + 1) * 128]
                    else:
                        dst = w1qt[jj][:, a * 128:(a + 1) * 128]
                    ve.tensor_copy(dst, pb[x % 4][:, 0:128]).then_inc(SEM["s_qev"], 1)
                    vstate["p"] += 1

            for pos in range(NIL):
                if pos >= 1:
                    v_drain_below(8 * (qslot_idx(pos - 1) + 1))
                if pos >= 2:
                    W(ve, "s_qt", 8 * (qslot_idx(pos - 2) + 1))
                W(ve, "s_qs", 2 * pos + 2)
                ve.tensor_scalar(qstage[:, (pos % 2) * 1024:(pos % 2) * 1024 + 512],
                                 qslotA, M23, -M23, ALU.add, ALU.add)
                ve.tensor_scalar(qstage[:, (pos % 2) * 1024 + 512:(pos % 2) * 1024 + 1024],
                                 qslotB, M23, -M23, ALU.add, ALU.add).then_inc(SEM["s_q"], 1)
            def mm1_max(blkN):
                W(ve, "s_ev1", blkN + 1)
                ve.tensor_reduce(tcol, h1st[blkN % 4], AX.X, ALU.max)
                if blkN == 0:
                    ve.tensor_copy(mx2col, tcol).then_inc(SEM["s_m2acc"], 1)
                else:
                    ve.tensor_tensor(mx2col, mx2col, tcol, ALU.max).then_inc(SEM["s_m2acc"], 1)
                if blkN % 8 == 0:
                    kk = blkN // 8
                    dt, fq = kk // 4, kk % 4
                    W(ve, "s_w1a", 32 + kk + 1)
                    if fq == 0:
                        ve.tensor_reduce(maxw2[:, dt:dt + 1], ysf[:, 0:1024], AX.X, ALU.max
                                         ).then_inc(SEM["s_wmv1"], 1)
                    else:
                        ve.tensor_reduce(tco2, ysf[:, 0:1024], AX.X, ALU.max).then_inc(SEM["s_wmv1"], 1)
                        ve.tensor_tensor(maxw2[:, dt:dt + 1], maxw2[:, dt:dt + 1], tco2, ALU.max)

            for k in range(32):
                v_drain_below(8 * (32 + k))
                if k >= 2:
                    W(ve, "s_qt", 8 * (wslot_idx(k - 2) + 1))
                W(ve, "s_w1s", k + 1)
                ve.tensor_scalar(bstg1[k % 2], wqin, M23, -M23, ALU.add, ALU.add
                                 ).then_inc(SEM["s_w1v"], 1)
                if k >= 1:
                    for ch in range(NCH):
                        mm1_max(8 * (k - 1) + ch)
            v_drain_below(NTRS)
            for ch in range(NCH):
                mm1_max(8 * 31 + ch)
            ve.tensor_scalar(maxw2, maxw2, 1e-8, None, ALU.max)
            ve.reciprocal(rw2c, maxw2)
            ve.drain()
            ve.tensor_tensor(gbt[:, 0:8], rw2c, maxw2, ALU.mult)
            ve.tensor_scalar(gbt[:, 0:8], gbt[:, 0:8], -1.0, 2.0, ALU.mult, ALU.add)
            ve.tensor_tensor(rw2c, rw2c, gbt[:, 0:8], ALU.mult)
            ve.tensor_scalar_mul(rw2c, rw2c, 127.0).then_inc(SEM["s_w1mx"], 1)
            W(ve, "s_mx1tp", 2)
            ve.tensor_reduce(gro[:, 4:5], pb[4][0:1, 0:128], AX.X, ALU.max).then_inc(SEM["s_mx1v"], 1)
            W(ve, "s_mx1r", 32)
            ve.tensor_scalar(gro[:, 5:6], gro[:, 5:6], 1e-8, None, ALU.max)
            ve.tensor_scalar(gro[:, 6:7], gro[:, 5:6], 1.0 / 255.0, None, ALU.mult)
            ve.reciprocal(gro[:, 7:8], gro[:, 5:6])
            ve.drain()
            ve.tensor_tensor(trow[:, 0:1], gro[:, 7:8], gro[:, 5:6], ALU.mult)
            ve.tensor_scalar(trow[:, 0:1], trow[:, 0:1], -1.0, 2.0, ALU.mult, ALU.add)
            ve.tensor_tensor(gro[:, 7:8], gro[:, 7:8], trow[:, 0:1], ALU.mult)
            ve.tensor_scalar_mul(gro[:, 7:8], gro[:, 7:8], 255.0).then_inc(SEM["s_s1row"], 1)
            W(ve, "s_s1rm", 2)
            ve.tensor_copy(s2c, pb[5][:, 0:1])
            ve.tensor_copy(is2c, pb[5][:, 1:2])
            W(ve, "s_bc", 32)
            ve.tensor_scalar(gbt[:, 0:8], maxw2, 1.0 / 127.0, None, ALU.mult)
            ve.tensor_tensor(gbt[:, 0:8], gbt[:, 0:8], s2c.broadcast_to([P, 8]), ALU.mult)
            ve.tensor_tensor(sc2g, gbt[:, 0:8], gbs, ALU.mult)
            ve.tensor_tensor(gbt[:, 8:16], b2cc, rw2c, ALU.mult)
            ve.tensor_tensor(gbt[:, 8:16], gbt[:, 8:16], is2c.broadcast_to([P, 8]), ALU.mult)
            ve.tensor_scalar(gbt[:, 8:16], gbt[:, 8:16], 1.0 / 255.0, None, ALU.mult)
            ve.tensor_scalar(gbt[:, 8:16], gbt[:, 8:16], M23, -M23, ALU.add, ALU.add)
            ve.tensor_tensor(b2qg, gbt[:, 8:16], sc2g, ALU.mult).then_inc(SEM["s_colq"], 1)
            for k in range(32):
                W(ve, "s_w1s", 32 + k + 1)
                if k >= 2:
                    W(ve, "s_w2tp", 8 * (k - 1))
                ve.tensor_scalar(bstg2[k % 2], wqin, M23, -M23, ALU.add, ALU.add
                                 ).then_inc(SEM["s_w1v"], 1)
                if k >= 1:
                    kk = k - 1
                    dt, fq = kk // 4, kk % 4
                    for j in range(4):
                        x = kk * 8 + j
                        W(ve, "s_w2tp", x + 1)
                        ve.tensor_copy(w2qt[fq * 8 + j][:, dt * 128:(dt + 1) * 128],
                                       pb[x % 4][:, 0:128]).then_inc(SEM["s_w2ev"], 1)
            kk = 31
            dt, fq = kk // 4, kk % 4
            for j in range(4):
                x = kk * 8 + j
                W(ve, "s_w2tp", x + 1)
                ve.tensor_copy(w2qt[fq * 8 + j][:, dt * 128:(dt + 1) * 128],
                               pb[x % 4][:, 0:128]).then_inc(SEM["s_w2ev"], 1)
            vadd = {"p": 0}

            def v_adds_below(lim):
                while vadd["p"] < min(lim, 64):
                    j = vadd["p"]
                    W(ve, "s_e2s", j + 1)
                    W(ve, "s_xin", 32 * (j // 2 + 1))
                    ve.tensor_tensor(outst[j % 2], outst[j % 2], x2tin[j % 2], ALU.add
                                     ).then_inc(SEM["s_e2v"], 1)
                    vadd["p"] += 1

            for ch in range(NCH):
                if ch >= 3:
                    v_adds_below(16 * ((ch - 3) // 2 + 1))
                for ft in range(NFT):
                    k2 = ch * NFT + ft
                    W(ve, "s_r2s", k2 + 1)
                    if ch >= 3:
                        W(ve, "s_mm2", (((ch - 3) // 2) + 1) * 16)
                    slot = x2qs[ch % 3]
                    ve.tensor_scalar(slot[:, ft * 512:(ft + 1) * 512], rstg[k2 % 2], M23, -M23,
                                     ALU.add, ALU.add).then_inc(SEM["s_rmg"], 1)
            v_adds_below(64)

        @block.scalar
        def _(sc):
            sc = DrainEng(sc, SEM["s_sch"], multi=[
                id(SEM[n]) for n in ("s_czq", "s_zq", "s_czt", "s_sq",
                                     "s_stat", "s_ln2s")])
            W(sc, "s_in", T_IN)
            sc.activation(ctsil, ct_s, AF.Silu).then_inc(SEM["s_silu"], 1)
            W(sc, "s_czq", 1)
            sc.activation(u0, u0, AF.Exp)
            sc.activation(u1, ssmq2_s[:, 256:384], AF.Exp).then_inc(SEM["s_czq"], 2)
            for xi in range(3):
                W(sc, "s_czq", {0: 4, 1: 10, 2: 13}[xi])
                sc.activation(u3, u3, AF.Sin)
                sc.activation(u1, u1, AF.Sin, bias=liota_s[0:32, 3:4])
                sc.activation(u4, u4, AF.Exp, scale=-1.0).then_inc(SEM["s_czq"], 2)
                if xi == 0:
                    W(sc, "s_czq", 7)
                    sc.activation(v_e, ssmq2_s[:, 0:128], AF.Exp).then_inc(SEM["s_czq"], 1)
            sc.activation(zq_t1, ssmq_s[:, 128:192], AF.Exp)
            sc.activation(zq_t2, ssmq_s[:, 0:64], AF.Exp).then_inc(SEM["s_zq"], 2)
            ntrig = 0
            for rnd in range(48):
                lh, kind, sub = round_kind(rnd)
                if kind == "m1":
                    ntrig += 1
                    sA, sB, sC, sD = tslot(0), tslot(1), tslot(2), tslot(3)
                    W(sc, "s_arg", 2 * ntrig - 1)
                    sc.activation(sA, sA, AF.Exp)
                    sc.activation(sB, rep[2][:, :], AF.Exp).then_inc(SEM["s_trig"], 1)
                    W(sc, "s_arg", 2 * ntrig)
                    sc.activation(sC, sC, AF.Sin)
                    sc.activation(sD, sD, AF.Sin, bias=liota_s[:, 3:4])
                    sc.activation(sB, sB, AF.Exp, scale=-1.0).then_inc(SEM["s_trig"], 2)
                if kind == "cm" and sub == 7:
                    czx = czh if lh == 0 else czq
                    for c in range(CPC):
                        k = lh * CPC + c
                        if ev_owner(k) != "s":
                            continue
                        kb = k // 4
                        if kb >= 1:
                            W(sc, "s_czd", 16 * kb)
                        W(sc, "s_tp", k + 1)
                        sc.activation(stage[:, c % 4, 0:127], pb[k % 6][0:64, 1:128], AF.Copy)
                        sc.activation(stage[:, c % 4, 127:128], czx[:, c:c + 1], AF.Copy).then_inc(SEM["s_ev_s"], 1)
                if rnd < 8:
                    q = rnd
                    W(sc, "s_h1", T_H1)
                    if q >= 1:
                        W(sc, "s_sq", 2 * q)
                    sc.activation(sqst, A_h1[:, 4096 * q:4096 * (q + 1)], AF.Square).then_inc(SEM["s_sq"], 1)
                if rnd == 15:
                    W(sc, "s_stat", 2)
                    sc.activation(statv_v, statv_v, AF.Sqrt, bias=liota_s[:, 4:5])
                    sc.activation(ysf[:, 512:768], statv_v, AF.Copy).then_inc(SEM["s_stat"], 2)
            W(sc, "s_zq", 3)
            sc.activation(zq_im, zq_im, AF.Sin)
            sc.activation(zq_re, zq_re, AF.Sin, bias=liota_s[0:64, 3:4])
            sc.activation(zq_t1, zq_t2, AF.Exp).then_inc(SEM["s_zq"], 2)
            for c in range(CPC):
                if bev_owner(c) != "s":
                    continue
                W(sc, "s_bmm", c + 1)
                par_ = c % 2
                bk = pb[c % 8]
                src_re = bk[0:32, 0:128].rearrange("p (b i) -> p b i", b=2, i=64)
                src_im = bk[32:64, 0:128].rearrange("p (b i) -> p b i", b=2, i=64)
                sc.activation(bar[par_ * 32:par_ * 32 + 32, c // 2, :, :], src_re, AF.Copy)
                sc.activation(bar[64 + par_ * 32:64 + par_ * 32 + 32, c // 2, :, :], src_im, AF.Copy
                              ).then_inc(SEM["s_bev_s"], 1)
            ybv2 = ybo_d.ap().rearrange("p (c h b i) -> p c h b i", c=CPC, h=NH, b=NB, i=NI)
            for c in range(CPC):
                W(sc, "s_ymm", c + 1)
                if c >= 4:
                    W(sc, "s_yout", 16 * (c // 4))
                sc.activation(ysf[:, (c % 4) * 256:(c % 4) * 256 + 128],
                              pb[(c % 4) * 2][:, 0:128], AF.Gelu)
                sc.activation(ysf[:, (c % 4) * 256 + 128:(c % 4) * 256 + 256],
                              pb[(c % 4) * 2 + 1][:, 0:128], AF.Gelu).then_inc(SEM["s_yev"], 1)
                if c % 32 == 31:
                    gg = c // 32
                    sc.drain()
                    W(sc, "s_yout", (gg + 1) * 8 * 16)
                    for s in range(3, 6):
                        b_s, j_q = s // 4, s % 4
                        srcv2 = ybv2[:, gg * 32:(gg + 1) * 32, :, b_s, j_q * 16:(j_q + 1) * 16]
                        dstv2 = rap(send_d[gg], s * TOK * 32,
                                    [(1024, P), (32, 32), (16, 2), (1, 16)])
                        sc.dma_start(out=dstv2, in_=srcv2).then_inc(
                            SEM[["s_send", "s_send1", "s_send2", "s_send3"][gg]], 16)
            W(sc, "s_acc", 96)
            for il in range(NIL):
                W(sc, "s_x2td", il * 16)
                for dt in range(4, 8):
                    n = il * 8 + dt
                    W(sc, "s_x2t", n + 1)
                    sc.activation(x2tst[:, dt * 128:(dt + 1) * 128], pb[n % 4][:, 0:128], AF.Copy
                                  ).then_inc(SEM["s_x2vs"], 1)
                xc = x2chunk(il)
                sc.activation(tpf[:, 0:1024], xc, AF.Copy, accum_out=mxc[:, il:il + 1])
                sc.activation(tpf[:, 0:1024], xc, AF.Square, accum_out=rsc[:, il:il + 1]
                              ).then_inc(SEM["s_ln2s"], 2)
                k = il
                W(sc, "s_wmi1", (k + 1) * 16)
                sc.activation(wqin, wqin, AF.Abs).then_inc(SEM["s_w1a"], 1)
            W(sc, "s_ln2s", 65)
            sc.activation(rsc, rsc, AF.Sqrt, bias=liota_s[:, 4:5]).then_inc(SEM["s_ln2s"], 2)
            W(sc, "s_ln2f", 1)
            W(sc, "s_x2td", 512)
            for il in range(NIL):
                sc.activation(x2chunk(il), x2chunk(il), AF.Identity,
                              scale=rsc[:, il:il + 1], bias=mrs[:, il:il + 1]).then_inc(SEM["s_h2s"], 1)
            sev = [x for x in range(NTRS) if QWT_OWNER[x] == "s"]
            sstate = {"p": 0}

            def s_drain_below(lim):
                while sstate["p"] < len(sev) and sev[sstate["p"]] < lim:
                    x = sev[sstate["p"]]
                    W(sc, "s_qt", x + 1)
                    kindd, a, jj = QWT_DST[x]
                    if kindd == "q":
                        dst = x1qt[jj][:, a * 128:(a + 1) * 128]
                    else:
                        dst = w1qt[jj][:, a * 128:(a + 1) * 128]
                    sc.activation(dst, pb[x % 4][:, 0:128], AF.Copy).then_inc(SEM["s_qes"], 1)
                    sstate["p"] += 1

            W(sc, "s_colq", 1)
            for pos in range(NIL):
                il = QORD[pos]
                if pos >= 1:
                    s_drain_below(8 * (qslot_idx(pos - 1) + 1))
                    W(sc, "s_q", pos)
                xc = x2chunk(il)
                sc.activation(qslotA, xc[:, 0:512], AF.Relu, scale=is1c)
                sc.activation(qslotB, xc[:, 512:1024], AF.Relu, scale=is1c).then_inc(SEM["s_qs"], 2)
            def mm1_evac(blkN):
                ft, ch = blkN // 8, blkN % 8
                W(sc, "s_mm1", blkN + 1)
                if blkN >= 4:
                    W(sc, "s_sp", 64 * (blkN // 4))
                    W(sc, "s_m2acc", blkN - 3)
                sc.activation(h1st[blkN % 4], pb[4 + ch % 4][:, 0:512], AF.Relu,
                              scale=sc1c[:, ft:ft + 1], bias=b1qc[:, ft:ft + 1]
                              ).then_inc(SEM["s_ev1"], 1)
                if blkN % 8 == 0:
                    kk = blkN // 8
                    W(sc, "s_gsend", 128 + (kk + 1) * 16)
                    sc.activation(ysf[:, 0:1024], ysf[:, 0:1024], AF.Abs).then_inc(SEM["s_w1a"], 1)

            W(sc, "s_w1mx", 1)
            W(sc, "s_x2td", 512)
            for k in range(32):
                s_drain_below(8 * (32 + k))
                W(sc, "s_w1i", (k + 1) * 16)
                sc.activation(wqin, wqin, AF.Identity, scale=rw1c[:, k:k + 1]
                              ).then_inc(SEM["s_w1s"], 1)
                if k >= 1:
                    for ch in range(NCH):
                        mm1_evac(8 * (k - 1) + ch)
            s_drain_below(NTRS)
            for ch in range(NCH):
                mm1_evac(8 * 31 + ch)
            for k in range(32):
                dtw = k // 4
                W(sc, "s_w1i", 512 + (k + 1) * 16)
                W(sc, "s_w1mx", 2)
                sc.activation(wqin, wqin, AF.Identity, scale=rw2c[:, dtw:dtw + 1]
                              ).then_inc(SEM["s_w1s"], 1)
                if k >= 1:
                    kk = k - 1
                    dt, fq = kk // 4, kk % 4
                    for j in range(4, 8):
                        x = kk * 8 + j
                        W(sc, "s_w2tp", x + 1)
                        sc.activation(w2qt[fq * 8 + j][:, dt * 128:(dt + 1) * 128],
                                      pb[x % 4][:, 0:128], AF.Copy).then_inc(SEM["s_w2es"], 1)
            kk = 31
            dt, fq = kk // 4, kk % 4
            for j in range(4, 8):
                x = kk * 8 + j
                W(sc, "s_w2tp", x + 1)
                sc.activation(w2qt[fq * 8 + j][:, dt * 128:(dt + 1) * 128],
                              pb[x % 4][:, 0:128], AF.Copy).then_inc(SEM["s_w2es"], 1)
            W(sc, "s_colq", 2)
            sevp = {"p": 0}

            def s_evacs_below(lim):
                while sevp["p"] < min(lim, 64):
                    j = sevp["p"]
                    chg, dtq, par_ = j // 16, (j % 16) // 2, j % 2
                    W(sc, "s_mm2", j + 1)
                    if j >= 2:
                        W(sc, "s_out", 32 * (j // 2))
                    bank = (dtq % 4) * 2 + par_
                    sc.activation(outst[j % 2], pb[bank][:, 0:512], AF.Identity,
                                  scale=sc2g[:, dtq:dtq + 1], bias=b2qg[:, dtq:dtq + 1]
                                  ).then_inc(SEM["s_e2s"], 1)
                    sevp["p"] += 1

            for ch in range(NCH):
                if ch >= 3:
                    s_evacs_below(16 * ((ch - 3) // 2 + 1))
                for ft in range(NFT):
                    k2 = ch * NFT + ft
                    W(sc, "s_hin", 64 * (k2 // 4 + 1))
                    if k2 >= 2:
                        W(sc, "s_rmg", k2 - 1)
                    sc.activation(rstg[k2 % 2], h1in[k2 % 4], AF.Identity, scale=is2c
                                  ).then_inc(SEM["s_r2s"], 1)
            s_evacs_below(64)
    return nc


_NC_CACHE = {}


def _get_nc():
    if "nc" not in _NC_CACHE:
        _NC_CACHE["nc"] = build_program()
    return _NC_CACHE["nc"]


def _mk_sel5():
    s = np.zeros((P, 3 * P), np.float32)
    for r in (0, 32, 64):
        s[r, 0:P] = 1.0
    for r in (1, 33):
        s[r, P:2 * P] = 1.0
    s[2, 2 * P:3 * P] = 1.0
    return s


def _host_core_inputs(core, u, log_dt, A_real, A_imag, C_real, C_imag, Dp,
                      w1, b1, w2, b2, cvec, wmod):
    ch0 = 128 * core
    b_me = core // 4
    j_me = core % 4
    v = u[:, :, ch0:ch0 + 128]
    v = np.ascontiguousarray(v.transpose(2, 0, 1)).reshape(CPC, NB, NI, NH, P)
    u_conv = np.ascontiguousarray(v.transpose(4, 0, 3, 1, 2).reshape(P, -1)).astype(np.float32)
    x = u[b_me, 4096 * j_me:4096 * (j_me + 1), :]
    u_tok = np.ascontiguousarray(
        x.reshape(NIL, P, D).transpose(1, 0, 2).reshape(P, NIL * D)).astype(np.float32)

    rowr = lambda a: np.ascontiguousarray(np.asarray(a).reshape(1, -1)).astype(np.float32)

    ld = log_dt[ch0:ch0 + 128]
    Ar = A_real[ch0:ch0 + 128]
    Ai = A_imag[ch0:ch0 + 128]
    Cr = C_real[ch0:ch0 + 128]
    Ci = C_imag[ch0:ch0 + 128]

    def tile64(par_n_c):
        t = np.zeros((64, 64), np.float32)
        for par_ in range(2):
            for n in range(NM):
                t[par_ * 32 + n, :] = par_n_c[2 * np.arange(64) + par_, n]
        return t

    def tile32(a):
        return np.ascontiguousarray(np.asarray(a).T).astype(np.float32)

    ldt_full = np.broadcast_to(ld[:, None], (CPC, NM))
    ssmq_arr = np.concatenate([tile64(Ar), tile64(Ai), tile64(ldt_full)], axis=1)
    ssmq2_arr = np.concatenate([tile32(Ar), tile32(Ai), tile32(ldt_full),
                                tile32(Cr), tile32(Ci)], axis=1)
    lio = np.zeros((P, 8), np.float32)
    lio[:, 0] = np.arange(P)
    lio[:, 1] = np.arange(P) + 128
    lio[:, 2] = -float(np.pi)
    lio[:, 3] = HALF_PI
    lio[:, 4] = 1e-5
    lio[:, 6] = 1.0 if b_me == 0 else 0.0
    lio[:, 7] = 1.0 if b_me == 1 else 0.0

    b1c = np.ascontiguousarray(b1.reshape(NFT, P).T).astype(np.float32)
    b2c = np.ascontiguousarray(b2.reshape(NDT, P).T).astype(np.float32)
    wm_ch = np.concatenate([wmod[:, g * 1024 + ch0:g * 1024 + ch0 + 128] for g in range(3)],
                           axis=1)

    return dict(
        u_conv=u_conv, u_tok=u_tok,
        w1_in=np.ascontiguousarray(w1).astype(np.float32),
        w2_in=np.ascontiguousarray(w2).astype(np.float32),
        b1c_in=b1c, b2c_in=b2c,
        cT_in=np.ascontiguousarray(cvec.T).astype(np.float32),
        wmod_ch=np.ascontiguousarray(wm_ch).astype(np.float32),
        wmod_tok=np.ascontiguousarray(wmod[:, 3 * 1024:6 * 1024]).astype(np.float32),
        arw_in=rowr(Ar), aiw_in=rowr(Ai), ldt_row=rowr(ldt_full),
        d_in=np.full((1, 8), float(Dp), np.float32),
        ssmq=np.ascontiguousarray(ssmq_arr),
        ssmq2=np.ascontiguousarray(ssmq2_arr),
        liota=lio,
        jmat=np.ascontiguousarray(np.eye(P, dtype=np.float32)[::-1]),
        ident=np.eye(P, dtype=np.float32),
        sel5=_mk_sel5(),
    )


def kernel(L0_attn_u, L0_attn_log_dt, L0_attn_A_real, L0_attn_A_imag,
           L0_attn_C_real, L0_attn_C_imag, L0_attn_D,
           L0_ffn_w1, L0_ffn_b1, L0_ffn_w2, L0_ffn_b2, L0_c, L0_wmod):
    import ml_dtypes
    u = np.asarray(L0_attn_u, np.float32)
    nc = _get_nc()
    idb = np.eye(P).astype(ml_dtypes.bfloat16)
    in_maps = []
    for core in range(8):
        im = _host_core_inputs(core, u, np.asarray(L0_attn_log_dt), np.asarray(L0_attn_A_real),
                               np.asarray(L0_attn_A_imag), np.asarray(L0_attn_C_real),
                               np.asarray(L0_attn_C_imag), np.asarray(L0_attn_D),
                               np.asarray(L0_ffn_w1), np.asarray(L0_ffn_b1),
                               np.asarray(L0_ffn_w2), np.asarray(L0_ffn_b2),
                               np.asarray(L0_c), np.asarray(L0_wmod))
        im["identb"] = idb
        in_maps.append(im)
    res = run_bass_kernel_spmd(nc, in_maps, core_ids=list(range(8)))
    out = np.zeros((2, 16384, 1024), np.float32)
    for core in range(8):
        b_me, j_me = core // 4, core % 4
        out[b_me, 4096 * j_me:4096 * (j_me + 1), :] = res.results[core]["out_e"].T
    return out


# version 10 (final)
# speedup vs baseline: 1.0214x; 1.0214x over previous
import numpy as np
import concourse.bass as bass
import concourse.mybir as mybir
from concourse.bass_utils import run_bass_kernel_spmd
from concourse.aot_env import allow_ici_all_to_all
from contextlib import ExitStack

F32 = mybir.dt.float32
BF16 = mybir.dt.bfloat16
AF = mybir.ActivationFunctionType
ALU = mybir.AluOpType
AX = mybir.AxisListType

P = 128
CPC = 128
NB = 2
NI = 64
NH = 2
NM = 32
Q = 256
L = Q * NI
D = 1024
FF = 4096
TOK = 4096
NIL = 32
NFT = 32
NDT = 8
NCH = 8
TWO_PI = float(2.0 * np.pi)
INV_2PI = float(1.0 / (2.0 * np.pi))
HALF_PI = float(0.5 * np.pi)
M23 = 8388608.0
KB_STRIDE = 383
KBUF_LEN = CPC * KB_STRIDE + 512
RW = 512
T_IN = 13 * 16
T_H1 = 4 * 16
N_RECV = 32 * 16

QORD = [14, 15] + list(range(31, 15, -1)) + list(range(13, -1, -1))

QWT = [("q", k) for k in range(32)] + [("w", k) for k in range(32)]
NSLOT = len(QWT)
NTRS = NSLOT * 8


def qslot_idx(p):
    return p


def wslot_idx(k):
    return 32 + k


QWT_OWNER = []
QWT_DST = []
for _kind, _k in QWT:
    for _j in range(8):
        QWT_OWNER.append("v" if _j < 4 else "s")
        if _kind == "q":
            QWT_DST.append(("q", QORD[_k], _j))
        else:
            QWT_DST.append(("w", _k, _j))


def qwt_gv(x):
    return sum(1 for y in range(x) if QWT_OWNER[y] == "v")


def ev_owner(k):
    return "v" if k % 6 < 3 else "s"


def bev_owner(c):
    return "v" if c % 8 < 4 else "s"


def cnt_upto(owner_fn, m, who):
    return sum(1 for x in range(m + 1) if owner_fn(x) == who)


def rap(handle, off, dims):
    return bass.AP(handle, off, [list(d) for d in dims])


def round_kind(rnd):
    return rnd // 24, ("m1", "k", "cm")[(rnd % 24) // 8], rnd % 8


def x2t_vcnt(m):
    return 4 * (m // 8) + min(m % 8 + 1, 4)


def x2t_scnt(m):
    return 4 * (m // 8) + max(m % 8 - 3, 0)


_DRAIN_AFTER = {"tensor_tensor", "tensor_copy", "tensor_scalar", "tensor_scalar_mul",
                "tensor_scalar_add", "tensor_scalar_sub", "tensor_scalar_max", "tensor_scalar_min",
                "tensor_single_scalar", "scalar_tensor_tensor", "tensor_reduce", "reduce_max",
                "reduce_sum", "memset", "activation", "tensor_max", "tensor_relu", "tensor_add",
                "tensor_sub", "tensor_mul", "select", "reciprocal"}


class _InstProxy:

    def __init__(self, ins, rec):
        object.__setattr__(self, "_ins", ins)
        object.__setattr__(self, "_rec", rec)

    def then_inc(self, sem, v=1):
        self._rec.append((sem, v))
        return self._ins.then_inc(sem, v)

    def __getattr__(self, a):
        return getattr(self._ins, a)


class DrainEng:

    def __init__(self, eng, sem=None, multi=()):
        self._e = eng
        self._sem = sem
        self._multi = set(multi)
        self._n = 0
        self._counts = {}
        self._pend = None
        self._edge = None

    def _settle(self):
        if self._pend is None:
            return
        ins, rec = self._pend
        self._pend = None
        if not rec:
            ins.then_inc(self._sem, 1)
            self._n += 1
            self._edge = ("chain", self._n)
            return
        sem, v = rec[-1]
        key = id(sem)
        if key in self._multi:
            self._e.drain()
            self._edge = None
            return
        self._counts[key] = self._counts.get(key, 0) + sum(x[1] for x in rec)
        self._edge = ("sem", sem, self._counts[key])

    def __getattr__(self, name):
        f = getattr(self._e, name)
        if name in _DRAIN_AFTER:
            if self._sem is None:
                def wrap(*a, **k):
                    r = f(*a, **k)
                    self._e.drain()
                    return r
                return wrap

            def wrap(*a, **k):
                self._settle()
                if self._edge is not None:
                    if self._edge[0] == "chain":
                        self._e.wait_ge(self._sem, self._edge[1])
                    else:
                        self._e.wait_ge(self._edge[1], self._edge[2])
                r = f(*a, **k)
                rec = []
                self._pend = (r, rec)
                return _InstProxy(r, rec)
            return wrap
        if name in ("sem_inc", "drain", "wait_ge"):
            def passthrough(*a, **k):
                if name == "sem_inc":
                    self._settle()
                    if self._edge is not None and self._edge[0] == "chain":
                        pass
                return f(*a, **k)
            return passthrough
        return f


def build_program():
    nc = bass.Bass("TRN2", target_bir_lowering=False)

    u_conv = nc.dram_tensor("u_conv", [P, CPC * NH * NB * NI], F32, kind="ExternalInput").ap()
    u_tok = nc.dram_tensor("u_tok", [P, NIL * D], F32, kind="ExternalInput").ap()
    w1_in = nc.dram_tensor("w1_in", [FF, D], F32, kind="ExternalInput").ap()
    w2_in = nc.dram_tensor("w2_in", [D, FF], F32, kind="ExternalInput").ap()
    b1c_in = nc.dram_tensor("b1c_in", [P, NFT], F32, kind="ExternalInput").ap()
    b2c_in = nc.dram_tensor("b2c_in", [P, NDT], F32, kind="ExternalInput").ap()
    cT_in = nc.dram_tensor("cT_in", [D, 2], F32, kind="ExternalInput").ap()
    wmod_ch = nc.dram_tensor("wmod_ch", [D, 3 * CPC], F32, kind="ExternalInput").ap()
    wmod_tok = nc.dram_tensor("wmod_tok", [D, 3 * D], F32, kind="ExternalInput").ap()
    arw_in = nc.dram_tensor("arw_in", [1, CPC * NM], F32, kind="ExternalInput").ap()
    aiw_in = nc.dram_tensor("aiw_in", [1, CPC * NM], F32, kind="ExternalInput").ap()
    ldt_row = nc.dram_tensor("ldt_row", [1, CPC * NM], F32, kind="ExternalInput").ap()
    d_in = nc.dram_tensor("d_in", [1, 8], F32, kind="ExternalInput").ap()
    ssmq = nc.dram_tensor("ssmq", [64, 192], F32, kind="ExternalInput").ap()
    ssmq2 = nc.dram_tensor("ssmq2", [32, 640], F32, kind="ExternalInput").ap()
    liota = nc.dram_tensor("liota", [P, 8], F32, kind="ExternalInput").ap()
    jmat = nc.dram_tensor("jmat", [P, P], F32, kind="ExternalInput").ap()
    ident = nc.dram_tensor("ident", [P, P], F32, kind="ExternalInput").ap()
    identb = nc.dram_tensor("identb", [P, P], BF16, kind="ExternalInput").ap()
    sel5 = nc.dram_tensor("sel5", [P, 3 * P], F32, kind="ExternalInput").ap()
    out_e = nc.dram_tensor("out_e", [D, TOK], F32, kind="ExternalOutput").ap()

    kbuf_d = nc.dram_tensor("kbuf_d", [1, KBUF_LEN], BF16)
    cz_d = nc.dram_tensor("cz_d", [64, CPC * 2 * 128], BF16)
    kimg_d = nc.dram_tensor("kimg_d", [P, CPC * 2 * 128], BF16)
    czt_d = nc.dram_tensor("czt_d", [1, 2 * CPC * NM], F32)
    modb_d = nc.dram_tensor("modb_d", [1, 4096], F32)
    ybo_d = nc.dram_tensor("ybo_d", [P, CPC * NH * NB * NI], F32)
    send_d = [nc.dram_tensor("send_d%d" % g, [8, TOK * 32], F32) for g in range(4)]
    recv_d = [nc.dram_tensor("recv_d%d" % g, [8, TOK * 32], F32) for g in range(4)]
    lnin_d = nc.dram_tensor("lnin_d", [P, 512], F32)
    lnout_d = nc.dram_tensor("lnout_d", [P, 512], F32)
    mx1i_d = nc.dram_tensor("mx1i_d", [1, 1], F32)
    mx1o_d = nc.dram_tensor("mx1o_d", [1, 1], F32)
    mx2i_d = nc.dram_tensor("mx2i_d", [1, 1], F32)
    mx2o_d = nc.dram_tensor("mx2o_d", [1, 1], F32)
    x2t_d = nc.dram_tensor("x2t_d", [1, D * TOK], F32)
    h1sp_d = nc.dram_tensor("h1sp_d", [1, NFT * NCH * P * 512], F32)

    RG = [list(range(8))]

    es = ExitStack()
    with es:
        A_h1 = es.enter_context(nc.sbuf_tensor("A_h1", [P, 32768], BF16))
        A_m1 = es.enter_context(nc.sbuf_tensor("A_m1", [P, 16384], BF16))
        A_zb = es.enter_context(nc.sbuf_tensor("A_zb", [P, 16384], BF16))
        A_tm = es.enter_context(nc.sbuf_tensor("A_tm", [P, 16384], BF16))
        A_st = es.enter_context(nc.sbuf_tensor("A_st", [64, 512], BF16))
        A_ys = es.enter_context(nc.sbuf_tensor("A_ys", [P, 2048], BF16))
        A_sm = es.enter_context(nc.sbuf_tensor("A_sm", [P, 2048], BF16))
        A_rw = es.enter_context(nc.sbuf_tensor("A_rw", [P, 8192], BF16))
        A_tp = es.enter_context(nc.sbuf_tensor("A_tp", [P, 4096], BF16))
        A_fs = es.enter_context(nc.sbuf_tensor("A_fs", [P, 6144], BF16))
        liota_s = es.enter_context(nc.sbuf_tensor("liota_s", [P, 8], F32))
        j_s = es.enter_context(nc.sbuf_tensor("j_s", [P, P], F32))
        id_s = es.enter_context(nc.sbuf_tensor("id_s", [P, P], F32))
        idb_s = es.enter_context(nc.sbuf_tensor("idb_s", [P, P], BF16))
        sel_s = es.enter_context(nc.sbuf_tensor("sel_s", [P, 3 * P], F32))
        pb = [es.enter_context(nc.psum_tensor("pb%d" % i, [P, 512], F32)) for i in range(8)]

        h1 = A_h1[:].rearrange("p (c h b i) -> p c h b i", c=CPC, h=NH, b=NB, i=NI)
        h1f2 = A_h1[:].rearrange("p (c t) -> p c t", c=CPC, t=256)
        m1f = A_m1[:].bitcast(F32)
        m1 = m1f.rearrange("p (c e n) -> p c e n", c=CPC, e=2, n=NM)
        bar = m1f.rearrange("p (c2 b i) -> p c2 b i", c2=64, b=NB, i=NI)
        toep = A_m1[:, 0:8192].rearrange("p (c k r) -> p c k r", c=32, k=2, r=128)
        czg = A_m1[:, 8192:16384].rearrange("p (c l r) -> p c l r", c=32, l=2, r=128)
        czg_pv = A_m1[:, 8192:16384].rearrange("p (c2 q l r) -> p c2 q l r", c2=16, q=2, l=2, r=128)
        zb = A_zb[:].rearrange("p (c k e n) -> p c k e n", c=CPC, k=2, e=2, n=NM)
        archb = A_zb[:, 0:8192].rearrange("p (c2 b j) -> p c2 b j", c2=64, b=NB, j=NI)
        tmf = A_tm[:].bitcast(F32)
        arch = tmf.rearrange("p (c2 b j) -> p c2 b j", c2=64, b=NB, j=NI)
        tpf = A_tp[:].bitcast(F32)

        def tslot(slot):
            return tpf[:, slot * 512:slot * 512 + 512]

        zbf32 = A_zb[:].bitcast(F32)
        u0 = zbf32[0:32, 0:128]
        u1 = zbf32[0:32, 128:256]
        u2 = zbf32[0:32, 256:384]
        u3 = zbf32[0:32, 384:512]
        u4 = zbf32[0:32, 512:640]
        u5 = zbf32[0:32, 640:768]
        u2a = zbf32[0:32, 768:896]
        v_zre = zbf32[0:32, 896:1024]
        v_zim = zbf32[0:32, 1024:1152]
        v_e = zbf32[0:32, 1152:1280]
        v_cr = zbf32[0:32, 1280:1408]
        v_ci = zbf32[0:32, 1408:1536]
        v_t = zbf32[0:32, 1536:1664]
        wmstv = zbf32[:, 2048:5120].rearrange("p (k g) -> p k g", k=8, g=384)
        wtst = [tmf[:, 0:1024], tmf[:, 1024:2048]]
        wtstv = [w.rearrange("p (k c) -> p k c", k=8, c=128) for w in wtst]
        sqst = A_tm[:, 4096:8192]

        rep = None
        stage = A_st[:].rearrange("k (c r) -> k c r", c=4, r=128)
        ysf = A_ys[:].bitcast(F32)
        rows = A_rw[:].bitcast(F32)
        R_AR, R_AI, R_DT = 0, 1, 2

        sm = A_sm[:].bitcast(F32)
        zrr = sm[:, 0:64]
        zi_t = sm[0:64, 64:128]
        p1v = sm[:, 128:256].rearrange("p (c b) -> p c b", c=64, b=2)
        t2v = sm[0:64, 256:384].rearrange("p (c b) -> p c b", c=64, b=2)
        t3v = sm[64:128, 384:512].rearrange("p (c b) -> p c b", c=64, b=2)
        czq = sm[0:64, 128:256]
        czh = sm[0:64, 256:384]
        kred = sm[:, 512:768].rearrange("p (l c) -> p l c", l=2, c=CPC)
        ktr = sm[:, 512:768].rearrange("p (l q) -> p l q", l=2, q=128)
        zq_re = sm[0:64, 768:832]
        zq_im = sm[0:64, 832:896]
        zq_t1 = sm[0:64, 896:960]
        zq_t2 = sm[0:64, 960:1024]
        dval = sm[0:1, 384:392]
        ctsil = sm[:, 400:416]
        mstg = sm[:, 416:422]
        tokt = sm[:, 422:424]
        gbt = sm[:, 424:440]


        x2a = A_h1[:].bitcast(F32)
        x2b = m1f
        x2c = tmf

        def x2chunk(il):
            if il < 16:
                return x2a[:, il * 1024:(il + 1) * 1024]
            if il < 24:
                return x2b[:, (il - 16) * 1024:(il - 16) * 1024 + 1024]
            return x2c[:, (il - 24) * 1024:(il - 24) * 1024 + 1024]

        x1qt = [A_zb[:, 0:4096], A_zb[:, 4096:8192], A_zb[:, 8192:12288], A_zb[:, 12288:16384],
                A_rw[:, 0:4096], A_rw[:, 4096:8192], A_tp[:, 0:4096], A_h1[:, 28672:32768]]
        w1qt = [A_m1[:, i * 4096:(i + 1) * 4096] for i in range(4)] + \
               [A_tm[:, i * 4096:(i + 1) * 4096] for i in range(4)]
        w2qt = [A_zb[:, i * 1024:(i + 1) * 1024] for i in range(16)] + \
               [A_rw[:, i * 1024:(i + 1) * 1024] for i in range(8)] + \
               [A_tm[:, i * 1024:(i + 1) * 1024] for i in range(8)]
        bstg1 = [A_fs[:, 2048:3072], A_fs[:, 3072:4096]]
        bstg2 = [A_tm[:, 8192:9216], A_tm[:, 9216:10240]]
        x2qs = [A_h1[:, 0:16384], A_h1[:, 16384:32768], A_m1[:, 0:16384]]

        fsf = A_fs[:].bitcast(F32)
        wqin = fsf[:, 0:1024]
        h1st = [fsf[:, 1024 + i * 512:1024 + (i + 1) * 512] for i in range(4)]
        h1in = h1st
        outst = [fsf[:, i * 512:(i + 1) * 512] for i in range(2)]
        x2tst = fsf[:, 1024:2048]
        rep = [fsf[:, 0:512], fsf[:, 512:1024], fsf[:, 1024:1536]]
        ssmq2_s = fsf[0:32, 1536:2176]
        ssmq_s = fsf[0:64, 2176:2368]
        ct_s = fsf[:, 2368:2384]
        gar_s = fsf[:, 1536:1792]
        stat_s = fsf[:, 2384:2896]
        mods_s = fsf[:, 2896:2920]
        statv_m = stat_s[:, 0:256]
        statv_v = stat_s[:, 256:512]
        m_bc = statv_m.rearrange("p (o t) -> p o t", o=1).broadcast_to([P, 128, 256])
        r_bc = statv_v.rearrange("p (o t) -> p o t", o=1).broadcast_to([P, 128, 256])
        x2tstv = x2tst.rearrange("p (dt q) -> p dt q", dt=8, q=128)
        qstage = A_ys[:, 0:2048]
        qslotB = fsf[:, 2048:2560]

        fsm = sm
        mxc = fsm[:, 0:32]
        rsc = fsm[:, 32:64]
        mrs = fsm[:, 64:96]
        maxw1 = fsm[:, 96:128]
        rw1c = fsm[:, 128:160]
        sc1c = fsm[:, 160:192]
        b1qc = fsm[:, 192:224]
        b1cc = fsm[:, 224:256]
        maxw2 = fsm[:, 256:264]
        rw2c = fsm[:, 264:272]
        sc2g = fsm[:, 272:280]
        b2qg = fsm[:, 280:288]
        b2cc = fsm[:, 288:296]
        s1c = fsm[:, 300:301]
        is1c = fsm[:, 301:302]
        s2c = fsm[:, 302:303]
        is2c = fsm[:, 303:304]
        gro = fsm[0:1, 304:312]
        mx1col = fsm[:, 312:313]
        mx2col = fsm[:, 313:314]
        tcol = fsm[:, 314:315]
        tco2 = fsm[:, 315:316]
        trow = fsm[0:1, 316:320]
        csm1 = fsm[:, 320:352]
        csm2 = fsm[:, 352:384]
        qslotA = fsm[:, 384:896]
        gbs = fsm[:, 896:904]
        x2tin = [ysf[:, 0:512], qslotA]
        rstg = [ysf[:, 512:1024], tmf[:, 5120:5632]]

        SEMN = ["s_in", "s_h1", "s_rep", "s_repev", "s_arg", "s_trig", "s_m1", "s_zb", "s_zbev", "s_kred", "s_cm", "s_tp", "s_ev_v", "s_ev_s", "s_czd", "s_zq", "s_czq", "s_bmm", "s_bev_v", "s_bev_s", "s_scan", "s_tab", "s_tabm", "s_ymm", "s_yev", "s_yout", "s_ktr", "s_kimg", "s_kz", "s_czt", "s_silu", "s_mch", "s_mchv", "s_modb", "s_modr", "s_wts", "s_mtok", "s_mtev", "s_chrm", "s_chre", "s_stat", "s_lnb", "s_cc1", "s_h1m", "s_yga", "s_send", "s_send1", "s_send2", "s_send3", "s_recv", "s_utok", "s_acc", "s_x2t", "s_x2ve", "s_x2vs", "s_x2td", "s_h2r", "s_ln2s", "s_ln2f", "s_h2s", "s_h2v", "s_mx1tp", "s_mx1v", "s_mx1d", "s_mx1r", "s_s1row", "s_s1rm", "s_colq", "s_bc", "s_sq", "s_wmi1", "s_w1a", "s_wmv1", "s_w1mx", "s_w1i", "s_w1s", "s_w1v", "s_qs", "s_q", "s_qt", "s_qev", "s_qes", "s_mm1", "s_ev1", "s_sp", "s_w2tp", "s_w2ev", "s_w2es", "s_m2acc", "s_hin", "s_gsend", "s_rmg", "s_r2s", "s_mm2", "s_e2s", "s_e2v", "s_xin", "s_out", "s_vch", "s_sch"]
        SEM = {}
        for nm_ in SEMN:
            SEM[nm_] = es.enter_context(nc.semaphore(nm_))

        def W(eng, s, v):
            if v > 0:
                eng.wait_ge(SEM[s], v)

        block = es.enter_context(nc.Block())

        @block.sync
        def _(sy):
            W(sy, "s_kz", 2)
            n_in = 0
            for dst_r, src in [(R_AR, arw_in), (R_AI, aiw_in), (R_DT, ldt_row)]:
                sy.dma_start(out=rows[dst_r:dst_r + 1, :], in_=src[:, :]).then_inc(SEM["s_in"], 16); n_in += 1
            sy.dma_start(out=dval[:, :], in_=d_in[:, :]).then_inc(SEM["s_in"], 16); n_in += 1
            sy.dma_start(out=liota_s[:], in_=liota[:, :]).then_inc(SEM["s_in"], 16); n_in += 1
            sy.dma_start(out=j_s[:], in_=jmat[:, :]).then_inc(SEM["s_in"], 16); n_in += 1
            sy.dma_start(out=id_s[:], in_=ident[:, :]).then_inc(SEM["s_in"], 16); n_in += 1
            sy.dma_start(out=idb_s[:], in_=identb[:, :]).then_inc(SEM["s_in"], 16); n_in += 1
            sy.dma_start(out=sel_s[:], in_=sel5[:, :]).then_inc(SEM["s_in"], 16); n_in += 1
            sy.dma_start(out=ssmq_s[:], in_=ssmq[:, :]).then_inc(SEM["s_in"], 16); n_in += 1
            sy.dma_start(out=ssmq2_s[:], in_=ssmq2[:, :]).then_inc(SEM["s_in"], 16); n_in += 1
            sy.dma_start(out=ct_s[:].rearrange("p (k o) -> p k o", k=8, o=2),
                         in_=cT_in.rearrange("(k p) o -> p k o", k=8),
                         allow_slow_non_contiguous=True).then_inc(SEM["s_in"], 16); n_in += 1
            sy.dma_start(out=wmstv, in_=wmod_ch.rearrange("(k p) g -> p k g", k=8)
                         ).then_inc(SEM["s_in"], 16); n_in += 1
            assert n_in * 16 == T_IN, n_in
            sy.dma_start(out=rap(kbuf_d, 0, [(383, 128), (1, 127)]),
                         in_=A_ys[:, 0:127]).then_inc(SEM["s_kz"], 16)
            sy.dma_start(out=rap(kbuf_d, CPC * 383, [(512, 1), (1, 512)]),
                         in_=A_ys[0:1, 0:512]).then_inc(SEM["s_kz"], 16)
            def wtst_blk(blk):
                if blk >= 1:
                    W(sy, "s_mtok", blk)
                sy.dma_start(out=wtstv[blk % 2],
                             in_=wmod_tok.rearrange("(k p) f -> p k f", k=8)[:, :, blk * 128:(blk + 1) * 128]
                             ).then_inc(SEM["s_wts"], 16)

            for blk in range(0, 4):
                wtst_blk(blk)
            for g in range(3):
                W(sy, "s_mchv", g + 1)
                sy.dma_start(out=rap(modb_d, g * 256, [(2, 128), (1, 2)]),
                             in_=mstg[:, 2 * g:2 * g + 2]).then_inc(SEM["s_modb"], 16)
            W(sy, "s_modb", 48)
            sy.dma_start(out=rows[64:65, 0:768], in_=modb_d.ap()[:, 0:768]).then_inc(SEM["s_modr"], 16)
            for blk in range(4, 8):
                wtst_blk(blk)
            W(sy, "s_czt", 1)
            for t_i, t_src in enumerate([v_cr, v_ci]):
                dst = rap(czt_d, t_i * CPC * NM, [(1, 32), (32, 128)])
                sy.dma_start(out=dst, in_=t_src, allow_slow_non_contiguous=True).then_inc(SEM["s_czt"], 16)
            W(sy, "s_czt", 33)
            sy.dma_start(out=rows[32:33, :], in_=czt_d.ap()[:, 0:CPC * NM]).then_inc(SEM["s_czt"], 16)
            sy.dma_start(out=rows[33:34, :], in_=czt_d.ap()[:, CPC * NM:2 * CPC * NM]).then_inc(SEM["s_czt"], 16)
            for blk in range(8, 11):
                wtst_blk(blk)
            W(sy, "s_stat", 1)
            sy.dma_start(out=lnin_d.ap()[:, :], in_=stat_s[:, :]).then_inc(SEM["s_lnb"], 16)
            W(sy, "s_cc1", 1)
            sy.dma_start(out=stat_s[:, :], in_=lnout_d.ap()[:, :]).then_inc(SEM["s_lnb"], 16)
            for blk in range(11, 24):
                wtst_blk(blk)

            def cz_blk(kb):
                lh, cb = kb // 32, kb % 32
                k_hi = lh * 128 + (cb + 1) * 4 - 1
                W(sy, "s_ev_v", cnt_upto(ev_owner, k_hi, "v"))
                W(sy, "s_ev_s", cnt_upto(ev_owner, k_hi, "s"))
                dst = rap(cz_d, lh * 128 + cb * 4 * 256, [(CPC * 2 * 128, 64), (256, 4), (1, 128)])
                sy.dma_start(out=dst, in_=stage[:, :, :]).then_inc(SEM["s_czd"], 16)

            for kb in range(0, 32):
                cz_blk(kb)
            W(sy, "s_mtev", 24)
            sy.dma_start(out=rap(modb_d, 768, [(1, 128), (128, 8)]),
                         in_=mods_s[:, 8:16], allow_slow_non_contiguous=True).then_inc(SEM["s_modb"], 16)
            sy.dma_start(out=rap(modb_d, 768 + 1024, [(1, 128), (128, 8)]),
                         in_=mods_s[:, 0:8], allow_slow_non_contiguous=True).then_inc(SEM["s_modb"], 16)
            W(sy, "s_modb", 80)
            sy.dma_start(out=rows[64:65, 768:2816], in_=modb_d.ap()[:, 768:2816]).then_inc(SEM["s_modr"], 16)
            for kb in range(32, 64):
                cz_blk(kb)
            W(sy, "s_kz", 34)
            W(sy, "s_h1", 80)
            for blk in range(2):
                kimg_w = rap(kimg_d, blk * 128, [(256, CPC), (32768, P), (1, 128)])
                kimg_r = rap(kbuf_d, 127 + 128 * blk, [(KB_STRIDE, CPC), (-1, P), (1, 128)])
                sy.dma_start(out=kimg_w, in_=kimg_r).then_inc(SEM["s_kimg"], 16)
            W(sy, "s_scan", 2)
            sy.dma_start(out=b1cc[:, :], in_=b1c_in[:, :]).then_inc(SEM["s_bc"], 16)
            sy.dma_start(out=b2cc[:, :], in_=b2c_in[:, :]).then_inc(SEM["s_bc"], 16)
            ybv = ybo_d.ap().rearrange("p (c h b i) -> p c h b i", c=CPC, h=NH, b=NB, i=NI)
            for g in range(4):
                W(sy, "s_kimg", 32)
                if g == 0:
                    W(sy, "s_scan", 2)
                else:
                    W(sy, "s_ymm", g * 32)
                sy.dma_start(out=A_m1[:, 0:8192],
                             in_=kimg_d.ap()[:, g * 8192:(g + 1) * 8192]).then_inc(SEM["s_tab"], 16)
                W(sy, "s_tabm", g + 1)
                for reim in range(2):
                    for par_ in range(2):
                        dst2 = czg_pv[reim * 64 + par_ * 32:reim * 64 + par_ * 32 + 32, :, par_, :, :]
                        srcv = cz_d.ap().rearrange("k (gq c2 q l r) -> k gq c2 q l r",
                                                   gq=4, c2=16, q=2, l=2, r=128)
                        src2 = srcv[reim * 32:(reim + 1) * 32, g, :, par_, :, :]
                        sy.dma_start(out=dst2, in_=src2).then_inc(SEM["s_tab"], 16)
                for blk in range(g * 8, (g + 1) * 8):
                    W(sy, "s_yga", (blk + 1) * 4)
                    sy.dma_start(out=ybo_d.ap()[:, blk * 1024:(blk + 1) * 1024],
                                 in_=ysf[:, 0:1024]).then_inc(SEM["s_yout"], 16)
                W(sy, "s_yout", (g + 1) * 8 * 16)
                sgsem = ["s_send", "s_send1", "s_send2", "s_send3"][g]
                for s in range(3):
                    b_s, j_q = s // 4, s % 4
                    src = ybv[:, g * 32:(g + 1) * 32, :, b_s, j_q * 16:(j_q + 1) * 16]
                    dst = rap(send_d[g], s * TOK * 32, [(1024, P), (32, 32), (16, 2), (1, 16)])
                    sy.dma_start(out=dst, in_=src).then_inc(SEM[sgsem], 16)
            for il in range(NIL):
                W(sy, "s_ymm", 128)
                sy.dma_start(out=x2chunk(il), in_=u_tok[:, il * 1024:(il + 1) * 1024]).then_inc(SEM["s_utok"], 16)
            for il in range(NIL):
                W(sy, "s_x2ve", (il + 1) * 4)
                W(sy, "s_x2vs", (il + 1) * 4)
                dst = rap(x2t_d, il * 128, [(TOK, 128), (TOK * 128, 8), (1, 128)])
                sy.dma_start(out=dst, in_=x2tstv).then_inc(SEM["s_x2td"], 16)
                k = il
                if k >= 1:
                    W(sy, "s_wmv1", k)
                sy.dma_start(out=wqin, in_=w1_in[k * 128:(k + 1) * 128, :]).then_inc(SEM["s_wmi1"], 16)
            W(sy, "s_mx1v", 1)
            sy.dma_start(out=mx1i_d.ap()[:, 0:1], in_=gro[:, 0:1]).then_inc(SEM["s_mx1d"], 16)
            W(sy, "s_cc1", 6)
            sy.dma_start(out=gro[:, 1:2], in_=mx1o_d.ap()[:, 0:1]).then_inc(SEM["s_mx1r"], 16)
            W(sy, "s_wmv1", 32)
            for k in range(32):
                if k >= 1:
                    W(sy, "s_w1v", k)
                sy.dma_start(out=wqin, in_=w1_in[k * 128:(k + 1) * 128, :]).then_inc(SEM["s_w1i"], 16)
            W(sy, "s_w1v", 32)
            for blkN in range(NFT * NCH):
                W(sy, "s_ev1", blkN + 1)
                sy.dma_start(out=rap(h1sp_d, blkN * P * 512, [(512, P), (1, 512)]),
                             in_=h1st[blkN % 4]).then_inc(SEM["s_sp"], 16)
                if blkN % 8 == 0:
                    k = blkN // 8
                    dt, fq = k // 4, k % 4
                    if k >= 1:
                        W(sy, "s_wmv1", 32 + k)
                    sy.dma_start(out=wqin,
                                 in_=w2_in[dt * 128:(dt + 1) * 128, fq * 1024:(fq + 1) * 1024]
                                 ).then_inc(SEM["s_wmi1"], 16)
            W(sy, "s_mx1v", 2)
            sy.dma_start(out=mx2i_d.ap()[:, 0:1], in_=gro[:, 4:5]).then_inc(SEM["s_mx1d"], 16)
            W(sy, "s_cc1", 7)
            sy.dma_start(out=gro[:, 5:6], in_=mx2o_d.ap()[:, 0:1]).then_inc(SEM["s_mx1r"], 16)
            W(sy, "s_wmv1", 64)
            for k in range(32):
                dt, fq = k // 4, k % 4
                if k >= 1:
                    W(sy, "s_w1v", 32 + k)
                sy.dma_start(out=wqin,
                             in_=w2_in[dt * 128:(dt + 1) * 128, fq * 1024:(fq + 1) * 1024]
                             ).then_inc(SEM["s_w1i"], 16)
            W(sy, "s_sp", NFT * NCH * 16)
            W(sy, "s_w2tp", 256)
            sptr = {"x": 0, "o": 0}

            def xin_dma(j):
                chg, dt, par_ = j // 16, (j % 16) // 2, j % 2
                och = 2 * chg + par_
                if j >= 2:
                    W(sy, "s_e2v", j - 1)
                sy.dma_start(out=x2tin[j % 2],
                             in_=rap(x2t_d, dt * 128 * TOK + och * 512, [(TOK, 128), (1, 512)])
                             ).then_inc(SEM["s_xin"], 16)

            def out_dma(j):
                chg, dt, par_ = j // 16, (j % 16) // 2, j % 2
                och = 2 * chg + par_
                W(sy, "s_e2v", j + 1)
                sy.dma_start(out=out_e[dt * 128:(dt + 1) * 128, och * 512:(och + 1) * 512],
                             in_=outst[j % 2]).then_inc(SEM["s_out"], 16)

            def io_below(lim):
                while sptr["x"] < min(lim, 64):
                    if sptr["o"] < sptr["x"] - 1:
                        out_dma(sptr["o"]); sptr["o"] += 1
                    xin_dma(sptr["x"]); sptr["x"] += 1
                while sptr["o"] < min(lim - 1, 63):
                    out_dma(sptr["o"]); sptr["o"] += 1

            for ch in range(NCH):
                if ch >= 3:
                    io_below(16 * ((ch - 3) // 2 + 1))
                for ft in range(NFT):
                    k2 = ch * NFT + ft
                    if k2 >= 4:
                        W(sy, "s_r2s", k2 - 3)
                    blkN = ft * NCH + ch
                    sy.dma_start(out=h1in[k2 % 4],
                                 in_=rap(h1sp_d, blkN * P * 512, [(512, P), (1, 512)])
                                 ).then_inc(SEM["s_hin"], 16)
            io_below(64)
            out_dma(63)
            W(sy, "s_out", 64 * 16)

        @block.gpsimd
        def _(g):
            g.memset(A_ys[:, 0:512], 0.0).then_inc(SEM["s_kz"], 1)
            g.memset(A_rw[:, :], 0.0).then_inc(SEM["s_kz"], 1)
            for q_ in range(4):
                g.dma_start(out=A_h1[:, q_ * 8192:(q_ + 1) * 8192],
                            in_=u_conv[:, q_ * 8192:(q_ + 1) * 8192]).then_inc(SEM["s_h1"], 16)
            W(g, "s_lnb", 16)
            g.collective_compute("AllReduce", ALU.add, replica_groups=RG,
                                 ins=[lnin_d.ap().opt()], outs=[lnout_d.ap().opt()]).then_inc(SEM["s_cc1"], 1)
            W(g, "s_ktr", 3)
            g.dma_start(out=rap(kbuf_d, 127, [(383, 128), (128, 2), (1, 128)]),
                        in_=ktr[:, :, :]).then_inc(SEM["s_h1"], 16)
            def tabm_memset(g_i):
                if g_i == 0:
                    W(g, "s_scan", 2)
                else:
                    W(g, "s_ymm", g_i * 32)
                g.memset(A_m1[:, 8192:16384], 0.0).then_inc(SEM["s_tabm"], 1)

            ybg = ybo_d.ap().rearrange("p (c h b i) -> p c h b i", c=CPC, h=NH, b=NB, i=NI)

            def gp_sends(g_i):
                W(g, "s_yout", (g_i + 1) * 8 * 16)
                if g_i >= 1:
                    W(g, "s_cc1", g_i + 1)
                for s in range(6, 8):
                    b_s, j_q = s // 4, s % 4
                    srcg = ybg[:, g_i * 32:(g_i + 1) * 32, :, b_s, j_q * 16:(j_q + 1) * 16]
                    dstg = rap(send_d[g_i], s * TOK * 32, [(1024, P), (32, 32), (16, 2), (1, 16)])
                    g.dma_start(out=dstg, in_=srcg).then_inc(SEM["s_gsend"], 16)

            def a2a(g_i):
                W(g, ["s_send", "s_send1", "s_send2", "s_send3"][g_i], 6 * 16)
                W(g, "s_gsend", (g_i + 1) * 2 * 16)
                with allow_ici_all_to_all():
                    g.collective_compute("AllToAll", ALU.bypass, replica_groups=RG,
                                         ins=[send_d[g_i].ap().opt()],
                                         outs=[recv_d[g_i].ap().opt()]).then_inc(SEM["s_cc1"], 1)

            tabm_memset(0)
            tabm_memset(1)
            gp_sends(0)
            a2a(0)
            tabm_memset(2)
            gp_sends(1)
            a2a(1)
            tabm_memset(3)
            gp_sends(2)
            a2a(2)
            gp_sends(3)
            a2a(3)
            for n in range(32):
                g_i, src = n // 8, n % 8
                W(g, "s_cc1", g_i + 2)
                if n >= 2:
                    W(g, "s_acc", 3 * (n - 1))
                g.dma_start(out=tpf[:, (n % 2) * 1024:(n % 2) * 1024 + 1024],
                            in_=rap(recv_d[g_i], src * TOK * 32, [(1024, P), (1, 1024)])
                            ).then_inc(SEM["s_recv"], 16)
            W(g, "s_mx1d", 16)
            g.collective_compute("AllReduce", ALU.max, replica_groups=RG,
                                 ins=[mx1i_d.ap().opt()], outs=[mx1o_d.ap().opt()]).then_inc(SEM["s_cc1"], 1)
            W(g, "s_mx1d", 32)
            g.collective_compute("AllReduce", ALU.max, replica_groups=RG,
                                 ins=[mx2i_d.ap().opt()], outs=[mx2o_d.ap().opt()]).then_inc(SEM["s_cc1"], 1)

        @block.tensor
        def _(te):
            W(te, "s_in", T_IN)
            W(te, "s_silu", 1)
            for grp in range(3):
                for k in range(8):
                    te.matmul(pb[grp][:, 0:2], wmstv[:, k, grp * 128:(grp + 1) * 128],
                              ctsil[:, 2 * k:2 * k + 2], start=(k == 0), stop=(k == 7)
                              ).then_inc(SEM["s_mch"], 1 if k == 7 else 0)
            for rnd in range(48):
                lh, kind, sub = round_kind(rnd)
                if rnd == 8:
                    W(te, "s_czt", 65)
                if rnd == 24:
                    W(te, "s_ev_v", cnt_upto(ev_owner, 127, "v"))
                    W(te, "s_ev_s", cnt_upto(ev_owner, 127, "s"))
                W(te, "s_repev", rnd)
                sels = (0, 1, 2) if kind == "m1" else (3, 4)
                sl = slice(sub * RW, (sub + 1) * RW)
                for i_, j_ in enumerate(sels):
                    if kind == "m1":
                        lhsT, rhs2 = sel_s[0:3, j_ * P:(j_ + 1) * P], rows[0:3, sl]
                    else:
                        lhsT, rhs2 = sel_s[32:34, (j_ - 3) * P:(j_ - 2) * P], rows[32:34, sl]
                    mm = te.matmul(pb[4 + i_][:, 0:512], lhsT, rhs2, start=True, stop=True)
                    if i_ == len(sels) - 1:
                        mm.then_inc(SEM["s_rep"], 1)
                if rnd == 3:
                    W(te, "s_modr", 16)
                    W(te, "s_mtev", 3)
                    te.matmul(pb[7][:, 0:256], sel_s[64:65, 0:128], rows[64:65, 512:768],
                              start=True, stop=True).then_inc(SEM["s_chrm"], 1)
                if rnd < 24:
                    blk = rnd
                    if blk == 3:
                        W(te, "s_chre", 1)
                    W(te, "s_mtev", blk)
                    W(te, "s_wts", (blk + 1) * 16)
                    for k in range(8):
                        te.matmul(pb[7][:, 0:2], wtstv[blk % 2][:, k, :],
                                  ctsil[:, 2 * k:2 * k + 2], start=(k == 0), stop=(k == 7)
                                  ).then_inc(SEM["s_mtok"], 1 if k == 7 else 0)
                if rnd == 25:
                    W(te, "s_mtev", 24)
                    W(te, "s_chre", 2)
                    te.matmul(pb[7][:, 0:512], sel_s[64:65, 0:128], rows[64:65, 0:512],
                              start=True, stop=True).then_inc(SEM["s_chrm"], 1)
                if rnd == 39:
                    W(te, "s_kred", 18)
                    W(te, "s_zbev", 32)
                    te.transpose(pb[0][:, 0:128], kred[:, 0, :], id_s[:])
                    te.transpose(pb[1][:, 0:128], kred[:, 1, :], id_s[:]).then_inc(SEM["s_ktr"], 1)
                if kind == "m1" and sub == 7:
                    W(te, "s_m1", (lh + 1) * 8)
                    for jj in range(16):
                        if lh * 16 + jj >= 4:
                            W(te, "s_zbev", lh * 16 + jj - 3)
                        te.matmul(pb[jj % 4][:, 0:512], j_s[:], m1f[:, jj * 512:(jj + 1) * 512],
                                  start=True, stop=True).then_inc(SEM["s_zb"], 1)
                if kind == "cm" and sub == 7:
                    W(te, "s_cm", (lh + 1) * 8)
                    if lh == 1:
                        W(te, "s_ktr", 3)
                    for c in range(CPC):
                        k = lh * CPC + c
                        if k >= 6:
                            ko = k - 6
                            W(te, "s_ev_" + ev_owner(ko), cnt_upto(ev_owner, ko, ev_owner(ko)))
                        te.transpose(pb[k % 6][0:64, 0:128], m1[:, c, :, :], id_s[:]).then_inc(SEM["s_tp"], 1)
            W(te, "s_czd", 64 * 16)
            W(te, "s_h1", T_H1)
            W(te, "s_h1m", 4)
            W(te, "s_zbev", 32)
            for c in range(CPC):
                if c >= 8:
                    co = c - 8
                    W(te, "s_bev_" + bev_owner(co), cnt_upto(bev_owner, co, bev_owner(co)))
                bk = pb[c % 8]
                te.matmul(bk[0:64, 0:128], zb[:, c, 0, :, :], h1[:, c, 0, :, :], start=True, stop=False)
                te.matmul(bk[0:64, 0:128], zb[:, c, 1, :, :], h1[:, c, 1, :, :], start=False, stop=True
                          ).then_inc(SEM["s_bmm"], 1)
            for g_i in range(4):
                W(te, "s_tab", (g_i + 1) * 80)
                if g_i == 0:
                    W(te, "s_scan", 2)
                for cl in range(32):
                    c = g_i * 32 + cl
                    if c >= 4:
                        W(te, "s_yev", c - 3)
                    ylo = pb[(c % 4) * 2][:, 0:128]
                    yhi = pb[(c % 4) * 2 + 1][:, 0:128]
                    sarch = archb[:, c // 2, :, :]
                    te.matmul(ylo, toep[:, cl, 0, :], h1[:, c, 0, :, :], start=True, stop=False)
                    te.matmul(yhi, toep[:, cl, 0, :], h1[:, c, 1, :, :], start=True, stop=False)
                    te.matmul(yhi, toep[:, cl, 1, :], h1[:, c, 0, :, :], start=False, stop=False)
                    te.matmul(ylo, czg[:, cl, 0, :], sarch, start=False, stop=True)
                    te.matmul(yhi, czg[:, cl, 1, :], sarch, start=False, stop=True).then_inc(SEM["s_ymm"], 1)
            W(te, "s_yev", 128)
            W(te, "s_modr", 32)
            W(te, "s_chre", 3)
            te.matmul(pb[4][:, 0:512], sel_s[64:65, 0:128], rows[64:65, 768:1280], start=True, stop=True)
            te.matmul(pb[5][:, 0:512], sel_s[64:65, 0:128], rows[64:65, 1280:1792], start=True, stop=True)
            te.matmul(pb[6][:, 0:512], sel_s[64:65, 0:128], rows[64:65, 1792:2304], start=True, stop=True)
            te.matmul(pb[7][:, 0:512], sel_s[64:65, 0:128], rows[64:65, 2304:2816], start=True, stop=True
                      ).then_inc(SEM["s_h2r"], 4)
            W(te, "s_acc", 96)
            for il in range(NIL):
                for dt in range(8):
                    n = il * 8 + dt
                    if n >= 4:
                        W(te, "s_x2ve", x2t_vcnt(n - 4))
                        W(te, "s_x2vs", x2t_scnt(n - 4))
                    te.transpose(pb[n % 4][:, 0:128], x2chunk(il)[:, dt * 128:(dt + 1) * 128],
                                 id_s[:]).then_inc(SEM["s_x2t"], 1)
            W(te, "s_h2v", 32)
            te.transpose(pb[4][0:1, 0:128], mx1col, id_s[:]).then_inc(SEM["s_mx1tp"], 1)
            W(te, "s_s1row", 1)
            te.matmul(pb[5][:, 0:2], sel_s[0:1, 0:128], gro[:, 2:4], start=True, stop=True
                      ).then_inc(SEM["s_s1rm"], 1)
            n_t = 0
            for kind, k in QWT:
                if kind == "q":
                    W(te, "s_q", k + 1)
                else:
                    W(te, "s_w1v", k + 1)
                for j in range(8):
                    if n_t >= 4:
                        gv = qwt_gv(n_t - 3)
                        W(te, "s_qev", gv)
                        W(te, "s_qes", (n_t - 3) - gv)
                    if kind == "q":
                        src = qstage[:, (k % 2) * 1024 + j * 128:(k % 2) * 1024 + (j + 1) * 128]
                    else:
                        src = bstg1[k % 2][:, j * 128:(j + 1) * 128]
                    te.matmul(pb[n_t % 4][:, 0:128], src, idb_s[:],
                              start=True, stop=True).then_inc(SEM["s_qt"], 1)
                    n_t += 1
            W(te, "s_qt", NTRS)
            W(te, "s_qev", 256)
            W(te, "s_qes", 256)
            for ft in range(NFT):
                for ch in range(NCH):
                    blkN = ft * 8 + ch
                    if blkN >= 8:
                        W(te, "s_ev1", blkN - 7)
                    for kt in range(8):
                        te.matmul(pb[ch][:, 0:512], w1qt[kt][:, ft * 128:(ft + 1) * 128],
                                  x1qt[kt][:, ch * 512:(ch + 1) * 512],
                                  start=(kt == 0), stop=(kt == 7)).then_inc(
                            SEM["s_mm1"], 1 if kt == 7 else 0)
            W(te, "s_m2acc", 256)
            W(te, "s_ev1", 256)
            te.transpose(pb[4][0:1, 0:128], mx2col, id_s[:]).then_inc(SEM["s_mx1tp"], 1)
            W(te, "s_s1row", 2)
            te.matmul(pb[5][:, 0:2], sel_s[0:1, 0:128], gro[:, 6:8], start=True, stop=True
                      ).then_inc(SEM["s_s1rm"], 1)
            n2 = 0
            for k in range(32):
                W(te, "s_w1v", 32 + k + 1)
                for j in range(8):
                    if n2 >= 4:
                        W(te, "s_w2ev", 4 * ((n2 - 4) // 8) + min((n2 - 4) % 8 + 1, 4))
                        W(te, "s_w2es", 4 * ((n2 - 4) // 8) + max((n2 - 4) % 8 - 3, 0))
                    te.matmul(pb[n2 % 4][:, 0:128], bstg2[k % 2][:, j * 128:(j + 1) * 128],
                              idb_s[:], start=True, stop=True).then_inc(SEM["s_w2tp"], 1)
                    n2 += 1
            for chg in range(4):
                W(te, "s_rmg", (2 * chg + 2) * 32)
                for dt in range(NDT):
                    if chg == 0:
                        W(te, "s_w2ev", min(4 * (4 * dt + 4), 128))
                        W(te, "s_w2es", min(4 * (4 * dt + 4), 128))
                    b2 = chg * 8 + dt
                    if b2 >= 4:
                        W(te, "s_e2s", 2 * (b2 - 4) + 2)
                    ps0 = pb[(dt % 4) * 2][:, 0:512]
                    ps1 = pb[(dt % 4) * 2 + 1][:, 0:512]
                    sl0 = x2qs[(2 * chg) % 3]
                    sl1 = x2qs[(2 * chg + 1) % 3]
                    for kt2 in range(32):
                        lhsT = w2qt[kt2][:, dt * 128:(dt + 1) * 128]
                        te.matmul(ps0, lhsT, sl0[:, kt2 * 512:(kt2 + 1) * 512],
                                  start=(kt2 == 0), stop=(kt2 == 31)).then_inc(
                            SEM["s_mm2"], 1 if kt2 == 31 else 0)
                        te.matmul(ps1, lhsT, sl1[:, kt2 * 512:(kt2 + 1) * 512],
                                  start=(kt2 == 0), stop=(kt2 == 31)).then_inc(
                            SEM["s_mm2"], 1 if kt2 == 31 else 0)

        @block.vector
        def _(ve):
            ve = DrainEng(ve, SEM["s_vch"], multi=[
                id(SEM[n]) for n in ("s_czq", "s_zq", "s_czt", "s_ktr", "s_sq",
                                     "s_stat", "s_ln2s")])
            W(ve, "s_in", T_IN)
            ar2, ai2 = ssmq2_s[:, 0:128], ssmq2_s[:, 128:256]
            dt2, cr2, ci2 = ssmq2_s[:, 256:384], ssmq2_s[:, 384:512], ssmq2_s[:, 512:640]
            ve.tensor_tensor(u0, ar2, dt2, ALU.add).then_inc(SEM["s_czq"], 1)
            W(ve, "s_czq", 3)
            ve.tensor_tensor(u2, ai2, u1, ALU.mult)
            for xi, (xs, czx) in enumerate([(1.0, None), (256.0, czq), (128.0, czh)]):
                ve.tensor_scalar_mul(u3, u2, xs)
                ve.tensor_scalar_mul(u5, u3, INV_2PI)
                ve.tensor_scalar(u1, u5, 0.25, None, ALU.add)
                ve.tensor_scalar(u5, u5, M23, -M23, ALU.add, ALU.add)
                ve.tensor_scalar(u1, u1, M23, -M23, ALU.add, ALU.add)
                ve.scalar_tensor_tensor(u1, u1, -TWO_PI, u3, ALU.mult, ALU.add)
                ve.scalar_tensor_tensor(u3, u5, -TWO_PI, u3, ALU.mult, ALU.add)
                ve.tensor_scalar(u1, u1, -1.5 * TWO_PI / 2.0, 0.5 * TWO_PI / 2.0, ALU.max, ALU.min)
                ve.tensor_scalar(u3, u3, -TWO_PI / 2.0, TWO_PI / 2.0, ALU.max, ALU.min)
                ve.tensor_scalar_mul(u4, u0, xs).then_inc(SEM["s_czq"], 1)
                W(ve, "s_czq", {0: 6, 1: 12, 2: 15}[xi])
                zre, zim = u1, u3
                ve.tensor_tensor(zre, zre, u4, ALU.mult)
                ve.tensor_tensor(zim, zim, u4, ALU.mult)
                if czx is None:
                    ve.tensor_copy(v_zre, zre)
                    ve.tensor_copy(v_zim, zim).then_inc(SEM["s_czq"], 1)
                    W(ve, "s_czq", 8)
                    ve.tensor_scalar(u2a, v_zre, -1.0, None, ALU.add)
                    ve.tensor_tensor(v_cr, cr2, u2a, ALU.mult)
                    ve.tensor_tensor(u5, ci2, v_zim, ALU.mult)
                    ve.tensor_tensor(v_cr, v_cr, u5, ALU.subtract)
                    ve.tensor_tensor(v_ci, cr2, v_zim, ALU.mult)
                    ve.tensor_tensor(u5, ci2, u2a, ALU.mult)
                    ve.tensor_tensor(v_ci, v_ci, u5, ALU.add)
                    ve.tensor_tensor(u5, v_e, v_e, ALU.mult)
                    ve.tensor_tensor(u2a, ai2, ai2, ALU.mult)
                    ve.tensor_tensor(u5, u5, u2a, ALU.add)
                    ve.reciprocal(u2a, u5)
                    ve.drain()
                    ve.tensor_tensor(u5, v_cr, v_e, ALU.mult)
                    ve.tensor_tensor(v_t, v_ci, ai2, ALU.mult)
                    ve.tensor_tensor(u5, v_t, u5, ALU.subtract)
                    ve.tensor_tensor(u5, u5, u2a, ALU.mult)
                    ve.tensor_tensor(v_t, v_ci, v_e, ALU.mult)
                    ve.tensor_tensor(v_ci, v_cr, ai2, ALU.mult)
                    ve.tensor_tensor(v_ci, v_ci, v_t, ALU.add)
                    ve.tensor_scalar_mul(v_ci, v_ci, -1.0)
                    ve.tensor_tensor(v_ci, v_ci, u2a, ALU.mult)
                    ve.tensor_copy(v_cr, u5).then_inc(SEM["s_czt"], 1)
                    ve.drain()
                    ve.sem_inc(SEM["s_czq"], 1)
                    continue
                ve.tensor_tensor(u2a, v_cr, zre, ALU.mult)
                ve.tensor_tensor(u5, v_ci, zim, ALU.mult)
                ve.tensor_tensor(u2a, u2a, u5, ALU.subtract)
                ve.tensor_copy(czx[0:32, :], u2a)
                ve.tensor_tensor(u2a, v_cr, zim, ALU.mult)
                ve.tensor_tensor(u5, v_ci, zre, ALU.mult)
                ve.tensor_tensor(u2a, u2a, u5, ALU.add)
                ve.tensor_copy(czx[32:64, :], u2a)
            ntrig = 0
            for rnd in range(48):
                lh, kind, sub = round_kind(rnd)
                W(ve, "s_rep", rnd + 1)
                nmm = 3 if kind == "m1" else 2
                for i_ in range(nmm):
                    cp = ve.tensor_copy(rep[i_][:, :], pb[4 + i_][:, 0:512])
                    if i_ == nmm - 1:
                        cp.then_inc(SEM["s_repev"], 1)
                c0 = sub * 16
                if kind == "m1":
                    if lh == 1 and sub == 0:
                        W(ve, "s_tp", 128)
                        W(ve, "s_zb", 16)
                    sA, sB, sC, sD = tslot(0), tslot(1), tslot(2), tslot(3)
                    lio = liota_s[:, lh:lh + 1].broadcast_to([P, 512])
                    ve.tensor_tensor(sA, rep[0][:, :], rep[2][:, :], ALU.add).then_inc(SEM["s_arg"], 1)
                    ntrig += 1
                    W(ve, "s_trig", 3 * ntrig - 2)
                    ve.tensor_tensor(sB, rep[1][:, :], sB, ALU.mult)
                    ve.tensor_tensor(sB, sB, lio, ALU.mult)
                    ve.tensor_scalar_mul(sC, sB, INV_2PI)
                    ve.tensor_scalar(sD, sC, 0.25, None, ALU.add)
                    ve.tensor_scalar(sC, sC, M23, -M23, ALU.add, ALU.add)
                    ve.tensor_scalar(sD, sD, M23, -M23, ALU.add, ALU.add)
                    ve.scalar_tensor_tensor(sC, sC, -TWO_PI, sB, ALU.mult, ALU.add)
                    ve.scalar_tensor_tensor(sD, sD, -TWO_PI, sB, ALU.mult, ALU.add)
                    ve.tensor_scalar(sC, sC, -TWO_PI / 2.0, TWO_PI / 2.0, ALU.max, ALU.min)
                    ve.tensor_scalar(sD, sD, -1.5 * TWO_PI / 2.0, 0.5 * TWO_PI / 2.0, ALU.max, ALU.min)
                    ve.tensor_tensor(sB, sA, lio, ALU.mult).then_inc(SEM["s_arg"], 1)
                    W(ve, "s_trig", 3 * ntrig)
                    av = sD.rearrange("p (c n) -> p c n", c=16, n=NM)
                    dv = sC.rearrange("p (c n) -> p c n", c=16, n=NM)
                    bv = sB.rearrange("p (c n) -> p c n", c=16, n=NM)
                    ve.tensor_tensor(m1[:, c0:c0 + 16, 0, :], bv, av, ALU.mult)
                    ve.tensor_tensor(m1[:, c0:c0 + 16, 1, :], bv, dv, ALU.mult).then_inc(SEM["s_m1"], 1)
                elif kind == "k":
                    re = m1[:, c0:c0 + 16, 0, :]
                    im = m1[:, c0:c0 + 16, 1, :]
                    w1v = tslot(0).rearrange("p (c n) -> p c n", c=16, n=NM)
                    w2v = tslot(1).rearrange("p (c n) -> p c n", c=16, n=NM)
                    r0v = rep[0][:, :].rearrange("p (c n) -> p c n", c=16, n=NM)
                    r1v = rep[1][:, :].rearrange("p (c n) -> p c n", c=16, n=NM)
                    ve.tensor_tensor(w1v, re, r0v, ALU.mult)
                    ve.tensor_tensor(w2v, im, r1v, ALU.mult)
                    ve.tensor_tensor(w1v, w1v, w2v, ALU.subtract)
                    ve.tensor_reduce(kred[:, lh, c0:c0 + 16], w1v, AX.X, ALU.add).then_inc(SEM["s_kred"], 1)
                    if lh == 1 and sub == 7:
                        ve.tensor_scalar_mul(kred[:, :, :], kred[:, :, :], 2.0)
                        dbc = dval[0:1, 0:1].broadcast_to([1, CPC])
                        ve.tensor_tensor(kred[0:1, 0, :], kred[0:1, 0, :], dbc, ALU.add).then_inc(SEM["s_kred"], 2)
                        W(ve, "s_ktr", 1)
                        ve.tensor_copy(ktr[:, 0, :], pb[0][:, 0:128])
                        ve.tensor_copy(ktr[:, 1, :], pb[1][:, 0:128]).then_inc(SEM["s_ktr"], 2)
                else:
                    W(ve, "s_zb", (lh + 1) * 16)
                    re = m1[:, c0:c0 + 16, 0, :]
                    im = m1[:, c0:c0 + 16, 1, :]
                    r0v = rep[0][:, :].rearrange("p (c n) -> p c n", c=16, n=NM)
                    r1v = rep[1][:, :].rearrange("p (c n) -> p c n", c=16, n=NM)
                    us = [tslot(i).rearrange("p (c n) -> p c n", c=16, n=NM) for i in range(4)]
                    ve.tensor_tensor(us[0], re, r0v, ALU.mult)
                    ve.tensor_tensor(us[1], im, r1v, ALU.mult)
                    ve.tensor_tensor(us[2], im, r0v, ALU.mult)
                    ve.tensor_tensor(us[3], re, r1v, ALU.mult)
                    ve.tensor_tensor(re, us[0], us[1], ALU.subtract)
                    ve.tensor_tensor(im, us[2], us[3], ALU.add).then_inc(SEM["s_cm"], 1)
                if kind == "m1" and sub == 7:
                    if lh == 0:
                        W(ve, "s_czt", 33)
                        W(ve, "s_czt", 65)
                    for jj in range(16):
                        W(ve, "s_zb", lh * 16 + jj + 1)
                        src = pb[jj % 4][:, 0:512].rearrange("p (c e n) -> p c e n", c=8, e=2, n=NM)
                        ve.tensor_copy(zb[:, jj * 8:(jj + 1) * 8, 1 - lh, :, :], src).then_inc(SEM["s_zbev"], 1)
                if kind == "cm" and sub == 7:
                    czx = czh if lh == 0 else czq
                    for c in range(CPC):
                        k = lh * CPC + c
                        if ev_owner(k) != "v":
                            continue
                        kb = k // 4
                        if kb >= 1:
                            W(ve, "s_czd", 16 * kb)
                        W(ve, "s_tp", k + 1)
                        ve.tensor_copy(stage[:, c % 4, 0:127], pb[k % 6][0:64, 1:128])
                        ve.tensor_copy(stage[:, c % 4, 127:128], czx[:, c:c + 1]).then_inc(SEM["s_ev_v"], 1)
                if rnd < 3:
                    W(ve, "s_mch", rnd + 1)
                    ve.tensor_copy(mstg[:, 2 * rnd:2 * rnd + 2], pb[rnd][:, 0:2]).then_inc(SEM["s_mchv"], 1)
                if rnd == 3:
                    W(ve, "s_chrm", 1)
                    ve.tensor_copy(gar_s, pb[7][:, 0:256]).then_inc(SEM["s_chre"], 1)
                if rnd == 5:
                    W(ve, "s_kz", 34)
                    ga_r = gar_s
                    tt1 = ysf[:, 256:512]
                    tt2 = ysf[:, 512:768]
                    ve.tensor_scalar(tt1, ga_r, 0.0, None, ALU.is_ge)
                    ve.tensor_scalar(tt1, tt1, 2.0, -1.0, ALU.mult, ALU.add)
                    ve.tensor_scalar_mul(tt2, ga_r, -1.0)
                    ve.tensor_tensor(tt2, tt2, ga_r, ALU.max)
                    ve.tensor_scalar(tt2, tt2, 0.25, None, ALU.max)
                    ve.tensor_tensor(ga_r, tt1, tt2, ALU.mult)
                if rnd < 24:
                    W(ve, "s_mtok", rnd + 1)
                    grp, dtt = rnd // 8, rnd % 8
                    ve.tensor_tensor(tokt, pb[7][:, 0:2], liota_s[:, 6:8], ALU.mult)
                    ve.tensor_reduce(mods_s[:, grp * 8 + dtt:grp * 8 + dtt + 1], tokt, AX.X, ALU.add
                                     ).then_inc(SEM["s_mtev"], 1)
                if rnd == 24:
                    W(ve, "s_modr", 16)
                    ve.tensor_scalar(rows[64:65, 256:512], rows[64:65, 256:512], 1.0, None, ALU.add
                                     ).then_inc(SEM["s_chre"], 1)
                if rnd == 25:
                    W(ve, "s_modr", 32)
                    ve.tensor_scalar(rows[64:65, 768:1792], rows[64:65, 768:1792], 1.0, None, ALU.add
                                     ).then_inc(SEM["s_chre"], 1)
                if rnd == 26:
                    gbc = mods_s[:, 16:24]
                    tt1, tt2 = gbt[:, 0:8], gbt[:, 8:16]
                    ve.tensor_scalar(tt1, gbc, 0.0, None, ALU.is_ge)
                    ve.tensor_scalar(tt1, tt1, 2.0, -1.0, ALU.mult, ALU.add)
                    ve.tensor_scalar_mul(tt2, gbc, -1.0)
                    ve.tensor_tensor(tt2, tt2, gbc, ALU.max)
                    ve.tensor_scalar(tt2, tt2, 0.25, None, ALU.max)
                    ve.tensor_tensor(gbc, tt1, tt2, ALU.mult)
                if rnd < 8:
                    q = rnd
                    W(ve, "s_h1", T_H1)
                    W(ve, "s_sq", 2 * q + 1)
                    uview = A_h1[:, 4096 * q:4096 * (q + 1)].rearrange("p (c t) -> p t c", c=16, t=256)
                    sview = sqst[:, :].rearrange("p (c t) -> p t c", c=16, t=256)
                    if q == 0:
                        ve.tensor_reduce(statv_m, uview, AX.X, ALU.add)
                        ve.tensor_reduce(statv_v, sview, AX.X, ALU.add).then_inc(SEM["s_sq"], 1)
                    else:
                        ve.tensor_reduce(ysf[:, 512:768], uview, AX.X, ALU.add)
                        ve.tensor_reduce(ysf[:, 768:1024], sview, AX.X, ALU.add).then_inc(SEM["s_sq"], 1)
                        ve.tensor_tensor(statv_m, statv_m, ysf[:, 512:768], ALU.add)
                        ve.tensor_tensor(statv_v, statv_v, ysf[:, 768:1024], ALU.add)
                    if q == 7:
                        ve.drain()
                        ve.sem_inc(SEM["s_stat"], 1)
                if rnd == 14:
                    W(ve, "s_lnb", 32)
                    tt1 = ysf[:, 256:512]
                    ve.tensor_scalar_mul(statv_m, statv_m, 1.0 / 1024.0)
                    ve.tensor_scalar_mul(statv_v, statv_v, 1.0 / 1024.0)
                    ve.tensor_tensor(tt1, statv_m, statv_m, ALU.mult)
                    ve.tensor_tensor(statv_v, statv_v, tt1, ALU.subtract).then_inc(SEM["s_stat"], 1)
                if rnd == 17:
                    W(ve, "s_stat", 4)
                    tt1 = ysf[:, 256:512]
                    ve.reciprocal(statv_v, statv_v)
                    ve.drain()
                    ve.tensor_tensor(tt1, statv_v, ysf[:, 512:768], ALU.mult)
                    ve.tensor_scalar(tt1, tt1, -1.0, 2.0, ALU.mult, ALU.add)
                    ve.tensor_tensor(statv_v, statv_v, tt1, ALU.mult)
                if rnd == 19:
                    ve.tensor_tensor(h1f2, h1f2, m_bc, ALU.subtract).then_inc(SEM["s_h1m"], 1)
                if rnd == 22:
                    ve.tensor_tensor(h1f2, h1f2, r_bc, ALU.mult).then_inc(SEM["s_h1m"], 1)
                if rnd == 27:
                    W(ve, "s_chrm", 2)
                    for b in range(2):
                        ca_b = pb[7][:, 256:512].rearrange("p (c b o q) -> p c b o q",
                                                           c=128, b=2, o=1, q=1
                                                           )[:, :, b, :, :].broadcast_to([P, 128, 2, 64])
                        ve.tensor_tensor(h1[:, :, :, b, :], h1[:, :, :, b, :], ca_b, ALU.mult
                                         ).then_inc(SEM["s_h1m"], 1 if b == 1 else 0)
                if rnd == 31:
                    for b in range(2):
                        sa_b = pb[7][:, 0:256].rearrange("p (c b o q) -> p c b o q",
                                                         c=128, b=2, o=1, q=1
                                                         )[:, :, b, :, :].broadcast_to([P, 128, 2, 64])
                        ve.tensor_tensor(h1[:, :, :, b, :], h1[:, :, :, b, :], sa_b, ALU.add
                                         ).then_inc(SEM["s_h1m"], 1 if b == 1 else 0)
            W(ve, "s_zq", 2)
            sq_ai = ssmq_s[:, 64:128]
            ve.tensor_tensor(zq_im, sq_ai, zq_t1, ALU.mult)
            ve.tensor_scalar_mul(zq_im, zq_im, 256.0)
            ve.tensor_tensor(zq_t2, zq_t2, zq_t1, ALU.mult)
            ve.tensor_scalar_mul(zq_t2, zq_t2, -256.0)
            ve.tensor_scalar_mul(zq_t1, zq_im, INV_2PI)
            ve.tensor_scalar(zq_re, zq_t1, 0.25, None, ALU.add)
            ve.tensor_scalar(zq_t1, zq_t1, M23, -M23, ALU.add, ALU.add)
            ve.tensor_scalar(zq_re, zq_re, M23, -M23, ALU.add, ALU.add)
            ve.scalar_tensor_tensor(zq_re, zq_re, -TWO_PI, zq_im, ALU.mult, ALU.add)
            ve.scalar_tensor_tensor(zq_im, zq_t1, -TWO_PI, zq_im, ALU.mult, ALU.add)
            ve.tensor_scalar(zq_re, zq_re, -1.5 * TWO_PI / 2.0, 0.5 * TWO_PI / 2.0, ALU.max, ALU.min)
            ve.tensor_scalar(zq_im, zq_im, -TWO_PI / 2.0, TWO_PI / 2.0, ALU.max, ALU.min).then_inc(SEM["s_zq"], 1)
            W(ve, "s_zq", 5)
            ve.tensor_tensor(zq_re, zq_re, zq_t1, ALU.mult)
            ve.tensor_tensor(zq_im, zq_im, zq_t1, ALU.mult)
            ve.tensor_copy(zrr[0:64, :], zq_re)
            ve.tensor_copy(zrr[64:128, :], zq_re)
            ve.tensor_copy(zi_t, zq_im)
            for c in range(CPC):
                if bev_owner(c) != "v":
                    continue
                W(ve, "s_bmm", c + 1)
                par_ = c % 2
                bk = pb[c % 8]
                src_re = bk[0:32, 0:128].rearrange("p (b i) -> p b i", b=2, i=64)
                src_im = bk[32:64, 0:128].rearrange("p (b i) -> p b i", b=2, i=64)
                ve.tensor_copy(bar[par_ * 32:par_ * 32 + 32, c // 2, :, :], src_re)
                ve.tensor_copy(bar[64 + par_ * 32:64 + par_ * 32 + 32, c // 2, :, :], src_im
                               ).then_inc(SEM["s_bev_v"], 1)
            W(ve, "s_bev_v", cnt_upto(bev_owner, CPC - 1, "v"))
            W(ve, "s_bev_s", cnt_upto(bev_owner, CPC - 1, "s"))
            ve.memset(arch[:, :, :, 0:1], 0.0)
            ve.tensor_copy(sm[64:128, 64:128], zi_t)
            zrrb = zrr.rearrange("p (c o) -> p c o", o=1).broadcast_to([P, 64, 2])
            zib = zi_t.rearrange("p (c o) -> p c o", o=1).broadcast_to([64, 64, 2])
            zib64 = sm[64:128, 64:128].rearrange("p (c o) -> p c o", o=1).broadcast_to([64, 64, 2])
            for j in range(63):
                ve.tensor_tensor(p1v, arch[:, :, :, j], zrrb, ALU.mult)
                ve.tensor_tensor(p1v, p1v, bar[:, :, :, j], ALU.add)
                ve.tensor_tensor(t2v, arch[64:128, :, :, j], zib64, ALU.mult)
                ve.tensor_tensor(t3v, arch[0:64, :, :, j], zib, ALU.mult)
                ve.scalar_tensor_tensor(arch[0:64, :, :, j + 1], t2v, -1.0, p1v[0:64], ALU.mult, ALU.add)
                ve.tensor_tensor(arch[64:128, :, :, j + 1], p1v[64:128], t3v, ALU.add)
            ve.tensor_scalar_mul(archb[0:64, :, :, :], arch[0:64, :, :, :], 2.0)
            ve.tensor_scalar_mul(archb[64:128, :, :, :], arch[64:128, :, :, :], -2.0).then_inc(SEM["s_scan"], 2)
            for c in range(CPC):
                W(ve, "s_yev", c + 1)
                blkv = ysf[:, (c % 4) * 256:(c % 4) * 256 + 256].rearrange(
                    "p (h b i) -> p h b i", h=2, b=2, i=64)
                gav = gar_s[:, 2 * c:2 * c + 2].rearrange(
                    "p (o b q) -> p o b q", o=1, b=2, q=1).broadcast_to([P, 2, 2, 64])
                ve.tensor_tensor(blkv, blkv, gav, ALU.mult).then_inc(SEM["s_yga"], 1)
            W(ve, "s_utok", NIL * 16)
            x2a_v = x2a.rearrange("p (i16 h d) -> p d h i16", i16=8, h=2, d=1024)
            x2b_v = x2b.rearrange("p (i16 h d) -> p d h i16", i16=4, h=2, d=1024)
            x2c_v = x2c.rearrange("p (i16 h d) -> p d h i16", i16=4, h=2, d=1024)
            for n in range(32):
                g_i, src_ = n // 8, n % 8
                off = src_ * 128 + g_i * 32
                W(ve, "s_recv", 32 * (n // 2 + 1))
                tv = tpf[:, (n % 2) * 1024:(n % 2) * 1024 + 1024].rearrange(
                    "p (cl h i) -> p cl h i", cl=32, h=2, i=16)
                ve.tensor_tensor(x2a_v[:, off:off + 32, :, :], x2a_v[:, off:off + 32, :, :],
                                 tv[:, :, :, 0:8], ALU.add).then_inc(SEM["s_acc"], 1)
                ve.tensor_tensor(x2b_v[:, off:off + 32, :, :], x2b_v[:, off:off + 32, :, :],
                                 tv[:, :, :, 8:12], ALU.add).then_inc(SEM["s_acc"], 1)
                ve.tensor_tensor(x2c_v[:, off:off + 32, :, :], x2c_v[:, off:off + 32, :, :],
                                 tv[:, :, :, 12:16], ALU.add).then_inc(SEM["s_acc"], 1)
            ve.tensor_copy(gbs, mods_s[:, 16:24])
            for il in range(NIL):
                W(ve, "s_x2td", il * 16)
                for dt in range(4):
                    n = il * 8 + dt
                    W(ve, "s_x2t", n + 1)
                    ve.tensor_copy(x2tst[:, dt * 128:(dt + 1) * 128], pb[n % 4][:, 0:128]
                                   ).then_inc(SEM["s_x2ve"], 1)
                k = il
                W(ve, "s_w1a", k + 1)
                ve.tensor_reduce(maxw1[:, k:k + 1], wqin, AX.X, ALU.max).then_inc(SEM["s_wmv1"], 1)
            ve.tensor_scalar(maxw1, maxw1, 1e-8, None, ALU.max)
            ve.reciprocal(rw1c, maxw1)
            ve.drain()
            ve.tensor_tensor(csm1, rw1c, maxw1, ALU.mult)
            ve.tensor_scalar(csm1, csm1, -1.0, 2.0, ALU.mult, ALU.add)
            ve.tensor_tensor(rw1c, rw1c, csm1, ALU.mult)
            ve.tensor_scalar_mul(rw1c, rw1c, 127.0).then_inc(SEM["s_w1mx"], 1)
            W(ve, "s_ln2s", 64)
            ve.tensor_scalar_mul(mxc, mxc, 1.0 / 1024.0)
            ve.tensor_scalar_mul(rsc, rsc, 1.0 / 1024.0)
            ve.tensor_tensor(csm1, mxc, mxc, ALU.mult)
            ve.tensor_tensor(rsc, rsc, csm1, ALU.subtract).then_inc(SEM["s_ln2s"], 1)
            W(ve, "s_ln2s", 67)
            ve.tensor_copy(csm2, rsc)
            ve.reciprocal(rsc, rsc)
            ve.drain()
            ve.tensor_tensor(csm1, rsc, csm2, ALU.mult)
            ve.tensor_scalar(csm1, csm1, -1.0, 2.0, ALU.mult, ALU.add)
            ve.tensor_tensor(rsc, rsc, csm1, ALU.mult)
            ve.tensor_tensor(mrs, mxc, rsc, ALU.mult)
            ve.tensor_scalar_mul(mrs, mrs, -1.0).then_inc(SEM["s_ln2f"], 1)
            W(ve, "s_h2r", 4)
            for il in range(NIL):
                W(ve, "s_h2s", il + 1)
                xc = x2chunk(il)
                ve.tensor_tensor(xc[:, 0:512], xc[:, 0:512], pb[4][:, 0:512], ALU.mult)
                ve.tensor_tensor(xc[:, 512:1024], xc[:, 512:1024], pb[5][:, 0:512], ALU.mult)
                ve.tensor_tensor(xc[:, 0:512], xc[:, 0:512], pb[6][:, 0:512], ALU.add)
                ve.tensor_tensor(xc[:, 512:1024], xc[:, 512:1024], pb[7][:, 0:512], ALU.add)
                ve.tensor_reduce(tcol, xc, AX.X, ALU.max)
                if il == 0:
                    ve.tensor_copy(mx1col, tcol).then_inc(SEM["s_h2v"], 1)
                else:
                    ve.tensor_tensor(mx1col, mx1col, tcol, ALU.max).then_inc(SEM["s_h2v"], 1)
            W(ve, "s_mx1tp", 1)
            ve.tensor_reduce(gro[:, 0:1], pb[4][0:1, 0:128], AX.X, ALU.max).then_inc(SEM["s_mx1v"], 1)
            W(ve, "s_mx1r", 16)
            ve.tensor_scalar(gro[:, 1:2], gro[:, 1:2], 1e-8, None, ALU.max)
            ve.tensor_scalar(gro[:, 2:3], gro[:, 1:2], 1.0 / 255.0, None, ALU.mult)
            ve.reciprocal(gro[:, 3:4], gro[:, 1:2])
            ve.drain()
            ve.tensor_tensor(trow[:, 0:1], gro[:, 3:4], gro[:, 1:2], ALU.mult)
            ve.tensor_scalar(trow[:, 0:1], trow[:, 0:1], -1.0, 2.0, ALU.mult, ALU.add)
            ve.tensor_tensor(gro[:, 3:4], gro[:, 3:4], trow[:, 0:1], ALU.mult)
            ve.tensor_scalar_mul(gro[:, 3:4], gro[:, 3:4], 255.0).then_inc(SEM["s_s1row"], 1)
            W(ve, "s_s1rm", 1)
            ve.tensor_copy(s1c, pb[5][:, 0:1])
            ve.tensor_copy(is1c, pb[5][:, 1:2])
            W(ve, "s_bc", 32)
            ve.tensor_scalar(csm1, maxw1, 1.0 / 127.0, None, ALU.mult)
            ve.tensor_tensor(sc1c, csm1, s1c.broadcast_to([P, 32]), ALU.mult)
            ve.tensor_tensor(csm1, b1cc, rw1c, ALU.mult)
            ve.tensor_tensor(csm1, csm1, is1c.broadcast_to([P, 32]), ALU.mult)
            ve.tensor_scalar(csm1, csm1, 1.0 / 255.0, None, ALU.mult)
            ve.tensor_scalar(csm1, csm1, M23, -M23, ALU.add, ALU.add)
            ve.tensor_tensor(b1qc, csm1, sc1c, ALU.mult).then_inc(SEM["s_colq"], 1)
            vev = [x for x in range(NTRS) if QWT_OWNER[x] == "v"]
            vstate = {"p": 0}

            def v_drain_below(lim):
                while vstate["p"] < len(vev) and vev[vstate["p"]] < lim:
                    x = vev[vstate["p"]]
                    W(ve, "s_qt", x + 1)
                    kindd, a, jj = QWT_DST[x]
                    if kindd == "q":
                        dst = x1qt[jj][:, a * 128:(a + 1) * 128]
                    else:
                        dst = w1qt[jj][:, a * 128:(a + 1) * 128]
                    ve.tensor_copy(dst, pb[x % 4][:, 0:128]).then_inc(SEM["s_qev"], 1)
                    vstate["p"] += 1

            for pos in range(NIL):
                if pos >= 1:
                    v_drain_below(8 * (qslot_idx(pos - 1) + 1))
                if pos >= 2:
                    W(ve, "s_qt", 8 * (qslot_idx(pos - 2) + 1))
                W(ve, "s_qs", 2 * pos + 2)
                ve.tensor_scalar(qstage[:, (pos % 2) * 1024:(pos % 2) * 1024 + 512],
                                 qslotA, M23, -M23, ALU.add, ALU.add)
                ve.tensor_scalar(qstage[:, (pos % 2) * 1024 + 512:(pos % 2) * 1024 + 1024],
                                 qslotB, M23, -M23, ALU.add, ALU.add).then_inc(SEM["s_q"], 1)
            for k in range(32):
                v_drain_below(8 * (32 + k))
                if k >= 2:
                    W(ve, "s_qt", 8 * (wslot_idx(k - 2) + 1))
                W(ve, "s_w1s", k + 1)
                ve.tensor_scalar(bstg1[k % 2], wqin, M23, -M23, ALU.add, ALU.add
                                 ).then_inc(SEM["s_w1v"], 1)
            v_drain_below(NTRS)
            for blkN in range(NFT * NCH):
                W(ve, "s_ev1", blkN + 1)
                ve.tensor_reduce(tcol, h1st[blkN % 4], AX.X, ALU.max)
                if blkN == 0:
                    ve.tensor_copy(mx2col, tcol).then_inc(SEM["s_m2acc"], 1)
                else:
                    ve.tensor_tensor(mx2col, mx2col, tcol, ALU.max).then_inc(SEM["s_m2acc"], 1)
                if blkN % 8 == 0:
                    k = blkN // 8
                    dt, fq = k // 4, k % 4
                    W(ve, "s_w1a", 32 + k + 1)
                    if fq == 0:
                        ve.tensor_reduce(maxw2[:, dt:dt + 1], wqin, AX.X, ALU.max
                                         ).then_inc(SEM["s_wmv1"], 1)
                    else:
                        ve.tensor_reduce(tco2, wqin, AX.X, ALU.max).then_inc(SEM["s_wmv1"], 1)
                        ve.tensor_tensor(maxw2[:, dt:dt + 1], maxw2[:, dt:dt + 1], tco2, ALU.max)
            ve.tensor_scalar(maxw2, maxw2, 1e-8, None, ALU.max)
            ve.reciprocal(rw2c, maxw2)
            ve.drain()
            ve.tensor_tensor(gbt[:, 0:8], rw2c, maxw2, ALU.mult)
            ve.tensor_scalar(gbt[:, 0:8], gbt[:, 0:8], -1.0, 2.0, ALU.mult, ALU.add)
            ve.tensor_tensor(rw2c, rw2c, gbt[:, 0:8], ALU.mult)
            ve.tensor_scalar_mul(rw2c, rw2c, 127.0).then_inc(SEM["s_w1mx"], 1)
            W(ve, "s_mx1tp", 2)
            ve.tensor_reduce(gro[:, 4:5], pb[4][0:1, 0:128], AX.X, ALU.max).then_inc(SEM["s_mx1v"], 1)
            W(ve, "s_mx1r", 32)
            ve.tensor_scalar(gro[:, 5:6], gro[:, 5:6], 1e-8, None, ALU.max)
            ve.tensor_scalar(gro[:, 6:7], gro[:, 5:6], 1.0 / 255.0, None, ALU.mult)
            ve.reciprocal(gro[:, 7:8], gro[:, 5:6])
            ve.drain()
            ve.tensor_tensor(trow[:, 0:1], gro[:, 7:8], gro[:, 5:6], ALU.mult)
            ve.tensor_scalar(trow[:, 0:1], trow[:, 0:1], -1.0, 2.0, ALU.mult, ALU.add)
            ve.tensor_tensor(gro[:, 7:8], gro[:, 7:8], trow[:, 0:1], ALU.mult)
            ve.tensor_scalar_mul(gro[:, 7:8], gro[:, 7:8], 255.0).then_inc(SEM["s_s1row"], 1)
            W(ve, "s_s1rm", 2)
            ve.tensor_copy(s2c, pb[5][:, 0:1])
            ve.tensor_copy(is2c, pb[5][:, 1:2])
            W(ve, "s_bc", 32)
            ve.tensor_scalar(gbt[:, 0:8], maxw2, 1.0 / 127.0, None, ALU.mult)
            ve.tensor_tensor(gbt[:, 0:8], gbt[:, 0:8], s2c.broadcast_to([P, 8]), ALU.mult)
            ve.tensor_tensor(sc2g, gbt[:, 0:8], gbs, ALU.mult)
            ve.tensor_tensor(gbt[:, 8:16], b2cc, rw2c, ALU.mult)
            ve.tensor_tensor(gbt[:, 8:16], gbt[:, 8:16], is2c.broadcast_to([P, 8]), ALU.mult)
            ve.tensor_scalar(gbt[:, 8:16], gbt[:, 8:16], 1.0 / 255.0, None, ALU.mult)
            ve.tensor_scalar(gbt[:, 8:16], gbt[:, 8:16], M23, -M23, ALU.add, ALU.add)
            ve.tensor_tensor(b2qg, gbt[:, 8:16], sc2g, ALU.mult).then_inc(SEM["s_colq"], 1)
            for k in range(32):
                W(ve, "s_w1s", 32 + k + 1)
                if k >= 2:
                    W(ve, "s_w2tp", 8 * (k - 1))
                ve.tensor_scalar(bstg2[k % 2], wqin, M23, -M23, ALU.add, ALU.add
                                 ).then_inc(SEM["s_w1v"], 1)
                if k >= 1:
                    kk = k - 1
                    dt, fq = kk // 4, kk % 4
                    for j in range(4):
                        x = kk * 8 + j
                        W(ve, "s_w2tp", x + 1)
                        ve.tensor_copy(w2qt[fq * 8 + j][:, dt * 128:(dt + 1) * 128],
                                       pb[x % 4][:, 0:128]).then_inc(SEM["s_w2ev"], 1)
            kk = 31
            dt, fq = kk // 4, kk % 4
            for j in range(4):
                x = kk * 8 + j
                W(ve, "s_w2tp", x + 1)
                ve.tensor_copy(w2qt[fq * 8 + j][:, dt * 128:(dt + 1) * 128],
                               pb[x % 4][:, 0:128]).then_inc(SEM["s_w2ev"], 1)
            vadd = {"p": 0}

            def v_adds_below(lim):
                while vadd["p"] < min(lim, 64):
                    j = vadd["p"]
                    W(ve, "s_e2s", j + 1)
                    W(ve, "s_xin", 32 * (j // 2 + 1))
                    ve.tensor_tensor(outst[j % 2], outst[j % 2], x2tin[j % 2], ALU.add
                                     ).then_inc(SEM["s_e2v"], 1)
                    vadd["p"] += 1

            for ch in range(NCH):
                if ch >= 3:
                    v_adds_below(16 * ((ch - 3) // 2 + 1))
                for ft in range(NFT):
                    k2 = ch * NFT + ft
                    W(ve, "s_r2s", k2 + 1)
                    if ch >= 3:
                        W(ve, "s_mm2", (((ch - 3) // 2) + 1) * 16)
                    slot = x2qs[ch % 3]
                    ve.tensor_scalar(slot[:, ft * 512:(ft + 1) * 512], rstg[k2 % 2], M23, -M23,
                                     ALU.add, ALU.add).then_inc(SEM["s_rmg"], 1)
            v_adds_below(64)

        @block.scalar
        def _(sc):
            sc = DrainEng(sc, SEM["s_sch"], multi=[
                id(SEM[n]) for n in ("s_czq", "s_zq", "s_czt", "s_sq",
                                     "s_stat", "s_ln2s")])
            W(sc, "s_in", T_IN)
            sc.activation(ctsil, ct_s, AF.Silu).then_inc(SEM["s_silu"], 1)
            W(sc, "s_czq", 1)
            sc.activation(u0, u0, AF.Exp)
            sc.activation(u1, ssmq2_s[:, 256:384], AF.Exp).then_inc(SEM["s_czq"], 2)
            for xi in range(3):
                W(sc, "s_czq", {0: 4, 1: 10, 2: 13}[xi])
                sc.activation(u3, u3, AF.Sin)
                sc.activation(u1, u1, AF.Sin, bias=liota_s[0:32, 3:4])
                sc.activation(u4, u4, AF.Exp, scale=-1.0).then_inc(SEM["s_czq"], 2)
                if xi == 0:
                    W(sc, "s_czq", 7)
                    sc.activation(v_e, ssmq2_s[:, 0:128], AF.Exp).then_inc(SEM["s_czq"], 1)
            sc.activation(zq_t1, ssmq_s[:, 128:192], AF.Exp)
            sc.activation(zq_t2, ssmq_s[:, 0:64], AF.Exp).then_inc(SEM["s_zq"], 2)
            ntrig = 0
            for rnd in range(48):
                lh, kind, sub = round_kind(rnd)
                if kind == "m1":
                    ntrig += 1
                    sA, sB, sC, sD = tslot(0), tslot(1), tslot(2), tslot(3)
                    W(sc, "s_arg", 2 * ntrig - 1)
                    sc.activation(sA, sA, AF.Exp)
                    sc.activation(sB, rep[2][:, :], AF.Exp).then_inc(SEM["s_trig"], 1)
                    W(sc, "s_arg", 2 * ntrig)
                    sc.activation(sC, sC, AF.Sin)
                    sc.activation(sD, sD, AF.Sin, bias=liota_s[:, 3:4])
                    sc.activation(sB, sB, AF.Exp, scale=-1.0).then_inc(SEM["s_trig"], 2)
                if kind == "cm" and sub == 7:
                    czx = czh if lh == 0 else czq
                    for c in range(CPC):
                        k = lh * CPC + c
                        if ev_owner(k) != "s":
                            continue
                        kb = k // 4
                        if kb >= 1:
                            W(sc, "s_czd", 16 * kb)
                        W(sc, "s_tp", k + 1)
                        sc.activation(stage[:, c % 4, 0:127], pb[k % 6][0:64, 1:128], AF.Copy)
                        sc.activation(stage[:, c % 4, 127:128], czx[:, c:c + 1], AF.Copy).then_inc(SEM["s_ev_s"], 1)
                if rnd < 8:
                    q = rnd
                    W(sc, "s_h1", T_H1)
                    if q >= 1:
                        W(sc, "s_sq", 2 * q)
                    sc.activation(sqst, A_h1[:, 4096 * q:4096 * (q + 1)], AF.Square).then_inc(SEM["s_sq"], 1)
                if rnd == 15:
                    W(sc, "s_stat", 2)
                    sc.activation(statv_v, statv_v, AF.Sqrt, bias=liota_s[:, 4:5])
                    sc.activation(ysf[:, 512:768], statv_v, AF.Copy).then_inc(SEM["s_stat"], 2)
            W(sc, "s_zq", 3)
            sc.activation(zq_im, zq_im, AF.Sin)
            sc.activation(zq_re, zq_re, AF.Sin, bias=liota_s[0:64, 3:4])
            sc.activation(zq_t1, zq_t2, AF.Exp).then_inc(SEM["s_zq"], 2)
            for c in range(CPC):
                if bev_owner(c) != "s":
                    continue
                W(sc, "s_bmm", c + 1)
                par_ = c % 2
                bk = pb[c % 8]
                src_re = bk[0:32, 0:128].rearrange("p (b i) -> p b i", b=2, i=64)
                src_im = bk[32:64, 0:128].rearrange("p (b i) -> p b i", b=2, i=64)
                sc.activation(bar[par_ * 32:par_ * 32 + 32, c // 2, :, :], src_re, AF.Copy)
                sc.activation(bar[64 + par_ * 32:64 + par_ * 32 + 32, c // 2, :, :], src_im, AF.Copy
                              ).then_inc(SEM["s_bev_s"], 1)
            ybv2 = ybo_d.ap().rearrange("p (c h b i) -> p c h b i", c=CPC, h=NH, b=NB, i=NI)
            for c in range(CPC):
                W(sc, "s_ymm", c + 1)
                if c >= 4:
                    W(sc, "s_yout", 16 * (c // 4))
                sc.activation(ysf[:, (c % 4) * 256:(c % 4) * 256 + 128],
                              pb[(c % 4) * 2][:, 0:128], AF.Gelu)
                sc.activation(ysf[:, (c % 4) * 256 + 128:(c % 4) * 256 + 256],
                              pb[(c % 4) * 2 + 1][:, 0:128], AF.Gelu).then_inc(SEM["s_yev"], 1)
                if c % 32 == 31:
                    gg = c // 32
                    sc.drain()
                    W(sc, "s_yout", (gg + 1) * 8 * 16)
                    for s in range(3, 6):
                        b_s, j_q = s // 4, s % 4
                        srcv2 = ybv2[:, gg * 32:(gg + 1) * 32, :, b_s, j_q * 16:(j_q + 1) * 16]
                        dstv2 = rap(send_d[gg], s * TOK * 32,
                                    [(1024, P), (32, 32), (16, 2), (1, 16)])
                        sc.dma_start(out=dstv2, in_=srcv2).then_inc(
                            SEM[["s_send", "s_send1", "s_send2", "s_send3"][gg]], 16)
            W(sc, "s_acc", 96)
            for il in range(NIL):
                W(sc, "s_x2td", il * 16)
                for dt in range(4, 8):
                    n = il * 8 + dt
                    W(sc, "s_x2t", n + 1)
                    sc.activation(x2tst[:, dt * 128:(dt + 1) * 128], pb[n % 4][:, 0:128], AF.Copy
                                  ).then_inc(SEM["s_x2vs"], 1)
                xc = x2chunk(il)
                sc.activation(tpf[:, 0:1024], xc, AF.Copy, accum_out=mxc[:, il:il + 1])
                sc.activation(tpf[:, 0:1024], xc, AF.Square, accum_out=rsc[:, il:il + 1]
                              ).then_inc(SEM["s_ln2s"], 2)
                k = il
                W(sc, "s_wmi1", (k + 1) * 16)
                sc.activation(wqin, wqin, AF.Abs).then_inc(SEM["s_w1a"], 1)
            W(sc, "s_ln2s", 65)
            sc.activation(rsc, rsc, AF.Sqrt, bias=liota_s[:, 4:5]).then_inc(SEM["s_ln2s"], 2)
            W(sc, "s_ln2f", 1)
            W(sc, "s_x2td", 512)
            for il in range(NIL):
                sc.activation(x2chunk(il), x2chunk(il), AF.Identity,
                              scale=rsc[:, il:il + 1], bias=mrs[:, il:il + 1]).then_inc(SEM["s_h2s"], 1)
            sev = [x for x in range(NTRS) if QWT_OWNER[x] == "s"]
            sstate = {"p": 0}

            def s_drain_below(lim):
                while sstate["p"] < len(sev) and sev[sstate["p"]] < lim:
                    x = sev[sstate["p"]]
                    W(sc, "s_qt", x + 1)
                    kindd, a, jj = QWT_DST[x]
                    if kindd == "q":
                        dst = x1qt[jj][:, a * 128:(a + 1) * 128]
                    else:
                        dst = w1qt[jj][:, a * 128:(a + 1) * 128]
                    sc.activation(dst, pb[x % 4][:, 0:128], AF.Copy).then_inc(SEM["s_qes"], 1)
                    sstate["p"] += 1

            W(sc, "s_colq", 1)
            for pos in range(NIL):
                il = QORD[pos]
                if pos >= 1:
                    s_drain_below(8 * (qslot_idx(pos - 1) + 1))
                    W(sc, "s_q", pos)
                xc = x2chunk(il)
                sc.activation(qslotA, xc[:, 0:512], AF.Relu, scale=is1c)
                sc.activation(qslotB, xc[:, 512:1024], AF.Relu, scale=is1c).then_inc(SEM["s_qs"], 2)
            W(sc, "s_w1mx", 1)
            for k in range(32):
                s_drain_below(8 * (32 + k))
                W(sc, "s_w1i", (k + 1) * 16)
                sc.activation(wqin, wqin, AF.Identity, scale=rw1c[:, k:k + 1]
                              ).then_inc(SEM["s_w1s"], 1)
            s_drain_below(NTRS)
            W(sc, "s_x2td", 512)
            for ft in range(NFT):
                for ch in range(NCH):
                    blkN = ft * 8 + ch
                    W(sc, "s_mm1", blkN + 1)
                    if blkN >= 4:
                        W(sc, "s_sp", 64 * (blkN // 4))
                        W(sc, "s_m2acc", blkN - 3)
                    sc.activation(h1st[blkN % 4], pb[ch][:, 0:512], AF.Relu,
                                  scale=sc1c[:, ft:ft + 1], bias=b1qc[:, ft:ft + 1]
                                  ).then_inc(SEM["s_ev1"], 1)
                    if blkN % 8 == 0:
                        k = blkN // 8
                        W(sc, "s_wmi1", 512 + (k + 1) * 16)
                        sc.activation(wqin, wqin, AF.Abs).then_inc(SEM["s_w1a"], 1)
            for k in range(32):
                dtw = k // 4
                W(sc, "s_w1i", 512 + (k + 1) * 16)
                W(sc, "s_w1mx", 2)
                sc.activation(wqin, wqin, AF.Identity, scale=rw2c[:, dtw:dtw + 1]
                              ).then_inc(SEM["s_w1s"], 1)
                if k >= 1:
                    kk = k - 1
                    dt, fq = kk // 4, kk % 4
                    for j in range(4, 8):
                        x = kk * 8 + j
                        W(sc, "s_w2tp", x + 1)
                        sc.activation(w2qt[fq * 8 + j][:, dt * 128:(dt + 1) * 128],
                                      pb[x % 4][:, 0:128], AF.Copy).then_inc(SEM["s_w2es"], 1)
            kk = 31
            dt, fq = kk // 4, kk % 4
            for j in range(4, 8):
                x = kk * 8 + j
                W(sc, "s_w2tp", x + 1)
                sc.activation(w2qt[fq * 8 + j][:, dt * 128:(dt + 1) * 128],
                              pb[x % 4][:, 0:128], AF.Copy).then_inc(SEM["s_w2es"], 1)
            W(sc, "s_colq", 2)
            sevp = {"p": 0}

            def s_evacs_below(lim):
                while sevp["p"] < min(lim, 64):
                    j = sevp["p"]
                    chg, dtq, par_ = j // 16, (j % 16) // 2, j % 2
                    W(sc, "s_mm2", j + 1)
                    if j >= 2:
                        W(sc, "s_out", 32 * (j // 2))
                    bank = (dtq % 4) * 2 + par_
                    sc.activation(outst[j % 2], pb[bank][:, 0:512], AF.Identity,
                                  scale=sc2g[:, dtq:dtq + 1], bias=b2qg[:, dtq:dtq + 1]
                                  ).then_inc(SEM["s_e2s"], 1)
                    sevp["p"] += 1

            for ch in range(NCH):
                if ch >= 3:
                    s_evacs_below(16 * ((ch - 3) // 2 + 1))
                for ft in range(NFT):
                    k2 = ch * NFT + ft
                    W(sc, "s_hin", 64 * (k2 // 4 + 1))
                    if k2 >= 2:
                        W(sc, "s_rmg", k2 - 1)
                    sc.activation(rstg[k2 % 2], h1in[k2 % 4], AF.Identity, scale=is2c
                                  ).then_inc(SEM["s_r2s"], 1)
            s_evacs_below(64)
    return nc


_NC_CACHE = {}


def _get_nc():
    if "nc" not in _NC_CACHE:
        _NC_CACHE["nc"] = build_program()
    return _NC_CACHE["nc"]


def _mk_sel5():
    s = np.zeros((P, 3 * P), np.float32)
    for r in (0, 32, 64):
        s[r, 0:P] = 1.0
    for r in (1, 33):
        s[r, P:2 * P] = 1.0
    s[2, 2 * P:3 * P] = 1.0
    return s


def _host_core_inputs(core, u, log_dt, A_real, A_imag, C_real, C_imag, Dp,
                      w1, b1, w2, b2, cvec, wmod):
    ch0 = 128 * core
    b_me = core // 4
    j_me = core % 4
    v = u[:, :, ch0:ch0 + 128]
    v = np.ascontiguousarray(v.transpose(2, 0, 1)).reshape(CPC, NB, NI, NH, P)
    u_conv = np.ascontiguousarray(v.transpose(4, 0, 3, 1, 2).reshape(P, -1)).astype(np.float32)
    x = u[b_me, 4096 * j_me:4096 * (j_me + 1), :]
    u_tok = np.ascontiguousarray(
        x.reshape(NIL, P, D).transpose(1, 0, 2).reshape(P, NIL * D)).astype(np.float32)

    rowr = lambda a: np.ascontiguousarray(np.asarray(a).reshape(1, -1)).astype(np.float32)

    ld = log_dt[ch0:ch0 + 128]
    Ar = A_real[ch0:ch0 + 128]
    Ai = A_imag[ch0:ch0 + 128]
    Cr = C_real[ch0:ch0 + 128]
    Ci = C_imag[ch0:ch0 + 128]

    def tile64(par_n_c):
        t = np.zeros((64, 64), np.float32)
        for par_ in range(2):
            for n in range(NM):
                t[par_ * 32 + n, :] = par_n_c[2 * np.arange(64) + par_, n]
        return t

    def tile32(a):
        return np.ascontiguousarray(np.asarray(a).T).astype(np.float32)

    ldt_full = np.broadcast_to(ld[:, None], (CPC, NM))
    ssmq_arr = np.concatenate([tile64(Ar), tile64(Ai), tile64(ldt_full)], axis=1)
    ssmq2_arr = np.concatenate([tile32(Ar), tile32(Ai), tile32(ldt_full),
                                tile32(Cr), tile32(Ci)], axis=1)
    lio = np.zeros((P, 8), np.float32)
    lio[:, 0] = np.arange(P)
    lio[:, 1] = np.arange(P) + 128
    lio[:, 2] = -float(np.pi)
    lio[:, 3] = HALF_PI
    lio[:, 4] = 1e-5
    lio[:, 6] = 1.0 if b_me == 0 else 0.0
    lio[:, 7] = 1.0 if b_me == 1 else 0.0

    b1c = np.ascontiguousarray(b1.reshape(NFT, P).T).astype(np.float32)
    b2c = np.ascontiguousarray(b2.reshape(NDT, P).T).astype(np.float32)
    wm_ch = np.concatenate([wmod[:, g * 1024 + ch0:g * 1024 + ch0 + 128] for g in range(3)],
                           axis=1)

    return dict(
        u_conv=u_conv, u_tok=u_tok,
        w1_in=np.ascontiguousarray(w1).astype(np.float32),
        w2_in=np.ascontiguousarray(w2).astype(np.float32),
        b1c_in=b1c, b2c_in=b2c,
        cT_in=np.ascontiguousarray(cvec.T).astype(np.float32),
        wmod_ch=np.ascontiguousarray(wm_ch).astype(np.float32),
        wmod_tok=np.ascontiguousarray(wmod[:, 3 * 1024:6 * 1024]).astype(np.float32),
        arw_in=rowr(Ar), aiw_in=rowr(Ai), ldt_row=rowr(ldt_full),
        d_in=np.full((1, 8), float(Dp), np.float32),
        ssmq=np.ascontiguousarray(ssmq_arr),
        ssmq2=np.ascontiguousarray(ssmq2_arr),
        liota=lio,
        jmat=np.ascontiguousarray(np.eye(P, dtype=np.float32)[::-1]),
        ident=np.eye(P, dtype=np.float32),
        sel5=_mk_sel5(),
    )


def kernel(L0_attn_u, L0_attn_log_dt, L0_attn_A_real, L0_attn_A_imag,
           L0_attn_C_real, L0_attn_C_imag, L0_attn_D,
           L0_ffn_w1, L0_ffn_b1, L0_ffn_w2, L0_ffn_b2, L0_c, L0_wmod):
    import ml_dtypes
    u = np.asarray(L0_attn_u, np.float32)
    nc = _get_nc()
    idb = np.eye(P).astype(ml_dtypes.bfloat16)
    in_maps = []
    for core in range(8):
        im = _host_core_inputs(core, u, np.asarray(L0_attn_log_dt), np.asarray(L0_attn_A_real),
                               np.asarray(L0_attn_A_imag), np.asarray(L0_attn_C_real),
                               np.asarray(L0_attn_C_imag), np.asarray(L0_attn_D),
                               np.asarray(L0_ffn_w1), np.asarray(L0_ffn_b1),
                               np.asarray(L0_ffn_w2), np.asarray(L0_ffn_b2),
                               np.asarray(L0_c), np.asarray(L0_wmod))
        im["identb"] = idb
        in_maps.append(im)
    res = run_bass_kernel_spmd(nc, in_maps, core_ids=list(range(8)))
    out = np.zeros((2, 16384, 1024), np.float32)
    for core in range(8):
        b_me, j_me = core // 4, core % 4
        out[b_me, 4096 * j_me:4096 * (j_me + 1), :] = res.results[core]["out_e"].T
    return out
